# version 44
# baseline (speedup 1.0000x reference)
import os
import numpy as np
import ml_dtypes
import concourse.bass as bass
import concourse.mybir as mybir
from concourse.bass_utils import run_bass_kernel_spmd
from contextlib import ExitStack

F32 = mybir.dt.float32
BF16 = mybir.dt.bfloat16
AF = mybir.ActivationFunctionType
ALU = mybir.AluOpType

T_SEQ = 4096
D = 1024
H = 16
DH = 64
CH = 128
NCHUNK = T_SEQ // CH
N_CORES = 8
EPS_LN = 1e-5
EPS_DEN = 1e-6

ENGINES = ("tensor", "vector", "scalar", "gpsimd", "sync")

DEBUG_TAPS = ()


class Sched:

    def __init__(self):
        self.ops = []
        self.lastw = {}
        self.readers = {}
        self.eng_n = {e: 0 for e in ENGINES}

    DEF_COST = {"tensor": 70, "vector": 700, "scalar": 900, "gpsimd": 1200,
                "sync": 1500}

    def add(self, engine, fn, reads=(), writes=(), dma=False, tag="", cost=None):
        i = len(self.ops)
        deps = set()
        for r in reads:
            w = self.lastw.get(r)
            if w is not None:
                deps.add(w)
        for r in writes:
            w = self.lastw.get(r)
            if w is not None:
                deps.add(w)
            deps.update(self.readers.get(r, ()))
        deps.discard(i)
        self.ops.append(dict(i=i, engine=engine, fn=fn, deps=sorted(deps),
                             dma=dma, tag=tag,
                             cost=float(cost if cost is not None
                                        else self.DEF_COST[engine])))
        for r in reads:
            self.readers.setdefault(r, []).append(i)
        for r in writes:
            self.lastw[r] = i
            self.readers[r] = []
        return i

    def schedule(self):
        ops = self.ops
        n = len(ops)
        succ = [[] for _ in range(n)]
        pend = [0] * n
        for op in ops:
            for d in op["deps"]:
                succ[d].append(op["i"])
                pend[op["i"]] += 1
        LAT = float(os.environ.get("SIMLAT", 580.0))
        prio = [0.0] * n
        for i in range(n - 1, -1, -1):
            m = 0.0
            for sj in succ[i]:
                if prio[sj] > m:
                    m = prio[sj]
            prio[i] = ops[i]["cost"] + (LAT if succ[i] else 0.0) + m
        eng_free = {e: 0.0 for e in ENGINES}
        avail = [0.0] * n
        done = [0.0] * n
        ready = {e: [] for e in ENGINES}
        for i in range(n):
            if pend[i] == 0:
                ready[ops[i]["engine"]].append(i)
        order = []
        remaining = n
        while remaining:
            best = None
            for e in ENGINES:
                lst = ready[e]
                if not lst:
                    continue
                ef = eng_free[e]
                for i in lst:
                    st = ef if ef > avail[i] else avail[i]
                    key = (st, -prio[i], i)
                    if best is None or key < best[0]:
                        best = (key, e, i)
            key, e, i = best
            st = key[0]
            ready[e].remove(i)
            done[i] = st + ops[i]["cost"]
            eng_free[e] = done[i]
            order.append(i)
            remaining -= 1
            for sj in succ[i]:
                pend[sj] -= 1
                t = done[i] + (LAT if ops[sj]["engine"] != e else 0.0)
                if avail[sj] < t:
                    avail[sj] = t
                if pend[sj] == 0:
                    ready[ops[sj]["engine"]].append(sj)
        cnt = {e: 0 for e in ENGINES}
        for i in order:
            ops[i]["eidx"] = cnt[ops[i]["engine"]]
            cnt[ops[i]["engine"]] += 1
        self.sim_makespan = max(done) if n else 0.0
        if os.environ.get('SIM_PRINT'):
            print('SIM makespan:', round(self.sim_makespan / 1000, 1), 'us')
            start_t = {i: done[i] - ops[i]["cost"] for i in range(n)}
            eng_prev = {}
            last = {e: None for e in ENGINES}
            for i in order:
                e2 = ops[i]["engine"]
                eng_prev[i] = last[e2]
                last[e2] = i
            cur = max(range(n), key=lambda i: done[i])
            from collections import Counter
            hist = Counter(); hist_t = Counter()
            while cur is not None:
                st = start_t[cur]
                nxt = None
                for d in ops[cur]["deps"]:
                    t = done[d] + (LAT if ops[d]["engine"] != ops[cur]["engine"] else 0)
                    if abs(t - st) < 1e-6:
                        nxt = d
                        break
                if nxt is None:
                    ep = eng_prev[cur]
                    if ep is not None and abs(done[ep] - st) < 1e-6:
                        nxt = ep
                import re as _re
                key = _re.sub(r"\d+", "#", ops[cur]["tag"]) or ops[cur]["engine"]
                hist[key] += 1; hist_t[key] += ops[cur]["cost"]
                cur = nxt
            print("SIM critical-path composition (count, total_us):")
            for k, v in hist.most_common(16):
                print(f"  {k:24s} n={v:5d}  t={hist_t[k]/1000:.1f}us")
        return order

    def plan(self, order):
        ops = self.ops
        n = len(ops)
        waits = [[] for _ in range(n)]
        needs_inc = [False] * n
        dma_no = {}
        dma_count = {e: 0 for e in ENGINES}
        clocks = [None] * n
        last_on = {e: None for e in ENGINES}

        def fresh_clock():
            c = {e: -1 for e in ENGINES}
            for e in ENGINES:
                for j in range(3):
                    c[f"{e}_dma{j}"] = 0
            return c

        for gi in order:
            op = ops[gi]
            i = op["i"]
            eng = op["engine"]
            prev = last_on[eng]
            clock = dict(clocks[prev]) if prev is not None else fresh_clock()
            if op["dma"]:
                dma_count[eng] += 1
                k = dma_count[eng]
                dma_no[i] = k
                j = (k - 1) % 3
                m_prev = (k - 1) // 3
                if m_prev > 0 and clock[f"{eng}_dma{j}"] < m_prev:
                    waits[i].append(("dma", (eng, j), m_prev))
                    clock[f"{eng}_dma{j}"] = m_prev
            for d in op["deps"]:
                dop = ops[d]
                de = dop["engine"]
                if dop["dma"]:
                    k = dma_no[d]
                    j = (k - 1) % 3
                    m = (k - 1) // 3 + 1
                    if clock[f"{de}_dma{j}"] >= m:
                        continue
                    waits[i].append(("dma", (de, j), m))
                    clock[f"{de}_dma{j}"] = m
                    dc = clocks[d]
                    for key, v in dc.items():
                        if clock[key] < v:
                            clock[key] = v
                else:
                    if de == eng and de in ("tensor", "sync"):
                        continue
                    if clock[de] >= dop["eidx"]:
                        continue
                    waits[i].append(("eng", de, d))
                    needs_inc[d] = True
                    clock[de] = dop["eidx"]
                    dc = clocks[d]
                    for key, v in dc.items():
                        if clock[key] < v:
                            clock[key] = v
            clocks[i] = clock
            last_on[eng] = i
        inc_no = {}
        cnt = {e: 0 for e in ENGINES}
        for gi in order:
            op = ops[gi]
            if needs_inc[op["i"]] and not op["dma"]:
                cnt[op["engine"]] += 1
                inc_no[op["i"]] = cnt[op["engine"]]
        return waits, needs_inc, dma_no, inc_no


def build_program():
    nc = bass.Bass("TRN2", target_bir_lowering=False)

    x_d = nc.dram_tensor("x", [T_SEQ, D], F32, kind="ExternalInput").ap()
    wq_d = nc.dram_tensor("wqT", [D, D], F32, kind="ExternalInput").ap()
    wk_d = nc.dram_tensor("wkT", [D, D], F32, kind="ExternalInput").ap()
    wv_d = nc.dram_tensor("wvT", [D, D], F32, kind="ExternalInput").ap()
    wo_d = nc.dram_tensor("woT", [D, D], F32, kind="ExternalInput").ap()
    wf_d = nc.dram_tensor("wfT", [D, D], F32, kind="ExternalInput").ap()
    tabC_d = nc.dram_tensor("tabC", [128, NCHUNK * 32], F32, kind="ExternalInput").ap()
    tabS_d = nc.dram_tensor("tabS", [128, NCHUNK * 32], F32, kind="ExternalInput").ap()
    mask_d = nc.dram_tensor("mask", [128, 128], BF16, kind="ExternalInput").ap()
    ident_d = nc.dram_tensor("ident", [128, 128], BF16, kind="ExternalInput").ap()
    out_d = nc.dram_tensor("out", [T_SEQ, D], F32, kind="ExternalOutput").ap()

    s = Sched()

    with ExitStack() as es:
        ec = es.enter_context

        wq = ec(nc.sbuf_tensor("wq", [128, 8 * D], BF16))
        wk = ec(nc.sbuf_tensor("wk", [128, 8 * D], BF16))
        wv = ec(nc.sbuf_tensor("wv", [128, 8 * D], BF16))
        wo = ec(nc.sbuf_tensor("wo", [128, 8 * D], BF16))
        wf = ec(nc.sbuf_tensor("wf", [128, 8 * D], BF16))
        tabC = ec(nc.sbuf_tensor("tabC_s", [128, NCHUNK * 32], BF16))
        tabS = ec(nc.sbuf_tensor("tabS_s", [128, NCHUNK * 32], BF16))
        maskt = ec(nc.sbuf_tensor("mask_s", [128, 128], BF16))
        ident = ec(nc.sbuf_tensor("ident_s", [128, 128], BF16))
        xs = [ec(nc.sbuf_tensor(f"xs{i}", [128, D], F32)) for i in range(3)]
        junkV = [ec(nc.sbuf_tensor(f"junkV{i}", [128, D], BF16)) for i in range(3)]
        junkS = [ec(nc.sbuf_tensor(f"junkS{i}", [128, D], BF16)) for i in range(3)]
        ln1 = [ec(nc.sbuf_tensor(f"ln1_{i}", [128, D], BF16)) for i in range(2)]
        ln2 = ec(nc.sbuf_tensor("ln2", [128, D], BF16))
        ln1T = [ec(nc.sbuf_tensor(f"ln1T{i}", [128, D], BF16)) for i in range(2)]
        ln2T = ec(nc.sbuf_tensor("ln2T", [128, D], BF16))
        qev = [ec(nc.sbuf_tensor(f"qev{i}", [128, D], BF16)) for i in range(2)]
        kev = [ec(nc.sbuf_tensor(f"kev{i}", [128, D], BF16)) for i in range(2)]
        qr = ec(nc.sbuf_tensor("qr", [128, D], BF16))
        kr = ec(nc.sbuf_tensor("kr", [128, D], BF16))
        qm = ec(nc.sbuf_tensor("qm", [128, D], BF16))
        km = ec(nc.sbuf_tensor("km", [128, D], BF16))
        qE = ec(nc.sbuf_tensor("qE", [128, D], BF16))
        kE = ec(nc.sbuf_tensor("kE", [128, D], BF16))
        qphi = [ec(nc.sbuf_tensor(f"qphi{i}", [128, D], BF16)) for i in range(2)]
        kphi = [ec(nc.sbuf_tensor(f"kphi{i}", [128, D], BF16)) for i in range(2)]
        rA = ec(nc.sbuf_tensor("rA", [128, 512], BF16))
        rB = ec(nc.sbuf_tensor("rB", [128, 512], BF16))
        rC = ec(nc.sbuf_tensor("rC", [128, 512], BF16))
        rD = ec(nc.sbuf_tensor("rD", [128, 512], BF16))
        qT = [ec(nc.sbuf_tensor(f"qT{i}", [96, H * 128], BF16)) for i in range(2)]
        kT = [ec(nc.sbuf_tensor(f"kT{i}", [96, H * 128], BF16)) for i in range(2)]
        vaug = [ec(nc.sbuf_tensor(f"vaug{i}", [128, H * 65], BF16)) for i in range(2)]
        Asb = ec(nc.sbuf_tensor("Asb", [128, H * 128], BF16))
        KVs = [ec(nc.sbuf_tensor(f"KVs{i}", [96, H * 65], BF16)) for i in range(2)]
        Ysb = ec(nc.sbuf_tensor("Ysb", [128, D], BF16))
        YT = ec(nc.sbuf_tensor("YT", [128, D], BF16))
        x2 = [ec(nc.sbuf_tensor(f"x2_{i}", [128, D], F32)) for i in range(2)]
        outb = [ec(nc.sbuf_tensor(f"outb{i}", [128, D], F32)) for i in range(2)]
        r16a = ec(nc.sbuf_tensor("r16a", [128, 16], F32))
        epsc = ec(nc.sbuf_tensor("epsc", [128, 1], F32))
        fenceV = ec(nc.sbuf_tensor("fenceV", [128, 1], F32))
        fenceS = ec(nc.sbuf_tensor("fenceS", [128, 1], F32))
        r16b = ec(nc.sbuf_tensor("r16b", [128, 16], F32))
        stats = []
        for t in range(3):
            st = {}
            for nm in ("s1", "s2", "nm", "msq", "v1", "v2", "std", "rstd", "nmr"):
                st[nm] = ec(nc.sbuf_tensor(f"st{t}_{nm}", [128, 1], F32))
            stats.append(st)

        kv = [ec(nc.psum_tensor("kv0", [64, 7 * 65], F32)),
              ec(nc.psum_tensor("kv1", [64, 7 * 65], F32)),
              ec(nc.psum_tensor("kv2", [64, 2 * 65], F32))]
        att = ec(nc.psum_tensor("att", [128, 512], F32))
        ptr = ec(nc.psum_tensor("ptr", [128, 1024], BF16))
        pjA = ec(nc.psum_tensor("pjA", [128, 512], F32))
        pjB = ec(nc.psum_tensor("pjB", [128, 512], F32))
        pjC = ec(nc.psum_tensor("pjC", [128, 512], F32))
        bank_pool = [(att, "att"), (pjA, "pjA"), (pjB, "pjB"), (pjC, "pjC")]
        bank_ctr = [0]

        def next_bank():
            b = bank_pool[bank_ctr[0] % 4]
            bank_ctr[0] += 1
            return b

        sems = {e: ec(nc.semaphore(f"sem_{e}")) for e in ("tensor", "vector", "scalar", "gpsimd")}
        dsems = {(e, j): ec(nc.semaphore(f"dsem_{e}{j}"))
                 for e in ("sync", "gpsimd") for j in range(3)}

        block = ec(nc.Block())

        for wsb, wdram, name in ((wq, wq_d, "wq"), (wk, wk_d, "wk"),
                                 (wv, wv_d, "wv"), (wo, wo_d, "wo"),
                                 (wf, wf_d, "wf")):
            s.add("gpsimd",
                  lambda e, a=wsb, b=wdram: e.dma_start(
                      out=a[:].rearrange("p (a o) -> p a o", a=8),
                      in_=b.rearrange("(a p) o -> p a o", p=128)),
                  writes=[name], dma=True, tag=f"load_{name}", cost=7000)
        s.add("scalar", lambda e: e.sign(wf[:], wf[:]),
              reads=["wf"], writes=["wf"], tag="sign_wf")
        s.add("sync", lambda e: e.dma_start(out=maskt[:], in_=mask_d),
              writes=["mask"], dma=True)
        s.add("sync", lambda e: e.dma_start(out=ident[:], in_=ident_d),
              writes=["ident"], dma=True)
        s.add("gpsimd", lambda e: e.dma_start(out=tabC[:], in_=tabC_d),
              writes=["tabC"], dma=True)
        s.add("gpsimd", lambda e: e.dma_start(out=tabS[:], in_=tabS_d),
              writes=["tabS"], dma=True)
        for i in range(2):
            s.add("vector", lambda e, i=i: e.memset(
                vaug[i][:].rearrange("p (h e) -> p h e", e=65)[:, :, 64], 1.0),
                writes=[f"vaug{i}"], tag="ones_vaug")
        s.add("vector", lambda e: e.memset(KVs[0][:], 0.0), writes=["KVs0"])
        s.add("vector", lambda e: e.memset(KVs[1][64:96, :], 0.0), writes=["KVs1"])
        for i in range(2):
            s.add("vector", lambda e, i=i: e.memset(qT[i][64:96, :], 0.0),
                  writes=[f"qT{i}A", f"qT{i}B"], tag="zpad_qT")
            s.add("vector", lambda e, i=i: e.memset(kT[i][64:96, :], 0.0),
                  writes=[f"kT{i}A", f"kT{i}B"], tag="zpad_kT")
        s.add("vector", lambda e: e.memset(epsc[:], EPS_LN), writes=["epsc"])
        s.add("vector", lambda e: e.memset(junkS[0][:], 0.0), writes=["junkS0"])
        for bi in range(3):
            ncols = 130 if bi == 2 else 455
            s.add("tensor",
                  lambda e, bi=bi, ncols=ncols: e.matmul(
                      kv[bi][0:64, 0:ncols], junkS[0][:, 0:64], junkS[0][:, 0:ncols],
                      start=True, stop=False, skip_group_check=True),
                  reads=["junkS0"], writes=[f"kv{bi}"], tag="kv_zero", cost=250)
            for _ in range(4):
                s.add("tensor", lambda e: e.nop(), tag="nop", cost=5)
        for j in range(min(2, NCHUNK)):
            s.add("sync", lambda e, j=j: e.dma_start(
                out=xs[j][:], in_=x_d[j * 128:(j + 1) * 128, :]),
                writes=[f"x{j}"], dma=True, tag=f"load_x{j}", cost=3500)

        tapset = set(DEBUG_TAPS)

        def tap(name, c, res, ap):
            if (name, c) not in tapset:
                return
            rl = list(res) if isinstance(res, (list, tuple)) else [res]
            shp = list(ap.shape)
            d = nc.dram_tensor(f"tap_{name}_{c}", shp, F32, kind="ExternalOutput").ap()
            s.add("gpsimd", lambda e, d=d, ap=ap: e.dma_start(out=d, in_=ap),
                  reads=rl, dma=True, tag=f"tap_{name}_{c}")

        def emit_ln_sums(c, xres, x_sb, sti):
            st = stats[sti]
            pre = f"st{sti}_"

            def R(nm):
                return pre + nm
            s.add("vector", lambda e: e.tensor_scalar(
                out=junkV[sti][:], in0=x_sb[:], scalar1=-1.0 / D, scalar2=0.0,
                op0=ALU.mult, op1=ALU.add, accum_out=st["s1"][:]),
                reads=[xres], writes=[f"junkV{sti}", R("s1")], tag=f"ln{sti}_sum")
            s.add("vector", lambda e: e.tensor_scalar(
                out=fenceV[:], in0=st["s1"][:], scalar1=1.0, scalar2=None,
                op0=ALU.mult),
                reads=[R("s1")], writes=[R("s1")], tag=f"ln{sti}_sumf", cost=80)
            s.add("scalar", lambda e: e.activation(
                junkS[sti][:], x_sb[:], AF.Square, accum_out=st["s2"][:]),
                reads=[xres], writes=[f"junkS{sti}", R("s2")], tag=f"ln{sti}_sq")
            s.add("scalar", lambda e: e.copy(fenceS[:], st["s2"][:]),
                reads=[R("s2")], writes=[R("s2")], tag=f"ln{sti}_sqf", cost=200)

        def emit_ln_rest(c, xres, x_sb, sti, lnres, ln_sb):
            st = stats[sti]
            pre = f"st{sti}_"

            def R(nm):
                return pre + nm
            s.add("scalar", lambda e: e.activation(st["msq"][:], st["s1"][:],
                                                   AF.Square),
                  reads=[R("s1")], writes=[R("msq")], cost=300)
            s.add("scalar", lambda e: e.activation(st["v2"][:], st["msq"][:],
                                                   AF.Copy, scale=-1.0,
                                                   bias=float(EPS_LN)),
                  reads=[R("msq")], writes=[R("v2")], cost=300)
            s.add("scalar", lambda e: e.activation(st["std"][:], st["s2"][:], AF.Ln,
                                                   scale=1.0 / D, bias=st["v2"][:]),
                  reads=[R("s2"), R("v2")], writes=[R("std")], cost=320)
            s.add("scalar", lambda e: e.activation(st["rstd"][:], st["std"][:], AF.Exp, scale=-0.5),
                  reads=[R("std")], writes=[R("rstd")], cost=320)
            s.add("scalar", lambda e: e.activation(st["nmr"][:], st["rstd"][:],
                                                   AF.Copy, scale=st["s1"][:]),
                  reads=[R("s1"), R("rstd")], writes=[R("nmr")], cost=300)
            s.add("vector", lambda e: e.tensor_scalar(
                out=ln_sb[:], in0=x_sb[:], scalar1=st["rstd"][:], scalar2=st["nmr"][:],
                op0=ALU.mult, op1=ALU.add),
                reads=[xres, R("rstd"), R("nmr")], writes=[lnres], tag=f"ln{sti}_apply")

        def emit_transposes(src_res, src_sb, dst_res, dst_sb, tag):
            for i in range(8):
                s.add("tensor", lambda e, i=i: e.transpose(
                    ptr[:, i * 128:(i + 1) * 128], src_sb[:, i * 128:(i + 1) * 128],
                    ident[:]),
                    reads=[src_res, "ident"],
                    writes=["ptr"], tag=f"tr_{tag}{i}")
            for hb in range(2):
                s.add("scalar", lambda e, hb=hb: e.copy(
                    dst_sb[:, hb * 512:(hb + 1) * 512],
                    ptr[:, hb * 512:(hb + 1) * 512]),
                    reads=["ptr"],
                    writes=[dst_res + ("A" if hb == 0 else "B")],
                    tag=f"ev_{tag}{hb}", cost=650)

        def emit_transposes64(src_res, src_sb, dst_res, dst_sb, tag, eng="scalar"):
            for i in range(8):
                s.add("tensor", lambda e, i=i: e.transpose(
                    ptr[:, i * 128:(i + 1) * 128], src_sb[:, i * 128:(i + 1) * 128],
                    ident[:]),
                    reads=[src_res, "ident"],
                    writes=["ptr"], tag=f"tr_{tag}{i}")
            pv = ptr[:].rearrange("p (a t) -> p a t", a=8)
            dv = dst_sb[0:64, :].rearrange("p (a hp t) -> p a hp t", a=8, hp=2)
            for hb in range(2):
                a0, a1 = 4 * hb, 4 * hb + 4
                rres = "ptr"
                wres = dst_res + ("A" if hb == 0 else "B")
                if eng == "scalar":
                    s.add("scalar", lambda e, a0=a0, a1=a1: e.copy(
                        dv[:, a0:a1, 0, :], pv[0:64, a0:a1, :]),
                        reads=[rres], writes=[wres], tag=f"ev_{tag}e{hb}", cost=360)
                    s.add("scalar", lambda e, a0=a0, a1=a1: e.copy(
                        dv[:, a0:a1, 1, :], pv[64:128, a0:a1, :]),
                        reads=[rres], writes=[wres], tag=f"ev_{tag}o{hb}", cost=360)
                else:
                    s.add("vector", lambda e, a0=a0, a1=a1: e.tensor_copy(
                        dv[:, a0:a1, 0, :], pv[0:64, a0:a1, :]),
                        reads=[rres], writes=[wres], tag=f"ev_{tag}e{hb}", cost=360)
                    s.add("vector", lambda e, a0=a0, a1=a1: e.tensor_copy(
                        dv[:, a0:a1, 1, :], pv[64:128, a0:a1, :]),
                        reads=[rres], writes=[wres], tag=f"ev_{tag}o{hb}", cost=360)

        def emit_proj(wres, w_sb, lres, lsb, tag, evac_half):
            lresA, lresB = lres
            banks = (next_bank(), next_bank())
            for o in range(8):
                pj, pjres = banks[0] if o < 4 else banks[1]
                off = (o % 4) * 128
                for i in range(8):
                    s.add("tensor", lambda e, o=o, i=i, pj=pj, off=off, lsb=lsb, w_sb=w_sb: e.matmul(
                        pj[:, off:off + 128],
                        lsb[:, i * 128:(i + 1) * 128],
                        w_sb[:, i * D + o * 128: i * D + o * 128 + 128],
                        start=(i == 0), stop=(i == 7)),
                        reads=[lresA if i < 4 else lresB, wres],
                        writes=[pjres], tag=f"{tag}_o{o}i{i}")
                if o == 3:
                    evac_half(0, banks[0])
            evac_half(1, banks[1])

        def sb_evac(dst_res, dst_sb, tag):
            def evac_half(half, bank):
                pj, pjres = bank
                s.add("scalar", lambda e, half=half, pj=pj, dst_sb=dst_sb: e.copy(
                    dst_sb[:, half * 512:(half + 1) * 512], pj[:]),
                    reads=[pjres], writes=[dst_res], tag=f"{tag}_h{half}")
            return evac_half

        def emit_rope_phi(c, src_res, src_sb, rdst_res, rdst, mdst_res, mdst,
                          edst_res, edst, pdst_res, pdst, tag):
            sv = src_sb[:].rearrange("p (h x j) -> p x h j", x=2, j=32)
            LO, HI = sv[:, 0], sv[:, 1]
            Cb = tabC[:, c * 32:(c + 1) * 32].unsqueeze(1).broadcast_to([128, H, 32])
            Sb = tabS[:, c * 32:(c + 1) * 32].unsqueeze(1).broadcast_to([128, H, 32])
            rAv = rA[:].rearrange("p (h j) -> p h j", j=32)
            rBv = rB[:].rearrange("p (h j) -> p h j", j=32)
            rCv = rC[:].rearrange("p (h j) -> p h j", j=32)
            rDv = rD[:].rearrange("p (h j) -> p h j", j=32)
            dv = rdst[:].rearrange("p (h x j) -> p x h j", x=2, j=32)
            s.add("gpsimd", lambda e: e.tensor_tensor(out=rAv, in0=HI, in1=Sb, op=ALU.mult),
                  reads=[src_res, "tabS"], writes=["rA"], tag=f"{tag}_rA")
            s.add("gpsimd", lambda e: e.tensor_tensor(out=rBv, in0=LO, in1=Sb, op=ALU.mult),
                  reads=[src_res, "tabS"], writes=["rB"], tag=f"{tag}_rB")
            s.add("vector", lambda e: e.tensor_tensor(out=rCv, in0=LO, in1=Cb, op=ALU.mult),
                  reads=[src_res, "tabC"], writes=["rC"], tag=f"{tag}_rC")
            s.add("vector", lambda e: e.tensor_tensor(out=rDv, in0=HI, in1=Cb, op=ALU.mult),
                  reads=[src_res, "tabC"], writes=["rD"], tag=f"{tag}_rD")
            s.add("vector", lambda e: e.tensor_tensor(out=dv[:, 0], in0=rCv, in1=rAv, op=ALU.subtract),
                  reads=["rC", "rA"], writes=[rdst_res], tag=f"{tag}_lo")
            s.add("vector", lambda e: e.tensor_tensor(out=dv[:, 1], in0=rBv, in1=rDv, op=ALU.add),
                  reads=["rB", "rD"], writes=[rdst_res], tag=f"{tag}_hi")
            s.add("vector", lambda e: e.tensor_scalar(
                out=mdst[:], in0=rdst[:], scalar1=0.0, scalar2=None, op0=ALU.min),
                reads=[rdst_res], writes=[mdst_res], tag=f"{tag}_min")
            s.add("scalar", lambda e: e.activation(edst[:], mdst[:], AF.Exp),
                  reads=[mdst_res], writes=[edst_res], tag=f"{tag}_exp")
            s.add("vector", lambda e: e.scalar_tensor_tensor(
                out=pdst[:], in0=rdst[:], scalar=0.0, in1=edst[:],
                op0=ALU.max, op1=ALU.add),
                reads=[rdst_res, edst_res], writes=[pdst_res], tag=f"{tag}_phi")

        def kv_slot(h):
            b = min(h // 7, 2)
            return b, (h - 7 * b) * 65

        def front_ln_sums(c):
            emit_ln_sums(c, f"x{c % 3}", xs[c % 3], c % 2)

        def front_ln_rest(c):
            i2 = c % 2
            emit_ln_rest(c, f"x{c % 3}", xs[c % 3], i2, f"ln1_{i2}", ln1[i2])
            tap("ln1", c, f"ln1_{i2}", ln1[i2][:])
            emit_transposes(f"ln1_{i2}", ln1[i2], f"ln1T{i2}", ln1T[i2], f"ln1T{c}")
            tap("ln1T", c, (f"ln1T{i2}A", f"ln1T{i2}B"), ln1T[i2][:])

        def front_q(c):
            i2 = c % 2
            emit_proj("wq", wq, (f"ln1T{i2}A", f"ln1T{i2}B"), ln1T[i2], f"q{c}",
                      sb_evac(f"qev{i2}", qev[i2], f"qev{c}"))
            tap("qev", c, f"qev{i2}", qev[i2][:])

        def front_k(c):
            i2 = c % 2
            emit_proj("wk", wk, (f"ln1T{i2}A", f"ln1T{i2}B"), ln1T[i2], f"k{c}",
                      sb_evac(f"kev{i2}", kev[i2], f"kev{c}"))
            tap("kev", c, f"kev{i2}", kev[i2][:])

        def front_v(c):
            v_sb = vaug[c % 2]
            vres = f"vaug{c % 2}"
            vv = v_sb[:].rearrange("p (h e) -> p h e", e=65)

            def evac_half(half, bank):
                pj, pjres = bank
                s.add("scalar", lambda e, half=half, pj=pj, vv=vv: e.copy(
                    vv[:, half * 8:(half + 1) * 8, 0:64],
                    pj[:].rearrange("p (h e) -> p h e", e=64)),
                    reads=[pjres], writes=[vres], tag=f"vev{c}_h{half}")
            emit_proj("wv", wv, (f"ln1T{c % 2}A", f"ln1T{c % 2}B"), ln1T[c % 2],
                      f"v{c}", evac_half)
            tap("vaug", c, vres, v_sb[:])

        def front_rope_q(c):
            i2 = c % 2
            emit_rope_phi(c, f"qev{i2}", qev[i2], "qr", qr, "qm", qm, "qE", qE,
                          f"qphi{i2}", qphi[i2], f"rq{c}")
            tap("qphi", c, f"qphi{i2}", qphi[i2][:])

        def front_rope_k(c):
            i2 = c % 2
            emit_rope_phi(c, f"kev{i2}", kev[i2], "kr", kr, "km", km, "kE", kE,
                          f"kphi{i2}", kphi[i2], f"rk{c}")
            tap("kphi", c, f"kphi{i2}", kphi[i2][:])

        def front_qkT(c):
            i2 = c % 2
            emit_transposes64(f"qphi{i2}", qphi[i2], f"qT{i2}", qT[i2], f"qT{c}")
            emit_transposes64(f"kphi{i2}", kphi[i2], f"kT{i2}", kT[i2], f"kT{c}", eng="vector")
            tap("qT", c, (f"qT{i2}A", f"qT{i2}B"), qT[i2][0:64, :])
            tap("kT", c, (f"kT{i2}A", f"kT{i2}B"), kT[i2][0:64, :])

        def back_attn(c):
            i2 = c % 2
            qTs, kTs = qT[i2], kT[i2]
            qTh = lambda h: f"qT{i2}A" if h < 8 else f"qT{i2}B"
            kTh = lambda h: f"kT{i2}A" if h < 8 else f"kT{i2}B"
            v_sb, vres = vaug[i2], f"vaug{i2}"
            kph, kphres = kphi[i2], f"kphi{i2}"
            kvres_r = f"KVs{i2}"
            kvres_w = f"KVs{(c + 1) % 2}"
            mb = maskt[:].unsqueeze(1).broadcast_to([128, 4, 128])
            for g in range(4):
                ab, abres = next_bank()
                for hh in range(4):
                    h = g * 4 + hh
                    s.add("tensor", lambda e, h=h, hh=hh, qTs=qTs, kTs=kTs, ab=ab: e.matmul(
                        ab[:, hh * 128:(hh + 1) * 128],
                        kTs[0:96, h * 128:(h + 1) * 128],
                        qTs[0:96, h * 128:(h + 1) * 128],
                        start=True, stop=True, skip_group_check=True),
                        reads=[qTh(h), kTh(h)], writes=[abres], tag=f"A{c}_h{h}")
                s.add("vector", lambda e, g=g, ab=ab: e.tensor_tensor(
                    out=Asb[:, g * 512:(g + 1) * 512].rearrange("p (h t) -> p h t", t=128),
                    in0=ab[:, 0:512].rearrange("p (h t) -> p h t", t=128),
                    in1=mb, op=ALU.mult),
                    reads=[abres, "mask"], writes=[f"A{g}"], tag=f"mask{c}_g{g}")
                for hh in range(4):
                    h = g * 4 + hh
                    b, off = kv_slot(h)
                    s.add("tensor", lambda e, h=h, b=b, off=off, v_sb=v_sb, kph=kph, c=c: e.matmul(
                        kv[b][0:64, off:off + 65],
                        kph[:, h * 64:(h + 1) * 64],
                        v_sb[:, h * 65:(h + 1) * 65],
                        start=False, stop=(c == NCHUNK - 1), skip_group_check=True),
                        reads=[kphres, vres], writes=[f"kv{b}"], tag=f"kv{c}_h{h}")
                tap(f"Asb{g}", c, f"A{g}", Asb[:, g * 512:(g + 1) * 512])
            if c < NCHUNK - 1:
                kvv = KVs[(c + 1) % 2]
                s.add("scalar", lambda e, kvv=kvv: e.copy(kvv[0:64, 0:455], kv[0][0:64, :]),
                      reads=["kv0"], writes=[kvres_w], tag=f"kvev{c}_0", cost=550)
                s.add("scalar", lambda e, kvv=kvv: e.copy(kvv[0:64, 455:910], kv[1][0:64, :]),
                      reads=["kv1"], writes=[kvres_w], tag=f"kvev{c}_1", cost=550)
                s.add("scalar", lambda e, kvv=kvv: e.copy(kvv[0:64, 910:1040], kv[2][0:64, :]),
                      reads=["kv2"], writes=[kvres_w], tag=f"kvev{c}_2", cost=550)
                tap("KVs", c, kvres_w, KVs[(c + 1) % 2][0:64, :])
            ytiles_after = {0: (0, 1, 2), 1: (3, 4, 5, 6), 2: (7,)}
            for p in range(3):
                nh = 2 if p == 2 else 7
                h0 = 7 * p
                yb, ybres = next_bank()
                for hh in range(nh):
                    h = h0 + hh
                    s.add("tensor", lambda e, h=h, hh=hh, v_sb=v_sb, yb=yb: e.matmul(
                        yb[:, hh * 65:hh * 65 + 65],
                        Asb[:, h * 128:(h + 1) * 128],
                        v_sb[:, h * 65:(h + 1) * 65],
                        start=True, stop=False, skip_group_check=True),
                        reads=[f"A{h // 4}", vres], writes=[ybres], tag=f"Yi{c}_h{h}")
                    s.add("tensor", lambda e, h=h, hh=hh, qTs=qTs, kvsb=KVs[i2], yb=yb: e.matmul(
                        yb[:, hh * 65:hh * 65 + 65],
                        qTs[0:96, h * 128:(h + 1) * 128],
                        kvsb[0:96, h * 65:(h + 1) * 65],
                        start=False, stop=True, skip_group_check=True),
                        reads=[qTh(h), kvres_r], writes=[ybres], tag=f"Yx{c}_h{h}")
                pyv = yb[:, 0:455].rearrange("p (h e) -> p h e", e=65)
                s.add("vector", lambda e, nh=nh, h0=h0, pyv=pyv: e.tensor_scalar(
                    out=r16a[:, h0:h0 + nh], in0=pyv[:, 0:nh, 64], scalar1=EPS_DEN,
                    scalar2=None, op0=ALU.max),
                    reads=[ybres], writes=[f"r16a{p}"], tag=f"den{c}_p{p}", cost=220)
                s.add("vector", lambda e, nh=nh, h0=h0: e.reciprocal(
                    r16b[:, h0:h0 + nh], r16a[:, h0:h0 + nh]),
                    reads=[f"r16a{p}"], writes=[f"r16b{p}"], cost=220)
                s.add("vector", lambda e, nh=nh, h0=h0, pyv=pyv: e.tensor_tensor(
                    out=Ysb[:, h0 * 64:(h0 + nh) * 64].rearrange("p (h e) -> p h e", e=64),
                    in0=pyv[:, 0:nh, 0:64],
                    in1=r16b[:, h0:h0 + nh].unsqueeze(2).broadcast_to([128, nh, 64]),
                    op=ALU.mult),
                    reads=[ybres, f"r16b{p}"], writes=[f"Y{p}"], tag=f"ydiv{c}_p{p}")
                for i in ytiles_after[p]:
                    s.add("tensor", lambda e, i=i: e.transpose(
                        ptr[:, i * 128:(i + 1) * 128], Ysb[:, i * 128:(i + 1) * 128],
                        ident[:]),
                        reads=[f"Y{min((2 * i) // 7, 2)}", f"Y{min((2 * i + 1) // 7, 2)}", "ident"],
                        writes=["ptr"], tag=f"tr_YT{c}{i}")
                tap(f"Ysb{p}", c, f"Y{p}",
                    Ysb[:, 7 * p * 64: (7 * p + nh) * 64])
            for hb in range(2):
                s.add("scalar", lambda e, hb=hb: e.copy(
                    YT[:, hb * 512:(hb + 1) * 512], ptr[:, hb * 512:(hb + 1) * 512]),
                    reads=["ptr"],
                    writes=["YTA" if hb == 0 else "YTB"],
                    tag=f"ev_YT{c}{hb}", cost=650)

        def back_wo(c):
            x_sb, xres = xs[c % 3], f"x{c % 3}"
            x2sb, x2res = x2[c % 2], f"x2_{c % 2}"

            def evac_half(half, bank):
                pj, pjres = bank
                s.add("vector", lambda e, half=half, pj=pj, x2sb=x2sb, x_sb=x_sb: e.tensor_tensor(
                    out=x2sb[:, half * 512:(half + 1) * 512], in0=pj[:],
                    in1=x_sb[:, half * 512:(half + 1) * 512], op=ALU.add),
                    reads=[pjres, xres], writes=[x2res], tag=f"res1_{c}_h{half}")
            emit_proj("wo", wo, ("YTA", "YTB"), YT, f"wo{c}", evac_half)
            tap("x2", c, x2res, x2sb[:])
            emit_ln_sums(c, x2res, x2sb, 2)

        def back_ffn(c):
            x2sb, x2res = x2[c % 2], f"x2_{c % 2}"
            osb, outres = outb[c % 2], f"out{c % 2}"
            emit_ln_rest(c, x2res, x2sb, 2, "ln2", ln2)
            emit_transposes("ln2", ln2, "ln2T", ln2T, f"ln2T{c}")

            def evac_half(half, bank):
                pj, pjres = bank
                s.add("vector", lambda e, half=half, pj=pj, osb=osb, x2sb=x2sb: e.tensor_tensor(
                    out=osb[:, half * 512:(half + 1) * 512], in0=pj[:],
                    in1=x2sb[:, half * 512:(half + 1) * 512], op=ALU.add),
                    reads=[pjres, x2res], writes=[outres], tag=f"res2_{c}_h{half}")
            emit_proj("wf", wf, ("ln2TA", "ln2TB"), ln2T, f"ffn{c}", evac_half)
            s.add("sync", lambda e, c=c, osb=osb: e.dma_start(
                out=out_d[c * 128:(c + 1) * 128, :], in_=osb[:]),
                reads=[outres], dma=True, tag=f"store{c}", cost=3500)

        if NCHUNK > 2:
            s.add("sync", lambda e: e.dma_start(out=xs[2][:], in_=x_d[256:384, :]),
                  writes=["x2"], dma=True, tag="load_x2", cost=3500)
        front_ln_sums(0)
        front_ln_rest(0)
        front_q(0)
        front_rope_q(0)
        front_k(0)
        front_rope_k(0)
        front_v(0)
        front_qkT(0)
        for c in range(NCHUNK):
            n = c + 1
            if n < NCHUNK:
                front_ln_sums(n)
            back_attn(c)
            if n < NCHUNK:
                front_ln_rest(n)
                front_q(n)
                front_rope_q(n)
            back_wo(c)
            if n < NCHUNK:
                front_k(n)
                front_rope_k(n)
            back_ffn(c)
            if n < NCHUNK:
                front_v(n)
                front_qkT(n)
            if c + 3 < NCHUNK:
                s.add("sync", lambda e, c=c: e.dma_start(
                    out=xs[(c + 3) % 3][:], in_=x_d[(c + 3) * 128:(c + 4) * 128, :]),
                    writes=[f"x{(c + 3) % 3}"], dma=True, tag=f"load_x{c + 3}", cost=3500)

        order = s.schedule()
        waits, needs_inc, dma_no, inc_no = s.plan(order)
        by_engine = {e: [s.ops[gi] for gi in order if s.ops[gi]["engine"] == e]
                     for e in ENGINES}
        n_sync_dma = sum(1 for op in by_engine["sync"] if op["dma"])

        def emit_engine(name, eng):
            for op in by_engine[name]:
                i = op["i"]
                for (kind, tgt, val) in waits[i]:
                    if kind == "dma":
                        eng.wait_ge(dsems[tgt], 16 * val)
                    else:
                        eng.wait_ge(sems[tgt], inc_no[val])
                instr = op["fn"](eng)
                if op["dma"]:
                    k = dma_no[op["i"]]
                    instr.then_inc(dsems[(name, (k - 1) % 3)], 16)
                elif needs_inc[i] and instr is not None:
                    instr.then_inc(sems[name], 1)
            if name in ("sync", "gpsimd"):
                nq = sum(1 for op in by_engine[name] if op["dma"])
                for j in range(3):
                    m_tot = (nq - j + 2) // 3
                    if m_tot > 0:
                        eng.wait_ge(dsems[(name, j)], 16 * m_tot)

        @block.tensor
        def _(tensor):
            emit_engine("tensor", tensor)

        @block.vector
        def _(vector):
            emit_engine("vector", vector)

        @block.scalar
        def _(scalar):
            emit_engine("scalar", scalar)

        @block.gpsimd
        def _(gpsimd):
            emit_engine("gpsimd", gpsimd)

        @block.sync
        def _(sync):
            emit_engine("sync", sync)

    return nc


def _host_tables():
    inv = 10000.0 ** (-np.arange(32, dtype=np.float64) / 32.0)
    ang = np.arange(T_SEQ, dtype=np.float64)[:, None] * inv[None, :]
    C = np.cos(ang).reshape(NCHUNK, 128, 32).transpose(1, 0, 2).reshape(128, NCHUNK * 32)
    S = np.sin(ang).reshape(NCHUNK, 128, 32).transpose(1, 0, 2).reshape(128, NCHUNK * 32)
    mask = (np.arange(128)[:, None] <= np.arange(128)[None, :]).astype(np.float64)
    ident = np.eye(128)
    bf = ml_dtypes.bfloat16
    return C.astype(np.float32), S.astype(np.float32), mask.astype(bf), ident.astype(bf)


def kernel(L0_attn_x, L0_attn_Wq, L0_attn_Wk, L0_attn_Wv, L0_attn_Wo, L0_ffn_weight):
    x = np.asarray(L0_attn_x, dtype=np.float32)
    wqT = np.ascontiguousarray(np.asarray(L0_attn_Wq, dtype=np.float32).T)
    wkT = np.ascontiguousarray(np.asarray(L0_attn_Wk, dtype=np.float32).T)
    wvT = np.ascontiguousarray(np.asarray(L0_attn_Wv, dtype=np.float32).T)
    woT = np.ascontiguousarray(np.asarray(L0_attn_Wo, dtype=np.float32).T)
    wfT = np.ascontiguousarray(np.asarray(L0_ffn_weight, dtype=np.float32).T)
    tabC, tabS, mask, ident = _host_tables()

    nc = build_program()
    in_maps = []
    for core in range(N_CORES):
        in_maps.append({
            "x": np.ascontiguousarray(x[core]),
            "wqT": wqT, "wkT": wkT, "wvT": wvT, "woT": woT, "wfT": wfT,
            "tabC": tabC, "tabS": tabS, "mask": mask, "ident": ident,
        })
    res = run_bass_kernel_spmd(nc, in_maps, core_ids=list(range(N_CORES)))
    out = np.stack([np.asarray(r["out"]) for r in res.results], axis=0)
    return out


# version 45
# speedup vs baseline: 1.0228x; 1.0228x over previous
import os
import numpy as np
import ml_dtypes
import concourse.bass as bass
import concourse.mybir as mybir
from concourse.bass_utils import run_bass_kernel_spmd
from contextlib import ExitStack

F32 = mybir.dt.float32
BF16 = mybir.dt.bfloat16
AF = mybir.ActivationFunctionType
ALU = mybir.AluOpType

T_SEQ = 4096
D = 1024
H = 16
DH = 64
CH = 128
NCHUNK = T_SEQ // CH
N_CORES = 8
EPS_LN = 1e-5
EPS_DEN = 1e-6

ENGINES = ("tensor", "vector", "scalar", "gpsimd", "sync")

DEBUG_TAPS = ()


class Sched:

    def __init__(self):
        self.ops = []
        self.lastw = {}
        self.readers = {}
        self.eng_n = {e: 0 for e in ENGINES}

    DEF_COST = {"tensor": 70, "vector": 700, "scalar": 900, "gpsimd": 1200,
                "sync": 1500}

    def add(self, engine, fn, reads=(), writes=(), dma=False, tag="", cost=None):
        i = len(self.ops)
        deps = set()
        for r in reads:
            w = self.lastw.get(r)
            if w is not None:
                deps.add(w)
        for r in writes:
            w = self.lastw.get(r)
            if w is not None:
                deps.add(w)
            deps.update(self.readers.get(r, ()))
        deps.discard(i)
        self.ops.append(dict(i=i, engine=engine, fn=fn, deps=sorted(deps),
                             dma=dma, tag=tag,
                             cost=float(cost if cost is not None
                                        else self.DEF_COST[engine])))
        for r in reads:
            self.readers.setdefault(r, []).append(i)
        for r in writes:
            self.lastw[r] = i
            self.readers[r] = []
        return i

    def schedule(self):
        ops = self.ops
        n = len(ops)
        succ = [[] for _ in range(n)]
        pend = [0] * n
        for op in ops:
            for d in op["deps"]:
                succ[d].append(op["i"])
                pend[op["i"]] += 1
        LAT = float(os.environ.get("SIMLAT", 520.0))
        prio = [0.0] * n
        for i in range(n - 1, -1, -1):
            m = 0.0
            for sj in succ[i]:
                if prio[sj] > m:
                    m = prio[sj]
            prio[i] = ops[i]["cost"] + (LAT if succ[i] else 0.0) + m
        eng_free = {e: 0.0 for e in ENGINES}
        avail = [0.0] * n
        done = [0.0] * n
        ready = {e: [] for e in ENGINES}
        for i in range(n):
            if pend[i] == 0:
                ready[ops[i]["engine"]].append(i)
        order = []
        remaining = n
        while remaining:
            best = None
            for e in ENGINES:
                lst = ready[e]
                if not lst:
                    continue
                ef = eng_free[e]
                for i in lst:
                    st = ef if ef > avail[i] else avail[i]
                    key = (st, -prio[i], i)
                    if best is None or key < best[0]:
                        best = (key, e, i)
            key, e, i = best
            st = key[0]
            ready[e].remove(i)
            done[i] = st + ops[i]["cost"]
            eng_free[e] = done[i]
            order.append(i)
            remaining -= 1
            for sj in succ[i]:
                pend[sj] -= 1
                t = done[i] + (LAT if ops[sj]["engine"] != e else 0.0)
                if avail[sj] < t:
                    avail[sj] = t
                if pend[sj] == 0:
                    ready[ops[sj]["engine"]].append(sj)
        cnt = {e: 0 for e in ENGINES}
        for i in order:
            ops[i]["eidx"] = cnt[ops[i]["engine"]]
            cnt[ops[i]["engine"]] += 1
        self.sim_makespan = max(done) if n else 0.0
        if os.environ.get('SIM_PRINT'):
            print('SIM makespan:', round(self.sim_makespan / 1000, 1), 'us')
            start_t = {i: done[i] - ops[i]["cost"] for i in range(n)}
            eng_prev = {}
            last = {e: None for e in ENGINES}
            for i in order:
                e2 = ops[i]["engine"]
                eng_prev[i] = last[e2]
                last[e2] = i
            cur = max(range(n), key=lambda i: done[i])
            from collections import Counter
            hist = Counter(); hist_t = Counter()
            while cur is not None:
                st = start_t[cur]
                nxt = None
                for d in ops[cur]["deps"]:
                    t = done[d] + (LAT if ops[d]["engine"] != ops[cur]["engine"] else 0)
                    if abs(t - st) < 1e-6:
                        nxt = d
                        break
                if nxt is None:
                    ep = eng_prev[cur]
                    if ep is not None and abs(done[ep] - st) < 1e-6:
                        nxt = ep
                import re as _re
                key = _re.sub(r"\d+", "#", ops[cur]["tag"]) or ops[cur]["engine"]
                hist[key] += 1; hist_t[key] += ops[cur]["cost"]
                cur = nxt
            print("SIM critical-path composition (count, total_us):")
            for k, v in hist.most_common(16):
                print(f"  {k:24s} n={v:5d}  t={hist_t[k]/1000:.1f}us")
        return order

    def plan(self, order):
        ops = self.ops
        n = len(ops)
        waits = [[] for _ in range(n)]
        needs_inc = [False] * n
        dma_no = {}
        dma_count = {e: 0 for e in ENGINES}
        clocks = [None] * n
        last_on = {e: None for e in ENGINES}

        def fresh_clock():
            c = {e: -1 for e in ENGINES}
            for e in ENGINES:
                for j in range(3):
                    c[f"{e}_dma{j}"] = 0
            return c

        for gi in order:
            op = ops[gi]
            i = op["i"]
            eng = op["engine"]
            prev = last_on[eng]
            clock = dict(clocks[prev]) if prev is not None else fresh_clock()
            if op["dma"]:
                dma_count[eng] += 1
                k = dma_count[eng]
                dma_no[i] = k
                j = (k - 1) % 3
                m_prev = (k - 1) // 3
                if m_prev > 0 and clock[f"{eng}_dma{j}"] < m_prev:
                    waits[i].append(("dma", (eng, j), m_prev))
                    clock[f"{eng}_dma{j}"] = m_prev
            for d in op["deps"]:
                dop = ops[d]
                de = dop["engine"]
                if dop["dma"]:
                    k = dma_no[d]
                    j = (k - 1) % 3
                    m = (k - 1) // 3 + 1
                    if clock[f"{de}_dma{j}"] >= m:
                        continue
                    waits[i].append(("dma", (de, j), m))
                    clock[f"{de}_dma{j}"] = m
                    dc = clocks[d]
                    for key, v in dc.items():
                        if clock[key] < v:
                            clock[key] = v
                else:
                    if de == eng and de in ("tensor", "sync"):
                        continue
                    if clock[de] >= dop["eidx"]:
                        continue
                    waits[i].append(("eng", de, d))
                    needs_inc[d] = True
                    clock[de] = dop["eidx"]
                    dc = clocks[d]
                    for key, v in dc.items():
                        if clock[key] < v:
                            clock[key] = v
            clocks[i] = clock
            last_on[eng] = i
        inc_no = {}
        cnt = {e: 0 for e in ENGINES}
        for gi in order:
            op = ops[gi]
            if needs_inc[op["i"]] and not op["dma"]:
                cnt[op["engine"]] += 1
                inc_no[op["i"]] = cnt[op["engine"]]
        return waits, needs_inc, dma_no, inc_no


def build_program():
    nc = bass.Bass("TRN2", target_bir_lowering=False)

    x_d = nc.dram_tensor("x", [T_SEQ, D], F32, kind="ExternalInput").ap()
    wq_d = nc.dram_tensor("wqT", [D, D], F32, kind="ExternalInput").ap()
    wk_d = nc.dram_tensor("wkT", [D, D], F32, kind="ExternalInput").ap()
    wv_d = nc.dram_tensor("wvT", [D, D], F32, kind="ExternalInput").ap()
    wo_d = nc.dram_tensor("woT", [D, D], F32, kind="ExternalInput").ap()
    wf_d = nc.dram_tensor("wfT", [D, D], F32, kind="ExternalInput").ap()
    tabC_d = nc.dram_tensor("tabC", [128, NCHUNK * 32], F32, kind="ExternalInput").ap()
    tabS_d = nc.dram_tensor("tabS", [128, NCHUNK * 32], F32, kind="ExternalInput").ap()
    mask_d = nc.dram_tensor("mask", [128, 128], BF16, kind="ExternalInput").ap()
    ident_d = nc.dram_tensor("ident", [128, 128], BF16, kind="ExternalInput").ap()
    out_d = nc.dram_tensor("out", [T_SEQ, D], F32, kind="ExternalOutput").ap()

    s = Sched()

    with ExitStack() as es:
        ec = es.enter_context

        wq = ec(nc.sbuf_tensor("wq", [128, 8 * D], BF16))
        wk = ec(nc.sbuf_tensor("wk", [128, 8 * D], BF16))
        wv = ec(nc.sbuf_tensor("wv", [128, 8 * D], BF16))
        wo = ec(nc.sbuf_tensor("wo", [128, 8 * D], BF16))
        wf = ec(nc.sbuf_tensor("wf", [128, 8 * D], BF16))
        tabC = ec(nc.sbuf_tensor("tabC_s", [128, NCHUNK * 32], BF16))
        tabS = ec(nc.sbuf_tensor("tabS_s", [128, NCHUNK * 32], BF16))
        maskt = ec(nc.sbuf_tensor("mask_s", [128, 128], BF16))
        ident = ec(nc.sbuf_tensor("ident_s", [128, 128], BF16))
        xs = [ec(nc.sbuf_tensor(f"xs{i}", [128, D], F32)) for i in range(3)]
        junkV = [ec(nc.sbuf_tensor(f"junkV{i}", [128, D], BF16)) for i in range(3)]
        junkS = [ec(nc.sbuf_tensor(f"junkS{i}", [128, D], BF16)) for i in range(3)]
        ln1 = [ec(nc.sbuf_tensor(f"ln1_{i}", [128, D], BF16)) for i in range(2)]
        ln2 = ec(nc.sbuf_tensor("ln2", [128, D], BF16))
        ln1T = [ec(nc.sbuf_tensor(f"ln1T{i}", [128, D], BF16)) for i in range(2)]
        ln2T = ec(nc.sbuf_tensor("ln2T", [128, D], BF16))
        qev = [ec(nc.sbuf_tensor(f"qev{i}", [128, D], BF16)) for i in range(2)]
        kev = [ec(nc.sbuf_tensor(f"kev{i}", [128, D], BF16)) for i in range(2)]
        qr = ec(nc.sbuf_tensor("qr", [128, D], BF16))
        kr = ec(nc.sbuf_tensor("kr", [128, D], BF16))
        qm = ec(nc.sbuf_tensor("qm", [128, D], BF16))
        km = ec(nc.sbuf_tensor("km", [128, D], BF16))
        qE = ec(nc.sbuf_tensor("qE", [128, D], BF16))
        kE = ec(nc.sbuf_tensor("kE", [128, D], BF16))
        qphi = [ec(nc.sbuf_tensor(f"qphi{i}", [128, D], BF16)) for i in range(2)]
        kphi = [ec(nc.sbuf_tensor(f"kphi{i}", [128, D], BF16)) for i in range(2)]
        rA = ec(nc.sbuf_tensor("rA", [128, 512], BF16))
        rB = ec(nc.sbuf_tensor("rB", [128, 512], BF16))
        rC = ec(nc.sbuf_tensor("rC", [128, 512], BF16))
        rD = ec(nc.sbuf_tensor("rD", [128, 512], BF16))
        qT = [ec(nc.sbuf_tensor(f"qT{i}", [96, H * 128], BF16)) for i in range(2)]
        kT = [ec(nc.sbuf_tensor(f"kT{i}", [96, H * 128], BF16)) for i in range(2)]
        vaug = [ec(nc.sbuf_tensor(f"vaug{i}", [128, H * 65], BF16)) for i in range(2)]
        Asb = ec(nc.sbuf_tensor("Asb", [128, H * 128], BF16))
        KVs = [ec(nc.sbuf_tensor(f"KVs{i}", [96, H * 65], BF16)) for i in range(2)]
        Ysb = ec(nc.sbuf_tensor("Ysb", [128, D], BF16))
        YT = ec(nc.sbuf_tensor("YT", [128, D], BF16))
        x2 = [ec(nc.sbuf_tensor(f"x2_{i}", [128, D], F32)) for i in range(2)]
        outb = [ec(nc.sbuf_tensor(f"outb{i}", [128, D], F32)) for i in range(2)]
        r16a = ec(nc.sbuf_tensor("r16a", [128, 16], F32))
        epsc = ec(nc.sbuf_tensor("epsc", [128, 1], F32))
        fenceV = ec(nc.sbuf_tensor("fenceV", [128, 1], F32))
        fenceS = ec(nc.sbuf_tensor("fenceS", [128, 1], F32))
        r16b = ec(nc.sbuf_tensor("r16b", [128, 16], F32))
        stats = []
        for t in range(3):
            st = {}
            for nm in ("s1", "s2", "nm", "msq", "v1", "v2", "std", "rstd", "nmr"):
                st[nm] = ec(nc.sbuf_tensor(f"st{t}_{nm}", [128, 1], F32))
            stats.append(st)

        kv = [ec(nc.psum_tensor("kv0", [64, 7 * 65], F32)),
              ec(nc.psum_tensor("kv1", [64, 7 * 65], F32)),
              ec(nc.psum_tensor("kv2", [64, 2 * 65], F32))]
        att = ec(nc.psum_tensor("att", [128, 512], F32))
        ptr = ec(nc.psum_tensor("ptr", [128, 1024], BF16))
        pjA = ec(nc.psum_tensor("pjA", [128, 512], F32))
        pjB = ec(nc.psum_tensor("pjB", [128, 512], F32))
        pjC = ec(nc.psum_tensor("pjC", [128, 512], F32))
        bank_pool = [(att, "att"), (pjA, "pjA"), (pjB, "pjB"), (pjC, "pjC")]
        bank_ctr = [0]

        def next_bank():
            b = bank_pool[bank_ctr[0] % 4]
            bank_ctr[0] += 1
            return b

        sems = {e: ec(nc.semaphore(f"sem_{e}")) for e in ("tensor", "vector", "scalar", "gpsimd")}
        dsems = {(e, j): ec(nc.semaphore(f"dsem_{e}{j}"))
                 for e in ("sync", "gpsimd") for j in range(3)}

        block = ec(nc.Block())

        for wsb, wdram, name in ((wq, wq_d, "wq"), (wk, wk_d, "wk"),
                                 (wv, wv_d, "wv"), (wo, wo_d, "wo"),
                                 (wf, wf_d, "wf")):
            s.add("gpsimd",
                  lambda e, a=wsb, b=wdram: e.dma_start(
                      out=a[:].rearrange("p (a o) -> p a o", a=8),
                      in_=b.rearrange("(a p) o -> p a o", p=128)),
                  writes=[name], dma=True, tag=f"load_{name}", cost=7000)
        s.add("scalar", lambda e: e.sign(wf[:], wf[:]),
              reads=["wf"], writes=["wf"], tag="sign_wf")
        s.add("sync", lambda e: e.dma_start(out=maskt[:], in_=mask_d),
              writes=["mask"], dma=True)
        s.add("sync", lambda e: e.dma_start(out=ident[:], in_=ident_d),
              writes=["ident"], dma=True)
        s.add("gpsimd", lambda e: e.dma_start(out=tabC[:], in_=tabC_d),
              writes=["tabC"], dma=True)
        s.add("gpsimd", lambda e: e.dma_start(out=tabS[:], in_=tabS_d),
              writes=["tabS"], dma=True)
        for i in range(2):
            s.add("vector", lambda e, i=i: e.memset(
                vaug[i][:].rearrange("p (h e) -> p h e", e=65)[:, :, 64], 1.0),
                writes=[f"vaug{i}A", f"vaug{i}B"], tag="ones_vaug")
        s.add("vector", lambda e: e.memset(KVs[0][:], 0.0), writes=["KVs0"])
        s.add("vector", lambda e: e.memset(KVs[1][64:96, :], 0.0), writes=["KVs1"])
        for i in range(2):
            s.add("vector", lambda e, i=i: e.memset(qT[i][64:96, :], 0.0),
                  writes=[f"qT{i}A", f"qT{i}B"], tag="zpad_qT")
            s.add("vector", lambda e, i=i: e.memset(kT[i][64:96, :], 0.0),
                  writes=[f"kT{i}A", f"kT{i}B"], tag="zpad_kT")
        s.add("vector", lambda e: e.memset(epsc[:], EPS_LN), writes=["epsc"])
        s.add("vector", lambda e: e.memset(junkS[0][:], 0.0), writes=["junkS0"])
        for bi in range(3):
            ncols = 130 if bi == 2 else 455
            s.add("tensor",
                  lambda e, bi=bi, ncols=ncols: e.matmul(
                      kv[bi][0:64, 0:ncols], junkS[0][:, 0:64], junkS[0][:, 0:ncols],
                      start=True, stop=False, skip_group_check=True),
                  reads=["junkS0"], writes=[f"kv{bi}"], tag="kv_zero", cost=250)
            for _ in range(4):
                s.add("tensor", lambda e: e.nop(), tag="nop", cost=5)
        for j in range(min(2, NCHUNK)):
            s.add("sync", lambda e, j=j: e.dma_start(
                out=xs[j][:], in_=x_d[j * 128:(j + 1) * 128, :]),
                writes=[f"x{j}"], dma=True, tag=f"load_x{j}", cost=3500)

        tapset = set(DEBUG_TAPS)

        def tap(name, c, res, ap):
            if (name, c) not in tapset:
                return
            rl = list(res) if isinstance(res, (list, tuple)) else [res]
            shp = list(ap.shape)
            d = nc.dram_tensor(f"tap_{name}_{c}", shp, F32, kind="ExternalOutput").ap()
            s.add("gpsimd", lambda e, d=d, ap=ap: e.dma_start(out=d, in_=ap),
                  reads=rl, dma=True, tag=f"tap_{name}_{c}")

        def emit_ln_sums(c, xres, x_sb, sti):
            st = stats[sti]
            pre = f"st{sti}_"

            def R(nm):
                return pre + nm
            s.add("vector", lambda e: e.tensor_scalar(
                out=junkV[sti][:], in0=x_sb[:], scalar1=-1.0 / D, scalar2=0.0,
                op0=ALU.mult, op1=ALU.add, accum_out=st["s1"][:]),
                reads=[xres], writes=[f"junkV{sti}", R("s1")], tag=f"ln{sti}_sum")
            s.add("vector", lambda e: e.tensor_scalar(
                out=fenceV[:], in0=st["s1"][:], scalar1=1.0, scalar2=None,
                op0=ALU.mult),
                reads=[R("s1")], writes=[R("s1")], tag=f"ln{sti}_sumf", cost=80)
            s.add("scalar", lambda e: e.activation(
                junkS[sti][:], x_sb[:], AF.Square, accum_out=st["s2"][:]),
                reads=[xres], writes=[f"junkS{sti}", R("s2")], tag=f"ln{sti}_sq")
            s.add("scalar", lambda e: e.copy(fenceS[:], st["s2"][:]),
                reads=[R("s2")], writes=[R("s2")], tag=f"ln{sti}_sqf", cost=200)

        def emit_ln_rest(c, xres, x_sb, sti, lnres, ln_sb):
            st = stats[sti]
            pre = f"st{sti}_"

            def R(nm):
                return pre + nm
            s.add("scalar", lambda e: e.activation(st["msq"][:], st["s1"][:],
                                                   AF.Square),
                  reads=[R("s1")], writes=[R("msq")], cost=300)
            s.add("scalar", lambda e: e.activation(st["v2"][:], st["msq"][:],
                                                   AF.Copy, scale=-1.0,
                                                   bias=float(EPS_LN)),
                  reads=[R("msq")], writes=[R("v2")], cost=300)
            s.add("scalar", lambda e: e.activation(st["std"][:], st["s2"][:], AF.Ln,
                                                   scale=1.0 / D, bias=st["v2"][:]),
                  reads=[R("s2"), R("v2")], writes=[R("std")], cost=320)
            s.add("scalar", lambda e: e.activation(st["rstd"][:], st["std"][:], AF.Exp, scale=-0.5),
                  reads=[R("std")], writes=[R("rstd")], cost=320)
            s.add("scalar", lambda e: e.activation(st["nmr"][:], st["rstd"][:],
                                                   AF.Copy, scale=st["s1"][:]),
                  reads=[R("s1"), R("rstd")], writes=[R("nmr")], cost=300)
            s.add("vector", lambda e: e.tensor_scalar(
                out=ln_sb[:], in0=x_sb[:], scalar1=st["rstd"][:], scalar2=st["nmr"][:],
                op0=ALU.mult, op1=ALU.add),
                reads=[xres, R("rstd"), R("nmr")], writes=[lnres], tag=f"ln{sti}_apply")

        def emit_transposes(src_res, src_sb, dst_res, dst_sb, tag):
            for i in range(8):
                s.add("tensor", lambda e, i=i: e.transpose(
                    ptr[:, i * 128:(i + 1) * 128], src_sb[:, i * 128:(i + 1) * 128],
                    ident[:]),
                    reads=[src_res, "ident"],
                    writes=["ptr"], tag=f"tr_{tag}{i}")
            for hb in range(2):
                s.add("scalar", lambda e, hb=hb: e.copy(
                    dst_sb[:, hb * 512:(hb + 1) * 512],
                    ptr[:, hb * 512:(hb + 1) * 512]),
                    reads=["ptr"],
                    writes=[dst_res + ("A" if hb == 0 else "B")],
                    tag=f"ev_{tag}{hb}", cost=650)

        def emit_transposes64(src_res, src_sb, dst_res, dst_sb, tag, eng="scalar"):
            for i in range(8):
                s.add("tensor", lambda e, i=i: e.transpose(
                    ptr[:, i * 128:(i + 1) * 128], src_sb[:, i * 128:(i + 1) * 128],
                    ident[:]),
                    reads=[src_res, "ident"],
                    writes=["ptr"], tag=f"tr_{tag}{i}")
            pv = ptr[:].rearrange("p (a t) -> p a t", a=8)
            dv = dst_sb[0:64, :].rearrange("p (a hp t) -> p a hp t", a=8, hp=2)
            for hb in range(2):
                a0, a1 = 4 * hb, 4 * hb + 4
                rres = "ptr"
                wres = dst_res + ("A" if hb == 0 else "B")
                if eng == "scalar":
                    s.add("scalar", lambda e, a0=a0, a1=a1: e.copy(
                        dv[:, a0:a1, 0, :], pv[0:64, a0:a1, :]),
                        reads=[rres], writes=[wres], tag=f"ev_{tag}e{hb}", cost=360)
                    s.add("scalar", lambda e, a0=a0, a1=a1: e.copy(
                        dv[:, a0:a1, 1, :], pv[64:128, a0:a1, :]),
                        reads=[rres], writes=[wres], tag=f"ev_{tag}o{hb}", cost=360)
                else:
                    s.add("vector", lambda e, a0=a0, a1=a1: e.tensor_copy(
                        dv[:, a0:a1, 0, :], pv[0:64, a0:a1, :]),
                        reads=[rres], writes=[wres], tag=f"ev_{tag}e{hb}", cost=360)
                    s.add("vector", lambda e, a0=a0, a1=a1: e.tensor_copy(
                        dv[:, a0:a1, 1, :], pv[64:128, a0:a1, :]),
                        reads=[rres], writes=[wres], tag=f"ev_{tag}o{hb}", cost=360)

        def emit_proj(wres, w_sb, lres, lsb, tag, evac_half):
            lresA, lresB = lres
            banks = (next_bank(), next_bank())
            for o in range(8):
                pj, pjres = banks[0] if o < 4 else banks[1]
                off = (o % 4) * 128
                for i in range(8):
                    s.add("tensor", lambda e, o=o, i=i, pj=pj, off=off, lsb=lsb, w_sb=w_sb: e.matmul(
                        pj[:, off:off + 128],
                        lsb[:, i * 128:(i + 1) * 128],
                        w_sb[:, i * D + o * 128: i * D + o * 128 + 128],
                        start=(i == 0), stop=(i == 7)),
                        reads=[lresA if i < 4 else lresB, wres],
                        writes=[pjres], tag=f"{tag}_o{o}i{i}")
                if o == 3:
                    evac_half(0, banks[0])
            evac_half(1, banks[1])

        def sb_evac(dst_res, dst_sb, tag):
            def evac_half(half, bank):
                pj, pjres = bank
                s.add("scalar", lambda e, half=half, pj=pj, dst_sb=dst_sb: e.copy(
                    dst_sb[:, half * 512:(half + 1) * 512], pj[:]),
                    reads=[pjres], writes=[dst_res], tag=f"{tag}_h{half}")
            return evac_half

        def emit_rope_phi(c, src_res, src_sb, rdst_res, rdst, mdst_res, mdst,
                          edst_res, edst, pdst_res, pdst, tag):
            sv = src_sb[:].rearrange("p (h x j) -> p x h j", x=2, j=32)
            LO, HI = sv[:, 0], sv[:, 1]
            Cb = tabC[:, c * 32:(c + 1) * 32].unsqueeze(1).broadcast_to([128, H, 32])
            Sb = tabS[:, c * 32:(c + 1) * 32].unsqueeze(1).broadcast_to([128, H, 32])
            rAv = rA[:].rearrange("p (h j) -> p h j", j=32)
            rBv = rB[:].rearrange("p (h j) -> p h j", j=32)
            rCv = rC[:].rearrange("p (h j) -> p h j", j=32)
            rDv = rD[:].rearrange("p (h j) -> p h j", j=32)
            dv = rdst[:].rearrange("p (h x j) -> p x h j", x=2, j=32)
            s.add("gpsimd", lambda e: e.tensor_tensor(out=rAv, in0=HI, in1=Sb, op=ALU.mult),
                  reads=[src_res, "tabS"], writes=["rA"], tag=f"{tag}_rA")
            s.add("gpsimd", lambda e: e.tensor_tensor(out=rBv, in0=LO, in1=Sb, op=ALU.mult),
                  reads=[src_res, "tabS"], writes=["rB"], tag=f"{tag}_rB")
            s.add("vector", lambda e: e.tensor_tensor(out=rCv, in0=LO, in1=Cb, op=ALU.mult),
                  reads=[src_res, "tabC"], writes=["rC"], tag=f"{tag}_rC")
            s.add("vector", lambda e: e.tensor_tensor(out=rDv, in0=HI, in1=Cb, op=ALU.mult),
                  reads=[src_res, "tabC"], writes=["rD"], tag=f"{tag}_rD")
            s.add("vector", lambda e: e.tensor_tensor(out=dv[:, 0], in0=rCv, in1=rAv, op=ALU.subtract),
                  reads=["rC", "rA"], writes=[rdst_res], tag=f"{tag}_lo")
            s.add("vector", lambda e: e.tensor_tensor(out=dv[:, 1], in0=rBv, in1=rDv, op=ALU.add),
                  reads=["rB", "rD"], writes=[rdst_res], tag=f"{tag}_hi")
            s.add("vector", lambda e: e.tensor_scalar(
                out=mdst[:], in0=rdst[:], scalar1=0.0, scalar2=None, op0=ALU.min),
                reads=[rdst_res], writes=[mdst_res], tag=f"{tag}_min")
            s.add("scalar", lambda e: e.activation(edst[:], mdst[:], AF.Exp),
                  reads=[mdst_res], writes=[edst_res], tag=f"{tag}_exp")
            s.add("vector", lambda e: e.scalar_tensor_tensor(
                out=pdst[:], in0=rdst[:], scalar=0.0, in1=edst[:],
                op0=ALU.max, op1=ALU.add),
                reads=[rdst_res, edst_res], writes=[pdst_res], tag=f"{tag}_phi")

        def kv_slot(h):
            b = min(h // 7, 2)
            return b, (h - 7 * b) * 65

        def front_ln_sums(c):
            emit_ln_sums(c, f"x{c % 3}", xs[c % 3], c % 2)

        def front_ln_rest(c):
            i2 = c % 2
            emit_ln_rest(c, f"x{c % 3}", xs[c % 3], i2, f"ln1_{i2}", ln1[i2])
            tap("ln1", c, f"ln1_{i2}", ln1[i2][:])
            emit_transposes(f"ln1_{i2}", ln1[i2], f"ln1T{i2}", ln1T[i2], f"ln1T{c}")
            tap("ln1T", c, (f"ln1T{i2}A", f"ln1T{i2}B"), ln1T[i2][:])

        def front_q(c):
            i2 = c % 2
            emit_proj("wq", wq, (f"ln1T{i2}A", f"ln1T{i2}B"), ln1T[i2], f"q{c}",
                      sb_evac(f"qev{i2}", qev[i2], f"qev{c}"))
            tap("qev", c, f"qev{i2}", qev[i2][:])

        def front_k(c):
            i2 = c % 2
            emit_proj("wk", wk, (f"ln1T{i2}A", f"ln1T{i2}B"), ln1T[i2], f"k{c}",
                      sb_evac(f"kev{i2}", kev[i2], f"kev{c}"))
            tap("kev", c, f"kev{i2}", kev[i2][:])

        def front_v(c):
            v_sb = vaug[c % 2]
            vres = f"vaug{c % 2}"
            vv = v_sb[:].rearrange("p (h e) -> p h e", e=65)

            def evac_half(half, bank):
                pj, pjres = bank
                s.add("scalar", lambda e, half=half, pj=pj, vv=vv: e.copy(
                    vv[:, half * 8:(half + 1) * 8, 0:64],
                    pj[:].rearrange("p (h e) -> p h e", e=64)),
                    reads=[pjres], writes=[vres + ("A" if half == 0 else "B")],
                    tag=f"vev{c}_h{half}")
            emit_proj("wv", wv, (f"ln1T{c % 2}A", f"ln1T{c % 2}B"), ln1T[c % 2],
                      f"v{c}", evac_half)
            tap("vaug", c, (vres + "A", vres + "B"), v_sb[:])

        def front_rope_q(c):
            i2 = c % 2
            emit_rope_phi(c, f"qev{i2}", qev[i2], "qr", qr, "qm", qm, "qE", qE,
                          f"qphi{i2}", qphi[i2], f"rq{c}")
            tap("qphi", c, f"qphi{i2}", qphi[i2][:])

        def front_rope_k(c):
            i2 = c % 2
            emit_rope_phi(c, f"kev{i2}", kev[i2], "kr", kr, "km", km, "kE", kE,
                          f"kphi{i2}", kphi[i2], f"rk{c}")
            tap("kphi", c, f"kphi{i2}", kphi[i2][:])

        def front_qkT(c):
            i2 = c % 2
            emit_transposes64(f"qphi{i2}", qphi[i2], f"qT{i2}", qT[i2], f"qT{c}")
            emit_transposes64(f"kphi{i2}", kphi[i2], f"kT{i2}", kT[i2], f"kT{c}", eng="vector")
            tap("qT", c, (f"qT{i2}A", f"qT{i2}B"), qT[i2][0:64, :])
            tap("kT", c, (f"kT{i2}A", f"kT{i2}B"), kT[i2][0:64, :])

        def back_attn(c):
            i2 = c % 2
            qTs, kTs = qT[i2], kT[i2]
            qTh = lambda h: f"qT{i2}A" if h < 8 else f"qT{i2}B"
            kTh = lambda h: f"kT{i2}A" if h < 8 else f"kT{i2}B"
            v_sb, vres = vaug[i2], f"vaug{i2}"
            kph, kphres = kphi[i2], f"kphi{i2}"
            kvres_r = f"KVs{i2}"
            kvres_w = f"KVs{(c + 1) % 2}"
            mb = maskt[:].unsqueeze(1).broadcast_to([128, 4, 128])
            for g in range(4):
                ab, abres = next_bank()
                for hh in range(4):
                    h = g * 4 + hh
                    s.add("tensor", lambda e, h=h, hh=hh, qTs=qTs, kTs=kTs, ab=ab: e.matmul(
                        ab[:, hh * 128:(hh + 1) * 128],
                        kTs[0:96, h * 128:(h + 1) * 128],
                        qTs[0:96, h * 128:(h + 1) * 128],
                        start=True, stop=True, skip_group_check=True),
                        reads=[qTh(h), kTh(h)], writes=[abres], tag=f"A{c}_h{h}")
                s.add("vector", lambda e, g=g, ab=ab: e.tensor_tensor(
                    out=Asb[:, g * 512:(g + 1) * 512].rearrange("p (h t) -> p h t", t=128),
                    in0=ab[:, 0:512].rearrange("p (h t) -> p h t", t=128),
                    in1=mb, op=ALU.mult),
                    reads=[abres, "mask"], writes=[f"A{g}"], tag=f"mask{c}_g{g}")
                for hh in range(4):
                    h = g * 4 + hh
                    b, off = kv_slot(h)
                    s.add("tensor", lambda e, h=h, b=b, off=off, v_sb=v_sb, kph=kph, c=c: e.matmul(
                        kv[b][0:64, off:off + 65],
                        kph[:, h * 64:(h + 1) * 64],
                        v_sb[:, h * 65:(h + 1) * 65],
                        start=False, stop=(c == NCHUNK - 1), skip_group_check=True),
                        reads=[kphres, vres + ("A" if h < 8 else "B")],
                        writes=[f"kv{b}"], tag=f"kv{c}_h{h}")
                tap(f"Asb{g}", c, f"A{g}", Asb[:, g * 512:(g + 1) * 512])
            if c < NCHUNK - 1:
                kvv = KVs[(c + 1) % 2]
                s.add("scalar", lambda e, kvv=kvv: e.copy(kvv[0:64, 0:455], kv[0][0:64, :]),
                      reads=["kv0"], writes=[kvres_w], tag=f"kvev{c}_0", cost=550)
                s.add("scalar", lambda e, kvv=kvv: e.copy(kvv[0:64, 455:910], kv[1][0:64, :]),
                      reads=["kv1"], writes=[kvres_w], tag=f"kvev{c}_1", cost=550)
                s.add("scalar", lambda e, kvv=kvv: e.copy(kvv[0:64, 910:1040], kv[2][0:64, :]),
                      reads=["kv2"], writes=[kvres_w], tag=f"kvev{c}_2", cost=550)
                tap("KVs", c, kvres_w, KVs[(c + 1) % 2][0:64, :])
            ytiles_after = {0: (0, 1, 2), 1: (3, 4, 5, 6), 2: (7,)}
            for p in range(3):
                nh = 2 if p == 2 else 7
                h0 = 7 * p
                yb, ybres = next_bank()
                for hh in range(nh):
                    h = h0 + hh
                    s.add("tensor", lambda e, h=h, hh=hh, v_sb=v_sb, yb=yb: e.matmul(
                        yb[:, hh * 65:hh * 65 + 65],
                        Asb[:, h * 128:(h + 1) * 128],
                        v_sb[:, h * 65:(h + 1) * 65],
                        start=True, stop=False, skip_group_check=True),
                        reads=[f"A{h // 4}", vres + ("A" if h < 8 else "B")],
                        writes=[ybres], tag=f"Yi{c}_h{h}")
                    s.add("tensor", lambda e, h=h, hh=hh, qTs=qTs, kvsb=KVs[i2], yb=yb: e.matmul(
                        yb[:, hh * 65:hh * 65 + 65],
                        qTs[0:96, h * 128:(h + 1) * 128],
                        kvsb[0:96, h * 65:(h + 1) * 65],
                        start=False, stop=True, skip_group_check=True),
                        reads=[qTh(h), kvres_r], writes=[ybres], tag=f"Yx{c}_h{h}")
                pyv = yb[:, 0:455].rearrange("p (h e) -> p h e", e=65)
                s.add("vector", lambda e, nh=nh, h0=h0, pyv=pyv: e.tensor_scalar(
                    out=r16a[:, h0:h0 + nh], in0=pyv[:, 0:nh, 64], scalar1=EPS_DEN,
                    scalar2=None, op0=ALU.max),
                    reads=[ybres], writes=[f"r16a{p}"], tag=f"den{c}_p{p}", cost=220)
                s.add("vector", lambda e, nh=nh, h0=h0: e.reciprocal(
                    r16b[:, h0:h0 + nh], r16a[:, h0:h0 + nh]),
                    reads=[f"r16a{p}"], writes=[f"r16b{p}"], cost=220)
                s.add("vector", lambda e, nh=nh, h0=h0, pyv=pyv: e.tensor_tensor(
                    out=Ysb[:, h0 * 64:(h0 + nh) * 64].rearrange("p (h e) -> p h e", e=64),
                    in0=pyv[:, 0:nh, 0:64],
                    in1=r16b[:, h0:h0 + nh].unsqueeze(2).broadcast_to([128, nh, 64]),
                    op=ALU.mult),
                    reads=[ybres, f"r16b{p}"], writes=[f"Y{p}"], tag=f"ydiv{c}_p{p}")
                for i in ytiles_after[p]:
                    s.add("tensor", lambda e, i=i: e.transpose(
                        ptr[:, i * 128:(i + 1) * 128], Ysb[:, i * 128:(i + 1) * 128],
                        ident[:]),
                        reads=[f"Y{min((2 * i) // 7, 2)}", f"Y{min((2 * i + 1) // 7, 2)}", "ident"],
                        writes=["ptr"], tag=f"tr_YT{c}{i}")
                tap(f"Ysb{p}", c, f"Y{p}",
                    Ysb[:, 7 * p * 64: (7 * p + nh) * 64])
            for hb in range(2):
                s.add("scalar", lambda e, hb=hb: e.copy(
                    YT[:, hb * 512:(hb + 1) * 512], ptr[:, hb * 512:(hb + 1) * 512]),
                    reads=["ptr"],
                    writes=["YTA" if hb == 0 else "YTB"],
                    tag=f"ev_YT{c}{hb}", cost=650)

        def back_wo(c):
            x_sb, xres = xs[c % 3], f"x{c % 3}"
            x2sb, x2res = x2[c % 2], f"x2_{c % 2}"

            def evac_half(half, bank):
                pj, pjres = bank
                s.add("vector", lambda e, half=half, pj=pj, x2sb=x2sb, x_sb=x_sb: e.tensor_tensor(
                    out=x2sb[:, half * 512:(half + 1) * 512], in0=pj[:],
                    in1=x_sb[:, half * 512:(half + 1) * 512], op=ALU.add),
                    reads=[pjres, xres], writes=[x2res], tag=f"res1_{c}_h{half}")
            emit_proj("wo", wo, ("YTA", "YTB"), YT, f"wo{c}", evac_half)
            tap("x2", c, x2res, x2sb[:])
            emit_ln_sums(c, x2res, x2sb, 2)

        def back_ffn(c):
            x2sb, x2res = x2[c % 2], f"x2_{c % 2}"
            osb, outres = outb[c % 2], f"out{c % 2}"
            emit_ln_rest(c, x2res, x2sb, 2, "ln2", ln2)
            emit_transposes("ln2", ln2, "ln2T", ln2T, f"ln2T{c}")

            def evac_half(half, bank):
                pj, pjres = bank
                s.add("vector", lambda e, half=half, pj=pj, osb=osb, x2sb=x2sb: e.tensor_tensor(
                    out=osb[:, half * 512:(half + 1) * 512], in0=pj[:],
                    in1=x2sb[:, half * 512:(half + 1) * 512], op=ALU.add),
                    reads=[pjres, x2res], writes=[outres], tag=f"res2_{c}_h{half}")
            emit_proj("wf", wf, ("ln2TA", "ln2TB"), ln2T, f"ffn{c}", evac_half)
            s.add("sync", lambda e, c=c, osb=osb: e.dma_start(
                out=out_d[c * 128:(c + 1) * 128, :], in_=osb[:]),
                reads=[outres], dma=True, tag=f"store{c}", cost=3500)

        if NCHUNK > 2:
            s.add("sync", lambda e: e.dma_start(out=xs[2][:], in_=x_d[256:384, :]),
                  writes=["x2"], dma=True, tag="load_x2", cost=3500)
        front_ln_sums(0)
        front_ln_rest(0)
        front_q(0)
        front_rope_q(0)
        front_k(0)
        front_rope_k(0)
        front_v(0)
        front_qkT(0)
        for c in range(NCHUNK):
            n = c + 1
            if n < NCHUNK:
                front_ln_sums(n)
            back_attn(c)
            if n < NCHUNK:
                front_ln_rest(n)
                front_q(n)
                front_rope_q(n)
            back_wo(c)
            if n < NCHUNK:
                front_k(n)
                front_rope_k(n)
            back_ffn(c)
            if n < NCHUNK:
                front_v(n)
                front_qkT(n)
            if c + 3 < NCHUNK:
                s.add("sync", lambda e, c=c: e.dma_start(
                    out=xs[(c + 3) % 3][:], in_=x_d[(c + 3) * 128:(c + 4) * 128, :]),
                    writes=[f"x{(c + 3) % 3}"], dma=True, tag=f"load_x{c + 3}", cost=3500)

        order = s.schedule()
        waits, needs_inc, dma_no, inc_no = s.plan(order)
        by_engine = {e: [s.ops[gi] for gi in order if s.ops[gi]["engine"] == e]
                     for e in ENGINES}
        n_sync_dma = sum(1 for op in by_engine["sync"] if op["dma"])

        def emit_engine(name, eng):
            for op in by_engine[name]:
                i = op["i"]
                for (kind, tgt, val) in waits[i]:
                    if kind == "dma":
                        eng.wait_ge(dsems[tgt], 16 * val)
                    else:
                        eng.wait_ge(sems[tgt], inc_no[val])
                instr = op["fn"](eng)
                if op["dma"]:
                    k = dma_no[op["i"]]
                    instr.then_inc(dsems[(name, (k - 1) % 3)], 16)
                elif needs_inc[i] and instr is not None:
                    instr.then_inc(sems[name], 1)
            if name in ("sync", "gpsimd"):
                nq = sum(1 for op in by_engine[name] if op["dma"])
                for j in range(3):
                    m_tot = (nq - j + 2) // 3
                    if m_tot > 0:
                        eng.wait_ge(dsems[(name, j)], 16 * m_tot)

        @block.tensor
        def _(tensor):
            emit_engine("tensor", tensor)

        @block.vector
        def _(vector):
            emit_engine("vector", vector)

        @block.scalar
        def _(scalar):
            emit_engine("scalar", scalar)

        @block.gpsimd
        def _(gpsimd):
            emit_engine("gpsimd", gpsimd)

        @block.sync
        def _(sync):
            emit_engine("sync", sync)

    return nc


def _host_tables():
    inv = 10000.0 ** (-np.arange(32, dtype=np.float64) / 32.0)
    ang = np.arange(T_SEQ, dtype=np.float64)[:, None] * inv[None, :]
    C = np.cos(ang).reshape(NCHUNK, 128, 32).transpose(1, 0, 2).reshape(128, NCHUNK * 32)
    S = np.sin(ang).reshape(NCHUNK, 128, 32).transpose(1, 0, 2).reshape(128, NCHUNK * 32)
    mask = (np.arange(128)[:, None] <= np.arange(128)[None, :]).astype(np.float64)
    ident = np.eye(128)
    bf = ml_dtypes.bfloat16
    return C.astype(np.float32), S.astype(np.float32), mask.astype(bf), ident.astype(bf)


def kernel(L0_attn_x, L0_attn_Wq, L0_attn_Wk, L0_attn_Wv, L0_attn_Wo, L0_ffn_weight):
    x = np.asarray(L0_attn_x, dtype=np.float32)
    wqT = np.ascontiguousarray(np.asarray(L0_attn_Wq, dtype=np.float32).T)
    wkT = np.ascontiguousarray(np.asarray(L0_attn_Wk, dtype=np.float32).T)
    wvT = np.ascontiguousarray(np.asarray(L0_attn_Wv, dtype=np.float32).T)
    woT = np.ascontiguousarray(np.asarray(L0_attn_Wo, dtype=np.float32).T)
    wfT = np.ascontiguousarray(np.asarray(L0_ffn_weight, dtype=np.float32).T)
    tabC, tabS, mask, ident = _host_tables()

    nc = build_program()
    in_maps = []
    for core in range(N_CORES):
        in_maps.append({
            "x": np.ascontiguousarray(x[core]),
            "wqT": wqT, "wkT": wkT, "wvT": wvT, "woT": woT, "wfT": wfT,
            "tabC": tabC, "tabS": tabS, "mask": mask, "ident": ident,
        })
    res = run_bass_kernel_spmd(nc, in_maps, core_ids=list(range(N_CORES)))
    out = np.stack([np.asarray(r["out"]) for r in res.results], axis=0)
    return out


# version 46
# speedup vs baseline: 1.0240x; 1.0012x over previous
import os
import numpy as np
import ml_dtypes
import concourse.bass as bass
import concourse.mybir as mybir
from concourse.bass_utils import run_bass_kernel_spmd
from contextlib import ExitStack

F32 = mybir.dt.float32
BF16 = mybir.dt.bfloat16
AF = mybir.ActivationFunctionType
ALU = mybir.AluOpType

T_SEQ = 4096
D = 1024
H = 16
DH = 64
CH = 128
NCHUNK = T_SEQ // CH
N_CORES = 8
EPS_LN = 1e-5
EPS_DEN = 1e-6

ENGINES = ("tensor", "vector", "scalar", "gpsimd", "sync")

DEBUG_TAPS = ()


class Sched:

    def __init__(self):
        self.ops = []
        self.lastw = {}
        self.readers = {}
        self.eng_n = {e: 0 for e in ENGINES}

    DEF_COST = {"tensor": 70, "vector": 700, "scalar": 900, "gpsimd": 1200,
                "sync": 1500}

    def add(self, engine, fn, reads=(), writes=(), dma=False, tag="", cost=None):
        i = len(self.ops)
        deps = set()
        for r in reads:
            w = self.lastw.get(r)
            if w is not None:
                deps.add(w)
        for r in writes:
            w = self.lastw.get(r)
            if w is not None:
                deps.add(w)
            deps.update(self.readers.get(r, ()))
        deps.discard(i)
        self.ops.append(dict(i=i, engine=engine, fn=fn, deps=sorted(deps),
                             dma=dma, tag=tag,
                             cost=float(cost if cost is not None
                                        else self.DEF_COST[engine])))
        for r in reads:
            self.readers.setdefault(r, []).append(i)
        for r in writes:
            self.lastw[r] = i
            self.readers[r] = []
        return i

    def schedule(self):
        ops = self.ops
        n = len(ops)
        succ = [[] for _ in range(n)]
        pend = [0] * n
        for op in ops:
            for d in op["deps"]:
                succ[d].append(op["i"])
                pend[op["i"]] += 1
        LAT = float(os.environ.get("SIMLAT", 520.0))
        prio = [0.0] * n
        for i in range(n - 1, -1, -1):
            m = 0.0
            for sj in succ[i]:
                if prio[sj] > m:
                    m = prio[sj]
            prio[i] = ops[i]["cost"] + (LAT if succ[i] else 0.0) + m
        eng_free = {e: 0.0 for e in ENGINES}
        avail = [0.0] * n
        done = [0.0] * n
        ready = {e: [] for e in ENGINES}
        for i in range(n):
            if pend[i] == 0:
                ready[ops[i]["engine"]].append(i)
        order = []
        remaining = n
        while remaining:
            best = None
            for e in ENGINES:
                lst = ready[e]
                if not lst:
                    continue
                ef = eng_free[e]
                for i in lst:
                    st = ef if ef > avail[i] else avail[i]
                    key = (st, -prio[i], i)
                    if best is None or key < best[0]:
                        best = (key, e, i)
            key, e, i = best
            st = key[0]
            ready[e].remove(i)
            done[i] = st + ops[i]["cost"]
            eng_free[e] = done[i]
            order.append(i)
            remaining -= 1
            for sj in succ[i]:
                pend[sj] -= 1
                t = done[i] + (LAT if ops[sj]["engine"] != e else 0.0)
                if avail[sj] < t:
                    avail[sj] = t
                if pend[sj] == 0:
                    ready[ops[sj]["engine"]].append(sj)
        cnt = {e: 0 for e in ENGINES}
        for i in order:
            ops[i]["eidx"] = cnt[ops[i]["engine"]]
            cnt[ops[i]["engine"]] += 1
        self.sim_makespan = max(done) if n else 0.0
        if os.environ.get('SIM_PRINT'):
            print('SIM makespan:', round(self.sim_makespan / 1000, 1), 'us')
            start_t = {i: done[i] - ops[i]["cost"] for i in range(n)}
            eng_prev = {}
            last = {e: None for e in ENGINES}
            for i in order:
                e2 = ops[i]["engine"]
                eng_prev[i] = last[e2]
                last[e2] = i
            cur = max(range(n), key=lambda i: done[i])
            from collections import Counter
            hist = Counter(); hist_t = Counter()
            while cur is not None:
                st = start_t[cur]
                nxt = None
                for d in ops[cur]["deps"]:
                    t = done[d] + (LAT if ops[d]["engine"] != ops[cur]["engine"] else 0)
                    if abs(t - st) < 1e-6:
                        nxt = d
                        break
                if nxt is None:
                    ep = eng_prev[cur]
                    if ep is not None and abs(done[ep] - st) < 1e-6:
                        nxt = ep
                import re as _re
                key = _re.sub(r"\d+", "#", ops[cur]["tag"]) or ops[cur]["engine"]
                hist[key] += 1; hist_t[key] += ops[cur]["cost"]
                cur = nxt
            print("SIM critical-path composition (count, total_us):")
            for k, v in hist.most_common(16):
                print(f"  {k:24s} n={v:5d}  t={hist_t[k]/1000:.1f}us")
        return order

    def plan(self, order):
        ops = self.ops
        n = len(ops)
        waits = [[] for _ in range(n)]
        needs_inc = [False] * n
        dma_no = {}
        dma_count = {e: 0 for e in ENGINES}
        clocks = [None] * n
        last_on = {e: None for e in ENGINES}

        def fresh_clock():
            c = {e: -1 for e in ENGINES}
            for e in ENGINES:
                for j in range(3):
                    c[f"{e}_dma{j}"] = 0
            return c

        for gi in order:
            op = ops[gi]
            i = op["i"]
            eng = op["engine"]
            prev = last_on[eng]
            clock = dict(clocks[prev]) if prev is not None else fresh_clock()
            if op["dma"]:
                dma_count[eng] += 1
                k = dma_count[eng]
                dma_no[i] = k
                j = (k - 1) % 3
                m_prev = (k - 1) // 3
                if m_prev > 0 and clock[f"{eng}_dma{j}"] < m_prev:
                    waits[i].append(("dma", (eng, j), m_prev))
                    clock[f"{eng}_dma{j}"] = m_prev
            for d in op["deps"]:
                dop = ops[d]
                de = dop["engine"]
                if dop["dma"]:
                    k = dma_no[d]
                    j = (k - 1) % 3
                    m = (k - 1) // 3 + 1
                    if clock[f"{de}_dma{j}"] >= m:
                        continue
                    waits[i].append(("dma", (de, j), m))
                    clock[f"{de}_dma{j}"] = m
                    dc = clocks[d]
                    for key, v in dc.items():
                        if clock[key] < v:
                            clock[key] = v
                else:
                    if de == eng and de in ("tensor", "sync"):
                        continue
                    if clock[de] >= dop["eidx"]:
                        continue
                    waits[i].append(("eng", de, d))
                    needs_inc[d] = True
                    clock[de] = dop["eidx"]
                    dc = clocks[d]
                    for key, v in dc.items():
                        if clock[key] < v:
                            clock[key] = v
            clocks[i] = clock
            last_on[eng] = i
        inc_no = {}
        cnt = {e: 0 for e in ENGINES}
        for gi in order:
            op = ops[gi]
            if needs_inc[op["i"]] and not op["dma"]:
                cnt[op["engine"]] += 1
                inc_no[op["i"]] = cnt[op["engine"]]
        return waits, needs_inc, dma_no, inc_no


def build_program():
    nc = bass.Bass("TRN2", target_bir_lowering=False)

    x_d = nc.dram_tensor("x", [T_SEQ, D], F32, kind="ExternalInput").ap()
    wq_d = nc.dram_tensor("wqT", [D, D], F32, kind="ExternalInput").ap()
    wk_d = nc.dram_tensor("wkT", [D, D], F32, kind="ExternalInput").ap()
    wv_d = nc.dram_tensor("wvT", [D, D], F32, kind="ExternalInput").ap()
    wo_d = nc.dram_tensor("woT", [D, D], F32, kind="ExternalInput").ap()
    wf_d = nc.dram_tensor("wfT", [D, D], F32, kind="ExternalInput").ap()
    tabC_d = nc.dram_tensor("tabC", [128, NCHUNK * 32], F32, kind="ExternalInput").ap()
    tabS_d = nc.dram_tensor("tabS", [128, NCHUNK * 32], F32, kind="ExternalInput").ap()
    mask_d = nc.dram_tensor("mask", [128, 128], BF16, kind="ExternalInput").ap()
    ident_d = nc.dram_tensor("ident", [128, 128], BF16, kind="ExternalInput").ap()
    out_d = nc.dram_tensor("out", [T_SEQ, D], F32, kind="ExternalOutput").ap()

    s = Sched()

    with ExitStack() as es:
        ec = es.enter_context

        wq = ec(nc.sbuf_tensor("wq", [128, 8 * D], BF16))
        wk = ec(nc.sbuf_tensor("wk", [128, 8 * D], BF16))
        wv = ec(nc.sbuf_tensor("wv", [128, 8 * D], BF16))
        wo = ec(nc.sbuf_tensor("wo", [128, 8 * D], BF16))
        wf = ec(nc.sbuf_tensor("wf", [128, 8 * D], BF16))
        tabC = ec(nc.sbuf_tensor("tabC_s", [128, NCHUNK * 32], BF16))
        tabS = ec(nc.sbuf_tensor("tabS_s", [128, NCHUNK * 32], BF16))
        maskt = ec(nc.sbuf_tensor("mask_s", [128, 128], BF16))
        ident = ec(nc.sbuf_tensor("ident_s", [128, 128], BF16))
        xs = [ec(nc.sbuf_tensor(f"xs{i}", [128, D], F32)) for i in range(3)]
        junkV = [ec(nc.sbuf_tensor(f"junkV{i}", [128, D], BF16)) for i in range(3)]
        junkS = [ec(nc.sbuf_tensor(f"junkS{i}", [128, D], BF16)) for i in range(3)]
        ln1 = [ec(nc.sbuf_tensor(f"ln1_{i}", [128, D], BF16)) for i in range(2)]
        ln2 = ec(nc.sbuf_tensor("ln2", [128, D], BF16))
        ln1T = [ec(nc.sbuf_tensor(f"ln1T{i}", [128, D], BF16)) for i in range(2)]
        ln2T = ec(nc.sbuf_tensor("ln2T", [128, D], BF16))
        qev = [ec(nc.sbuf_tensor(f"qev{i}", [128, D], BF16)) for i in range(2)]
        kev = [ec(nc.sbuf_tensor(f"kev{i}", [128, D], BF16)) for i in range(2)]
        qr = ec(nc.sbuf_tensor("qr", [128, D], BF16))
        kr = ec(nc.sbuf_tensor("kr", [128, D], BF16))
        qm = ec(nc.sbuf_tensor("qm", [128, D], BF16))
        km = ec(nc.sbuf_tensor("km", [128, D], BF16))
        qE = ec(nc.sbuf_tensor("qE", [128, D], BF16))
        kE = ec(nc.sbuf_tensor("kE", [128, D], BF16))
        qphi = [ec(nc.sbuf_tensor(f"qphi{i}", [128, D], BF16)) for i in range(2)]
        kphi = [ec(nc.sbuf_tensor(f"kphi{i}", [128, D], BF16)) for i in range(2)]
        rA = ec(nc.sbuf_tensor("rA", [128, 512], BF16))
        rB = ec(nc.sbuf_tensor("rB", [128, 512], BF16))
        rC = ec(nc.sbuf_tensor("rC", [128, 512], BF16))
        rD = ec(nc.sbuf_tensor("rD", [128, 512], BF16))
        qT = [ec(nc.sbuf_tensor(f"qT{i}", [96, H * 128], BF16)) for i in range(2)]
        kT = [ec(nc.sbuf_tensor(f"kT{i}", [96, H * 128], BF16)) for i in range(2)]
        vaug = [ec(nc.sbuf_tensor(f"vaug{i}", [128, H * 65], BF16)) for i in range(2)]
        Asb = ec(nc.sbuf_tensor("Asb", [128, H * 128], BF16))
        KVs = [ec(nc.sbuf_tensor(f"KVs{i}", [96, H * 65], BF16)) for i in range(2)]
        Ysb = ec(nc.sbuf_tensor("Ysb", [128, D], BF16))
        YT = ec(nc.sbuf_tensor("YT", [128, D], BF16))
        x2 = [ec(nc.sbuf_tensor(f"x2_{i}", [128, D], F32)) for i in range(2)]
        outb = [ec(nc.sbuf_tensor(f"outb{i}", [128, D], F32)) for i in range(2)]
        r16a = ec(nc.sbuf_tensor("r16a", [128, 16], F32))
        epsc = ec(nc.sbuf_tensor("epsc", [128, 1], F32))
        fenceV = ec(nc.sbuf_tensor("fenceV", [128, 1], F32))
        fenceS = ec(nc.sbuf_tensor("fenceS", [128, 1], F32))
        r16b = ec(nc.sbuf_tensor("r16b", [128, 16], F32))
        stats = []
        for t in range(3):
            st = {}
            for nm in ("s1", "s2", "nm", "msq", "v1", "v2", "std", "rstd", "nmr"):
                st[nm] = ec(nc.sbuf_tensor(f"st{t}_{nm}", [128, 1], F32))
            stats.append(st)

        kv = [ec(nc.psum_tensor("kv0", [64, 7 * 65], F32)),
              ec(nc.psum_tensor("kv1", [64, 7 * 65], F32)),
              ec(nc.psum_tensor("kv2", [64, 2 * 65], F32))]
        att = ec(nc.psum_tensor("att", [128, 512], F32))
        ptr = ec(nc.psum_tensor("ptr", [128, 1024], BF16))
        pjA = ec(nc.psum_tensor("pjA", [128, 512], F32))
        pjB = ec(nc.psum_tensor("pjB", [128, 512], F32))
        pjC = ec(nc.psum_tensor("pjC", [128, 512], F32))
        bank_pool = [(att, "att"), (pjA, "pjA"), (pjB, "pjB"), (pjC, "pjC")]
        bank_ctr = [0]

        def next_bank():
            b = bank_pool[bank_ctr[0] % 4]
            bank_ctr[0] += 1
            return b

        sems = {e: ec(nc.semaphore(f"sem_{e}")) for e in ("tensor", "vector", "scalar", "gpsimd")}
        dsems = {(e, j): ec(nc.semaphore(f"dsem_{e}{j}"))
                 for e in ("sync", "gpsimd") for j in range(3)}

        block = ec(nc.Block())

        for wsb, wdram, name in ((wq, wq_d, "wq"), (wk, wk_d, "wk"),
                                 (wv, wv_d, "wv"), (wo, wo_d, "wo"),
                                 (wf, wf_d, "wf")):
            s.add("gpsimd",
                  lambda e, a=wsb, b=wdram: e.dma_start(
                      out=a[:].rearrange("p (a o) -> p a o", a=8),
                      in_=b.rearrange("(a p) o -> p a o", p=128)),
                  writes=[name], dma=True, tag=f"load_{name}", cost=7000)
        s.add("scalar", lambda e: e.sign(wf[:], wf[:]),
              reads=["wf"], writes=["wf"], tag="sign_wf")
        s.add("sync", lambda e: e.dma_start(out=maskt[:], in_=mask_d),
              writes=["mask"], dma=True)
        s.add("sync", lambda e: e.dma_start(out=ident[:], in_=ident_d),
              writes=["ident"], dma=True)
        s.add("gpsimd", lambda e: e.dma_start(out=tabC[:], in_=tabC_d),
              writes=["tabC"], dma=True)
        s.add("gpsimd", lambda e: e.dma_start(out=tabS[:], in_=tabS_d),
              writes=["tabS"], dma=True)
        for i in range(2):
            s.add("vector", lambda e, i=i: e.memset(
                vaug[i][:].rearrange("p (h e) -> p h e", e=65)[:, :, 64], 1.0),
                writes=[f"vaug{i}"], tag="ones_vaug")
        s.add("vector", lambda e: e.memset(KVs[0][:], 0.0), writes=["KVs0"])
        s.add("vector", lambda e: e.memset(KVs[1][64:96, :], 0.0), writes=["KVs1"])
        for i in range(2):
            s.add("vector", lambda e, i=i: e.memset(qT[i][64:96, :], 0.0),
                  writes=[f"qT{i}A", f"qT{i}B"], tag="zpad_qT")
            s.add("vector", lambda e, i=i: e.memset(kT[i][64:96, :], 0.0),
                  writes=[f"kT{i}A", f"kT{i}B"], tag="zpad_kT")
        s.add("vector", lambda e: e.memset(epsc[:], EPS_LN), writes=["epsc"])
        s.add("vector", lambda e: e.memset(junkS[0][:], 0.0), writes=["junkS0"])
        for bi in range(3):
            ncols = 130 if bi == 2 else 455
            s.add("tensor",
                  lambda e, bi=bi, ncols=ncols: e.matmul(
                      kv[bi][0:64, 0:ncols], junkS[0][:, 0:64], junkS[0][:, 0:ncols],
                      start=True, stop=False, skip_group_check=True),
                  reads=["junkS0"], writes=[f"kv{bi}"], tag="kv_zero", cost=250)
            for _ in range(4):
                s.add("tensor", lambda e: e.nop(), tag="nop", cost=5)
        for j in range(min(2, NCHUNK)):
            s.add("sync", lambda e, j=j: e.dma_start(
                out=xs[j][:], in_=x_d[j * 128:(j + 1) * 128, :]),
                writes=[f"x{j}"], dma=True, tag=f"load_x{j}", cost=3500)

        tapset = set(DEBUG_TAPS)

        def tap(name, c, res, ap):
            if (name, c) not in tapset:
                return
            rl = list(res) if isinstance(res, (list, tuple)) else [res]
            shp = list(ap.shape)
            d = nc.dram_tensor(f"tap_{name}_{c}", shp, F32, kind="ExternalOutput").ap()
            s.add("gpsimd", lambda e, d=d, ap=ap: e.dma_start(out=d, in_=ap),
                  reads=rl, dma=True, tag=f"tap_{name}_{c}")

        def emit_ln_sums(c, xres, x_sb, sti):
            st = stats[sti]
            pre = f"st{sti}_"

            def R(nm):
                return pre + nm
            s.add("vector", lambda e: e.tensor_scalar(
                out=junkV[sti][:], in0=x_sb[:], scalar1=-1.0 / D, scalar2=0.0,
                op0=ALU.mult, op1=ALU.add, accum_out=st["s1"][:]),
                reads=[xres], writes=[f"junkV{sti}", R("s1")], tag=f"ln{sti}_sum")
            s.add("vector", lambda e: e.tensor_scalar(
                out=fenceV[:], in0=st["s1"][:], scalar1=1.0, scalar2=None,
                op0=ALU.mult),
                reads=[R("s1")], writes=[R("s1")], tag=f"ln{sti}_sumf", cost=80)
            s.add("scalar", lambda e: e.activation(
                junkS[sti][:], x_sb[:], AF.Square, accum_out=st["s2"][:]),
                reads=[xres], writes=[f"junkS{sti}", R("s2")], tag=f"ln{sti}_sq")
            s.add("scalar", lambda e: e.copy(fenceS[:], st["s2"][:]),
                reads=[R("s2")], writes=[R("s2")], tag=f"ln{sti}_sqf", cost=200)

        def emit_ln_rest(c, xres, x_sb, sti, lnres, ln_sb):
            st = stats[sti]
            pre = f"st{sti}_"

            def R(nm):
                return pre + nm
            s.add("scalar", lambda e: e.activation(st["msq"][:], st["s1"][:],
                                                   AF.Square),
                  reads=[R("s1")], writes=[R("msq")], cost=300)
            s.add("scalar", lambda e: e.activation(st["v2"][:], st["msq"][:],
                                                   AF.Copy, scale=-1.0,
                                                   bias=float(EPS_LN)),
                  reads=[R("msq")], writes=[R("v2")], cost=300)
            s.add("scalar", lambda e: e.activation(st["std"][:], st["s2"][:], AF.Ln,
                                                   scale=1.0 / D, bias=st["v2"][:]),
                  reads=[R("s2"), R("v2")], writes=[R("std")], cost=320)
            s.add("scalar", lambda e: e.activation(st["rstd"][:], st["std"][:], AF.Exp, scale=-0.5),
                  reads=[R("std")], writes=[R("rstd")], cost=320)
            s.add("scalar", lambda e: e.activation(st["nmr"][:], st["rstd"][:],
                                                   AF.Copy, scale=st["s1"][:]),
                  reads=[R("s1"), R("rstd")], writes=[R("nmr")], cost=300)
            s.add("vector", lambda e: e.tensor_scalar(
                out=ln_sb[:], in0=x_sb[:], scalar1=st["rstd"][:], scalar2=st["nmr"][:],
                op0=ALU.mult, op1=ALU.add),
                reads=[xres, R("rstd"), R("nmr")], writes=[lnres], tag=f"ln{sti}_apply")

        def emit_transposes(src_res, src_sb, dst_res, dst_sb, tag):
            for i in range(8):
                s.add("tensor", lambda e, i=i: e.transpose(
                    ptr[:, i * 128:(i + 1) * 128], src_sb[:, i * 128:(i + 1) * 128],
                    ident[:]),
                    reads=[src_res, "ident"],
                    writes=["ptr"], tag=f"tr_{tag}{i}")
            for hb in range(2):
                s.add("scalar", lambda e, hb=hb: e.copy(
                    dst_sb[:, hb * 512:(hb + 1) * 512],
                    ptr[:, hb * 512:(hb + 1) * 512]),
                    reads=["ptr"],
                    writes=[dst_res + ("A" if hb == 0 else "B")],
                    tag=f"ev_{tag}{hb}", cost=650)

        def emit_transposes64(src_res, src_sb, dst_res, dst_sb, tag, eng="scalar"):
            for i in range(8):
                s.add("tensor", lambda e, i=i: e.transpose(
                    ptr[:, i * 128:(i + 1) * 128], src_sb[:, i * 128:(i + 1) * 128],
                    ident[:]),
                    reads=[src_res, "ident"],
                    writes=["ptr"], tag=f"tr_{tag}{i}")
            pv = ptr[:].rearrange("p (a t) -> p a t", a=8)
            dv = dst_sb[0:64, :].rearrange("p (a hp t) -> p a hp t", a=8, hp=2)
            for hb in range(2):
                a0, a1 = 4 * hb, 4 * hb + 4
                rres = "ptr"
                wres = dst_res + ("A" if hb == 0 else "B")
                if eng == "scalar":
                    s.add("scalar", lambda e, a0=a0, a1=a1: e.copy(
                        dv[:, a0:a1, 0, :], pv[0:64, a0:a1, :]),
                        reads=[rres], writes=[wres], tag=f"ev_{tag}e{hb}", cost=360)
                    s.add("scalar", lambda e, a0=a0, a1=a1: e.copy(
                        dv[:, a0:a1, 1, :], pv[64:128, a0:a1, :]),
                        reads=[rres], writes=[wres], tag=f"ev_{tag}o{hb}", cost=360)
                else:
                    s.add("vector", lambda e, a0=a0, a1=a1: e.tensor_copy(
                        dv[:, a0:a1, 0, :], pv[0:64, a0:a1, :]),
                        reads=[rres], writes=[wres], tag=f"ev_{tag}e{hb}", cost=360)
                    s.add("vector", lambda e, a0=a0, a1=a1: e.tensor_copy(
                        dv[:, a0:a1, 1, :], pv[64:128, a0:a1, :]),
                        reads=[rres], writes=[wres], tag=f"ev_{tag}o{hb}", cost=360)

        def emit_proj(wres, w_sb, lres, lsb, tag, evac_half):
            lresA, lresB = lres
            banks = (next_bank(), next_bank())
            for o in range(8):
                pj, pjres = banks[0] if o < 4 else banks[1]
                off = (o % 4) * 128
                for i in range(8):
                    s.add("tensor", lambda e, o=o, i=i, pj=pj, off=off, lsb=lsb, w_sb=w_sb: e.matmul(
                        pj[:, off:off + 128],
                        lsb[:, i * 128:(i + 1) * 128],
                        w_sb[:, i * D + o * 128: i * D + o * 128 + 128],
                        start=(i == 0), stop=(i == 7)),
                        reads=[lresA if i < 4 else lresB, wres],
                        writes=[pjres], tag=f"{tag}_o{o}i{i}")
                if o == 3:
                    evac_half(0, banks[0])
            evac_half(1, banks[1])

        def sb_evac(dst_res, dst_sb, tag):
            def evac_half(half, bank):
                pj, pjres = bank
                s.add("scalar", lambda e, half=half, pj=pj, dst_sb=dst_sb: e.copy(
                    dst_sb[:, half * 512:(half + 1) * 512], pj[:]),
                    reads=[pjres], writes=[dst_res], tag=f"{tag}_h{half}")
            return evac_half

        def emit_rope_phi(c, src_res, src_sb, rdst_res, rdst, mdst_res, mdst,
                          edst_res, edst, pdst_res, pdst, tag):
            sv = src_sb[:].rearrange("p (h x j) -> p x h j", x=2, j=32)
            LO, HI = sv[:, 0], sv[:, 1]
            Cb = tabC[:, c * 32:(c + 1) * 32].unsqueeze(1).broadcast_to([128, H, 32])
            Sb = tabS[:, c * 32:(c + 1) * 32].unsqueeze(1).broadcast_to([128, H, 32])
            rAv = rA[:].rearrange("p (h j) -> p h j", j=32)
            rBv = rB[:].rearrange("p (h j) -> p h j", j=32)
            rCv = rC[:].rearrange("p (h j) -> p h j", j=32)
            rDv = rD[:].rearrange("p (h j) -> p h j", j=32)
            dv = rdst[:].rearrange("p (h x j) -> p x h j", x=2, j=32)
            s.add("gpsimd", lambda e: e.tensor_tensor(out=rAv, in0=HI, in1=Sb, op=ALU.mult),
                  reads=[src_res, "tabS"], writes=["rA"], tag=f"{tag}_rA")
            s.add("gpsimd", lambda e: e.tensor_tensor(out=rBv, in0=LO, in1=Sb, op=ALU.mult),
                  reads=[src_res, "tabS"], writes=["rB"], tag=f"{tag}_rB")
            s.add("vector", lambda e: e.tensor_tensor(out=rCv, in0=LO, in1=Cb, op=ALU.mult),
                  reads=[src_res, "tabC"], writes=["rC"], tag=f"{tag}_rC")
            s.add("vector", lambda e: e.tensor_tensor(out=rDv, in0=HI, in1=Cb, op=ALU.mult),
                  reads=[src_res, "tabC"], writes=["rD"], tag=f"{tag}_rD")
            s.add("vector", lambda e: e.tensor_tensor(out=dv[:, 0], in0=rCv, in1=rAv, op=ALU.subtract),
                  reads=["rC", "rA"], writes=[rdst_res], tag=f"{tag}_lo")
            s.add("vector", lambda e: e.tensor_tensor(out=dv[:, 1], in0=rBv, in1=rDv, op=ALU.add),
                  reads=["rB", "rD"], writes=[rdst_res], tag=f"{tag}_hi")
            s.add("vector", lambda e: e.tensor_scalar(
                out=mdst[:], in0=rdst[:], scalar1=0.0, scalar2=None, op0=ALU.min),
                reads=[rdst_res], writes=[mdst_res], tag=f"{tag}_min")
            s.add("scalar", lambda e: e.activation(edst[:], mdst[:], AF.Exp),
                  reads=[mdst_res], writes=[edst_res], tag=f"{tag}_exp")
            s.add("vector", lambda e: e.scalar_tensor_tensor(
                out=pdst[:], in0=rdst[:], scalar=0.0, in1=edst[:],
                op0=ALU.max, op1=ALU.add),
                reads=[rdst_res, edst_res], writes=[pdst_res], tag=f"{tag}_phi")

        def kv_slot(h):
            b = min(h // 7, 2)
            return b, (h - 7 * b) * 65

        def front_ln_sums(c):
            emit_ln_sums(c, f"x{c % 3}", xs[c % 3], c % 2)

        def front_ln_rest(c):
            i2 = c % 2
            emit_ln_rest(c, f"x{c % 3}", xs[c % 3], i2, f"ln1_{i2}", ln1[i2])
            tap("ln1", c, f"ln1_{i2}", ln1[i2][:])
            emit_transposes(f"ln1_{i2}", ln1[i2], f"ln1T{i2}", ln1T[i2], f"ln1T{c}")
            tap("ln1T", c, (f"ln1T{i2}A", f"ln1T{i2}B"), ln1T[i2][:])

        def front_q(c):
            i2 = c % 2
            emit_proj("wq", wq, (f"ln1T{i2}A", f"ln1T{i2}B"), ln1T[i2], f"q{c}",
                      sb_evac(f"qev{i2}", qev[i2], f"qev{c}"))
            tap("qev", c, f"qev{i2}", qev[i2][:])

        def front_k(c):
            i2 = c % 2
            emit_proj("wk", wk, (f"ln1T{i2}A", f"ln1T{i2}B"), ln1T[i2], f"k{c}",
                      sb_evac(f"kev{i2}", kev[i2], f"kev{c}"))
            tap("kev", c, f"kev{i2}", kev[i2][:])

        def front_v(c):
            v_sb = vaug[c % 2]
            vres = f"vaug{c % 2}"
            vv = v_sb[:].rearrange("p (h e) -> p h e", e=65)

            def evac_half(half, bank):
                pj, pjres = bank
                s.add("scalar", lambda e, half=half, pj=pj, vv=vv: e.copy(
                    vv[:, half * 8:(half + 1) * 8, 0:64],
                    pj[:].rearrange("p (h e) -> p h e", e=64)),
                    reads=[pjres], writes=[vres], tag=f"vev{c}_h{half}")
            emit_proj("wv", wv, (f"ln1T{c % 2}A", f"ln1T{c % 2}B"), ln1T[c % 2],
                      f"v{c}", evac_half)
            tap("vaug", c, vres, v_sb[:])

        def front_rope_q(c):
            i2 = c % 2
            emit_rope_phi(c, f"qev{i2}", qev[i2], "qr", qr, "qm", qm, "qE", qE,
                          f"qphi{i2}", qphi[i2], f"rq{c}")
            tap("qphi", c, f"qphi{i2}", qphi[i2][:])

        def front_rope_k(c):
            i2 = c % 2
            emit_rope_phi(c, f"kev{i2}", kev[i2], "kr", kr, "km", km, "kE", kE,
                          f"kphi{i2}", kphi[i2], f"rk{c}")
            tap("kphi", c, f"kphi{i2}", kphi[i2][:])

        def front_qkT(c):
            i2 = c % 2
            emit_transposes64(f"qphi{i2}", qphi[i2], f"qT{i2}", qT[i2], f"qT{c}")
            emit_transposes64(f"kphi{i2}", kphi[i2], f"kT{i2}", kT[i2], f"kT{c}", eng="vector")
            tap("qT", c, (f"qT{i2}A", f"qT{i2}B"), qT[i2][0:64, :])
            tap("kT", c, (f"kT{i2}A", f"kT{i2}B"), kT[i2][0:64, :])

        def back_attn(c):
            i2 = c % 2
            qTs, kTs = qT[i2], kT[i2]
            qTh = lambda h: f"qT{i2}A" if h < 8 else f"qT{i2}B"
            kTh = lambda h: f"kT{i2}A" if h < 8 else f"kT{i2}B"
            v_sb, vres = vaug[i2], f"vaug{i2}"
            kph, kphres = kphi[i2], f"kphi{i2}"
            kvres_r = f"KVs{i2}"
            kvres_w = f"KVs{(c + 1) % 2}"
            mb = maskt[:].unsqueeze(1).broadcast_to([128, 4, 128])
            for g in range(4):
                ab, abres = next_bank()
                for hh in range(4):
                    h = g * 4 + hh
                    s.add("tensor", lambda e, h=h, hh=hh, qTs=qTs, kTs=kTs, ab=ab: e.matmul(
                        ab[:, hh * 128:(hh + 1) * 128],
                        kTs[0:96, h * 128:(h + 1) * 128],
                        qTs[0:96, h * 128:(h + 1) * 128],
                        start=True, stop=True, skip_group_check=True),
                        reads=[qTh(h), kTh(h)], writes=[abres], tag=f"A{c}_h{h}")
                s.add("vector", lambda e, g=g, ab=ab: e.tensor_tensor(
                    out=Asb[:, g * 512:(g + 1) * 512].rearrange("p (h t) -> p h t", t=128),
                    in0=ab[:, 0:512].rearrange("p (h t) -> p h t", t=128),
                    in1=mb, op=ALU.mult),
                    reads=[abres, "mask"], writes=[f"A{g}"], tag=f"mask{c}_g{g}")
                for hh in range(4):
                    h = g * 4 + hh
                    b, off = kv_slot(h)
                    s.add("tensor", lambda e, h=h, b=b, off=off, v_sb=v_sb, kph=kph, c=c: e.matmul(
                        kv[b][0:64, off:off + 65],
                        kph[:, h * 64:(h + 1) * 64],
                        v_sb[:, h * 65:(h + 1) * 65],
                        start=False, stop=(c == NCHUNK - 1), skip_group_check=True),
                        reads=[kphres, vres], writes=[f"kv{b}"], tag=f"kv{c}_h{h}")
                tap(f"Asb{g}", c, f"A{g}", Asb[:, g * 512:(g + 1) * 512])
            if c < NCHUNK - 1:
                kvv = KVs[(c + 1) % 2]
                s.add("scalar", lambda e, kvv=kvv: e.copy(kvv[0:64, 0:455], kv[0][0:64, :]),
                      reads=["kv0"], writes=[kvres_w], tag=f"kvev{c}_0", cost=550)
                s.add("scalar", lambda e, kvv=kvv: e.copy(kvv[0:64, 455:910], kv[1][0:64, :]),
                      reads=["kv1"], writes=[kvres_w], tag=f"kvev{c}_1", cost=550)
                s.add("scalar", lambda e, kvv=kvv: e.copy(kvv[0:64, 910:1040], kv[2][0:64, :]),
                      reads=["kv2"], writes=[kvres_w], tag=f"kvev{c}_2", cost=550)
                tap("KVs", c, kvres_w, KVs[(c + 1) % 2][0:64, :])
            ytiles_after = {0: (0, 1, 2), 1: (3, 4, 5, 6), 2: (7,)}
            for p in range(3):
                nh = 2 if p == 2 else 7
                h0 = 7 * p
                yb, ybres = next_bank()
                for hh in range(nh):
                    h = h0 + hh
                    s.add("tensor", lambda e, h=h, hh=hh, v_sb=v_sb, yb=yb: e.matmul(
                        yb[:, hh * 65:hh * 65 + 65],
                        Asb[:, h * 128:(h + 1) * 128],
                        v_sb[:, h * 65:(h + 1) * 65],
                        start=True, stop=False, skip_group_check=True),
                        reads=[f"A{h // 4}", vres], writes=[ybres], tag=f"Yi{c}_h{h}")
                    s.add("tensor", lambda e, h=h, hh=hh, qTs=qTs, kvsb=KVs[i2], yb=yb: e.matmul(
                        yb[:, hh * 65:hh * 65 + 65],
                        qTs[0:96, h * 128:(h + 1) * 128],
                        kvsb[0:96, h * 65:(h + 1) * 65],
                        start=False, stop=True, skip_group_check=True),
                        reads=[qTh(h), kvres_r], writes=[ybres], tag=f"Yx{c}_h{h}")
                pyv = yb[:, 0:455].rearrange("p (h e) -> p h e", e=65)
                s.add("vector", lambda e, nh=nh, h0=h0, pyv=pyv: e.tensor_scalar(
                    out=r16a[:, h0:h0 + nh], in0=pyv[:, 0:nh, 64], scalar1=EPS_DEN,
                    scalar2=None, op0=ALU.max),
                    reads=[ybres], writes=[f"r16a{p}"], tag=f"den{c}_p{p}", cost=220)
                s.add("vector", lambda e, nh=nh, h0=h0: e.reciprocal(
                    r16b[:, h0:h0 + nh], r16a[:, h0:h0 + nh]),
                    reads=[f"r16a{p}"], writes=[f"r16b{p}"], cost=220)
                for dh, (a, b) in enumerate(((0, min(4, nh)), (min(4, nh), nh))):
                    if b <= a:
                        continue
                    s.add("vector", lambda e, a=a, b=b, h0=h0, pyv=pyv: e.tensor_tensor(
                        out=Ysb[:, (h0 + a) * 64:(h0 + b) * 64].rearrange("p (h e) -> p h e", e=64),
                        in0=pyv[:, a:b, 0:64],
                        in1=r16b[:, h0 + a:h0 + b].unsqueeze(2).broadcast_to([128, b - a, 64]),
                        op=ALU.mult),
                        reads=[ybres, f"r16b{p}"], writes=[f"Y{p}h{dh}"],
                        tag=f"ydiv{c}_p{p}h{dh}", cost=420)

                def yres(h):
                    pp = min(h // 7, 2)
                    return f"Y{pp}h{0 if h - 7 * pp < 4 else 1}"
                for i in ytiles_after[p]:
                    s.add("tensor", lambda e, i=i: e.transpose(
                        ptr[:, i * 128:(i + 1) * 128], Ysb[:, i * 128:(i + 1) * 128],
                        ident[:]),
                        reads=[yres(2 * i), yres(2 * i + 1), "ident"],
                        writes=["ptr"], tag=f"tr_YT{c}{i}")
                tap(f"Ysb{p}", c, (f"Y{p}h0", f"Y{p}h1" if nh > 4 else f"Y{p}h0"),
                    Ysb[:, 7 * p * 64: (7 * p + nh) * 64])
            for hb in range(2):
                s.add("scalar", lambda e, hb=hb: e.copy(
                    YT[:, hb * 512:(hb + 1) * 512], ptr[:, hb * 512:(hb + 1) * 512]),
                    reads=["ptr"],
                    writes=["YTA" if hb == 0 else "YTB"],
                    tag=f"ev_YT{c}{hb}", cost=650)

        def back_wo(c):
            x_sb, xres = xs[c % 3], f"x{c % 3}"
            x2sb, x2res = x2[c % 2], f"x2_{c % 2}"

            def evac_half(half, bank):
                pj, pjres = bank
                s.add("vector", lambda e, half=half, pj=pj, x2sb=x2sb, x_sb=x_sb: e.tensor_tensor(
                    out=x2sb[:, half * 512:(half + 1) * 512], in0=pj[:],
                    in1=x_sb[:, half * 512:(half + 1) * 512], op=ALU.add),
                    reads=[pjres, xres], writes=[x2res], tag=f"res1_{c}_h{half}")
            emit_proj("wo", wo, ("YTA", "YTB"), YT, f"wo{c}", evac_half)
            tap("x2", c, x2res, x2sb[:])
            emit_ln_sums(c, x2res, x2sb, 2)

        def back_ffn(c):
            x2sb, x2res = x2[c % 2], f"x2_{c % 2}"
            osb, outres = outb[c % 2], f"out{c % 2}"
            emit_ln_rest(c, x2res, x2sb, 2, "ln2", ln2)
            emit_transposes("ln2", ln2, "ln2T", ln2T, f"ln2T{c}")

            def evac_half(half, bank):
                pj, pjres = bank
                s.add("vector", lambda e, half=half, pj=pj, osb=osb, x2sb=x2sb: e.tensor_tensor(
                    out=osb[:, half * 512:(half + 1) * 512], in0=pj[:],
                    in1=x2sb[:, half * 512:(half + 1) * 512], op=ALU.add),
                    reads=[pjres, x2res], writes=[outres], tag=f"res2_{c}_h{half}")
            emit_proj("wf", wf, ("ln2TA", "ln2TB"), ln2T, f"ffn{c}", evac_half)
            s.add("sync", lambda e, c=c, osb=osb: e.dma_start(
                out=out_d[c * 128:(c + 1) * 128, :], in_=osb[:]),
                reads=[outres], dma=True, tag=f"store{c}", cost=3500)

        if NCHUNK > 2:
            s.add("sync", lambda e: e.dma_start(out=xs[2][:], in_=x_d[256:384, :]),
                  writes=["x2"], dma=True, tag="load_x2", cost=3500)
        front_ln_sums(0)
        front_ln_rest(0)
        front_q(0)
        front_rope_q(0)
        front_k(0)
        front_rope_k(0)
        front_v(0)
        front_qkT(0)
        for c in range(NCHUNK):
            n = c + 1
            if n < NCHUNK:
                front_ln_sums(n)
            back_attn(c)
            if n < NCHUNK:
                front_ln_rest(n)
                front_q(n)
                front_rope_q(n)
            back_wo(c)
            if n < NCHUNK:
                front_k(n)
                front_rope_k(n)
            back_ffn(c)
            if n < NCHUNK:
                front_v(n)
                front_qkT(n)
            if c + 3 < NCHUNK:
                s.add("sync", lambda e, c=c: e.dma_start(
                    out=xs[(c + 3) % 3][:], in_=x_d[(c + 3) * 128:(c + 4) * 128, :]),
                    writes=[f"x{(c + 3) % 3}"], dma=True, tag=f"load_x{c + 3}", cost=3500)

        order = s.schedule()
        waits, needs_inc, dma_no, inc_no = s.plan(order)
        by_engine = {e: [s.ops[gi] for gi in order if s.ops[gi]["engine"] == e]
                     for e in ENGINES}
        n_sync_dma = sum(1 for op in by_engine["sync"] if op["dma"])

        def emit_engine(name, eng):
            for op in by_engine[name]:
                i = op["i"]
                for (kind, tgt, val) in waits[i]:
                    if kind == "dma":
                        eng.wait_ge(dsems[tgt], 16 * val)
                    else:
                        eng.wait_ge(sems[tgt], inc_no[val])
                instr = op["fn"](eng)
                if op["dma"]:
                    k = dma_no[op["i"]]
                    instr.then_inc(dsems[(name, (k - 1) % 3)], 16)
                elif needs_inc[i] and instr is not None:
                    instr.then_inc(sems[name], 1)
            if name in ("sync", "gpsimd"):
                nq = sum(1 for op in by_engine[name] if op["dma"])
                for j in range(3):
                    m_tot = (nq - j + 2) // 3
                    if m_tot > 0:
                        eng.wait_ge(dsems[(name, j)], 16 * m_tot)

        @block.tensor
        def _(tensor):
            emit_engine("tensor", tensor)

        @block.vector
        def _(vector):
            emit_engine("vector", vector)

        @block.scalar
        def _(scalar):
            emit_engine("scalar", scalar)

        @block.gpsimd
        def _(gpsimd):
            emit_engine("gpsimd", gpsimd)

        @block.sync
        def _(sync):
            emit_engine("sync", sync)

    return nc


def _host_tables():
    inv = 10000.0 ** (-np.arange(32, dtype=np.float64) / 32.0)
    ang = np.arange(T_SEQ, dtype=np.float64)[:, None] * inv[None, :]
    C = np.cos(ang).reshape(NCHUNK, 128, 32).transpose(1, 0, 2).reshape(128, NCHUNK * 32)
    S = np.sin(ang).reshape(NCHUNK, 128, 32).transpose(1, 0, 2).reshape(128, NCHUNK * 32)
    mask = (np.arange(128)[:, None] <= np.arange(128)[None, :]).astype(np.float64)
    ident = np.eye(128)
    bf = ml_dtypes.bfloat16
    return C.astype(np.float32), S.astype(np.float32), mask.astype(bf), ident.astype(bf)


def kernel(L0_attn_x, L0_attn_Wq, L0_attn_Wk, L0_attn_Wv, L0_attn_Wo, L0_ffn_weight):
    x = np.asarray(L0_attn_x, dtype=np.float32)
    wqT = np.ascontiguousarray(np.asarray(L0_attn_Wq, dtype=np.float32).T)
    wkT = np.ascontiguousarray(np.asarray(L0_attn_Wk, dtype=np.float32).T)
    wvT = np.ascontiguousarray(np.asarray(L0_attn_Wv, dtype=np.float32).T)
    woT = np.ascontiguousarray(np.asarray(L0_attn_Wo, dtype=np.float32).T)
    wfT = np.ascontiguousarray(np.asarray(L0_ffn_weight, dtype=np.float32).T)
    tabC, tabS, mask, ident = _host_tables()

    nc = build_program()
    in_maps = []
    for core in range(N_CORES):
        in_maps.append({
            "x": np.ascontiguousarray(x[core]),
            "wqT": wqT, "wkT": wkT, "wvT": wvT, "woT": woT, "wfT": wfT,
            "tabC": tabC, "tabS": tabS, "mask": mask, "ident": ident,
        })
    res = run_bass_kernel_spmd(nc, in_maps, core_ids=list(range(N_CORES)))
    out = np.stack([np.asarray(r["out"]) for r in res.results], axis=0)
    return out


# version 47 (final)
# speedup vs baseline: 1.0303x; 1.0062x over previous
import os
import numpy as np
import ml_dtypes
import concourse.bass as bass
import concourse.mybir as mybir
from concourse.bass_utils import run_bass_kernel_spmd
from contextlib import ExitStack

F32 = mybir.dt.float32
BF16 = mybir.dt.bfloat16
AF = mybir.ActivationFunctionType
ALU = mybir.AluOpType

T_SEQ = 4096
D = 1024
H = 16
DH = 64
CH = 128
NCHUNK = T_SEQ // CH
N_CORES = 8
EPS_LN = 1e-5
EPS_DEN = 1e-6

ENGINES = ("tensor", "vector", "scalar", "gpsimd", "sync")

DEBUG_TAPS = ()


class Sched:

    def __init__(self):
        self.ops = []
        self.lastw = {}
        self.readers = {}
        self.eng_n = {e: 0 for e in ENGINES}

    DEF_COST = {"tensor": 70, "vector": 700, "scalar": 900, "gpsimd": 1200,
                "sync": 1500}

    def add(self, engine, fn, reads=(), writes=(), dma=False, tag="", cost=None):
        i = len(self.ops)
        deps = set()
        for r in reads:
            w = self.lastw.get(r)
            if w is not None:
                deps.add(w)
        for r in writes:
            w = self.lastw.get(r)
            if w is not None:
                deps.add(w)
            deps.update(self.readers.get(r, ()))
        deps.discard(i)
        self.ops.append(dict(i=i, engine=engine, fn=fn, deps=sorted(deps),
                             dma=dma, tag=tag,
                             cost=float(cost if cost is not None
                                        else self.DEF_COST[engine])))
        for r in reads:
            self.readers.setdefault(r, []).append(i)
        for r in writes:
            self.lastw[r] = i
            self.readers[r] = []
        return i

    def schedule(self):
        ops = self.ops
        n = len(ops)
        succ = [[] for _ in range(n)]
        pend = [0] * n
        for op in ops:
            for d in op["deps"]:
                succ[d].append(op["i"])
                pend[op["i"]] += 1
        LAT = float(os.environ.get("SIMLAT", 500.0))
        prio = [0.0] * n
        for i in range(n - 1, -1, -1):
            m = 0.0
            for sj in succ[i]:
                if prio[sj] > m:
                    m = prio[sj]
            prio[i] = ops[i]["cost"] + (LAT if succ[i] else 0.0) + m
        eng_free = {e: 0.0 for e in ENGINES}
        avail = [0.0] * n
        done = [0.0] * n
        ready = {e: [] for e in ENGINES}
        for i in range(n):
            if pend[i] == 0:
                ready[ops[i]["engine"]].append(i)
        order = []
        remaining = n
        while remaining:
            best = None
            for e in ENGINES:
                lst = ready[e]
                if not lst:
                    continue
                ef = eng_free[e]
                for i in lst:
                    st = ef if ef > avail[i] else avail[i]
                    key = (st, -prio[i], i)
                    if best is None or key < best[0]:
                        best = (key, e, i)
            key, e, i = best
            st = key[0]
            ready[e].remove(i)
            done[i] = st + ops[i]["cost"]
            eng_free[e] = done[i]
            order.append(i)
            remaining -= 1
            for sj in succ[i]:
                pend[sj] -= 1
                t = done[i] + (LAT if ops[sj]["engine"] != e else 0.0)
                if avail[sj] < t:
                    avail[sj] = t
                if pend[sj] == 0:
                    ready[ops[sj]["engine"]].append(sj)
        cnt = {e: 0 for e in ENGINES}
        for i in order:
            ops[i]["eidx"] = cnt[ops[i]["engine"]]
            cnt[ops[i]["engine"]] += 1
        self.sim_makespan = max(done) if n else 0.0
        if os.environ.get('SIM_PRINT'):
            print('SIM makespan:', round(self.sim_makespan / 1000, 1), 'us')
            start_t = {i: done[i] - ops[i]["cost"] for i in range(n)}
            eng_prev = {}
            last = {e: None for e in ENGINES}
            for i in order:
                e2 = ops[i]["engine"]
                eng_prev[i] = last[e2]
                last[e2] = i
            cur = max(range(n), key=lambda i: done[i])
            from collections import Counter
            hist = Counter(); hist_t = Counter()
            while cur is not None:
                st = start_t[cur]
                nxt = None
                for d in ops[cur]["deps"]:
                    t = done[d] + (LAT if ops[d]["engine"] != ops[cur]["engine"] else 0)
                    if abs(t - st) < 1e-6:
                        nxt = d
                        break
                if nxt is None:
                    ep = eng_prev[cur]
                    if ep is not None and abs(done[ep] - st) < 1e-6:
                        nxt = ep
                import re as _re
                key = _re.sub(r"\d+", "#", ops[cur]["tag"]) or ops[cur]["engine"]
                hist[key] += 1; hist_t[key] += ops[cur]["cost"]
                cur = nxt
            print("SIM critical-path composition (count, total_us):")
            for k, v in hist.most_common(16):
                print(f"  {k:24s} n={v:5d}  t={hist_t[k]/1000:.1f}us")
        return order

    def plan(self, order):
        ops = self.ops
        n = len(ops)
        waits = [[] for _ in range(n)]
        needs_inc = [False] * n
        dma_no = {}
        dma_count = {e: 0 for e in ENGINES}
        clocks = [None] * n
        last_on = {e: None for e in ENGINES}

        def fresh_clock():
            c = {e: -1 for e in ENGINES}
            for e in ENGINES:
                for j in range(3):
                    c[f"{e}_dma{j}"] = 0
            return c

        for gi in order:
            op = ops[gi]
            i = op["i"]
            eng = op["engine"]
            prev = last_on[eng]
            clock = dict(clocks[prev]) if prev is not None else fresh_clock()
            if op["dma"]:
                dma_count[eng] += 1
                k = dma_count[eng]
                dma_no[i] = k
                j = (k - 1) % 3
                m_prev = (k - 1) // 3
                if m_prev > 0 and clock[f"{eng}_dma{j}"] < m_prev:
                    waits[i].append(("dma", (eng, j), m_prev))
                    clock[f"{eng}_dma{j}"] = m_prev
            for d in op["deps"]:
                dop = ops[d]
                de = dop["engine"]
                if dop["dma"]:
                    k = dma_no[d]
                    j = (k - 1) % 3
                    m = (k - 1) // 3 + 1
                    if clock[f"{de}_dma{j}"] >= m:
                        continue
                    waits[i].append(("dma", (de, j), m))
                    clock[f"{de}_dma{j}"] = m
                    dc = clocks[d]
                    for key, v in dc.items():
                        if clock[key] < v:
                            clock[key] = v
                else:
                    if de == eng and de in ("tensor", "sync"):
                        continue
                    if clock[de] >= dop["eidx"]:
                        continue
                    waits[i].append(("eng", de, d))
                    needs_inc[d] = True
                    clock[de] = dop["eidx"]
                    dc = clocks[d]
                    for key, v in dc.items():
                        if clock[key] < v:
                            clock[key] = v
            clocks[i] = clock
            last_on[eng] = i
        inc_no = {}
        cnt = {e: 0 for e in ENGINES}
        for gi in order:
            op = ops[gi]
            if needs_inc[op["i"]] and not op["dma"]:
                cnt[op["engine"]] += 1
                inc_no[op["i"]] = cnt[op["engine"]]
        return waits, needs_inc, dma_no, inc_no


def build_program():
    nc = bass.Bass("TRN2", target_bir_lowering=False)

    x_d = nc.dram_tensor("x", [T_SEQ, D], F32, kind="ExternalInput").ap()
    wq_d = nc.dram_tensor("wqT", [D, D], F32, kind="ExternalInput").ap()
    wk_d = nc.dram_tensor("wkT", [D, D], F32, kind="ExternalInput").ap()
    wv_d = nc.dram_tensor("wvT", [D, D], F32, kind="ExternalInput").ap()
    wo_d = nc.dram_tensor("woT", [D, D], F32, kind="ExternalInput").ap()
    wf_d = nc.dram_tensor("wfT", [D, D], F32, kind="ExternalInput").ap()
    tabC_d = nc.dram_tensor("tabC", [128, NCHUNK * 32], F32, kind="ExternalInput").ap()
    tabS_d = nc.dram_tensor("tabS", [128, NCHUNK * 32], F32, kind="ExternalInput").ap()
    mask_d = nc.dram_tensor("mask", [128, 128], BF16, kind="ExternalInput").ap()
    ident_d = nc.dram_tensor("ident", [128, 128], BF16, kind="ExternalInput").ap()
    out_d = nc.dram_tensor("out", [T_SEQ, D], F32, kind="ExternalOutput").ap()

    s = Sched()

    with ExitStack() as es:
        ec = es.enter_context

        wq = ec(nc.sbuf_tensor("wq", [128, 8 * D], BF16))
        wk = ec(nc.sbuf_tensor("wk", [128, 8 * D], BF16))
        wv = ec(nc.sbuf_tensor("wv", [128, 8 * D], BF16))
        wo = ec(nc.sbuf_tensor("wo", [128, 8 * D], BF16))
        wf = ec(nc.sbuf_tensor("wf", [128, 8 * D], BF16))
        tabC = ec(nc.sbuf_tensor("tabC_s", [128, NCHUNK * 32], BF16))
        tabS = ec(nc.sbuf_tensor("tabS_s", [128, NCHUNK * 32], BF16))
        maskt = ec(nc.sbuf_tensor("mask_s", [128, 128], BF16))
        ident = ec(nc.sbuf_tensor("ident_s", [128, 128], BF16))
        xs = [ec(nc.sbuf_tensor(f"xs{i}", [128, D], F32)) for i in range(3)]
        junkV = [ec(nc.sbuf_tensor(f"junkV{i}", [128, D], BF16)) for i in range(3)]
        junkS = [ec(nc.sbuf_tensor(f"junkS{i}", [128, D], BF16)) for i in range(3)]
        ln1 = [ec(nc.sbuf_tensor(f"ln1_{i}", [128, D], BF16)) for i in range(2)]
        ln2 = ec(nc.sbuf_tensor("ln2", [128, D], BF16))
        ln1T = [ec(nc.sbuf_tensor(f"ln1T{i}", [128, D], BF16)) for i in range(2)]
        ln2T = ec(nc.sbuf_tensor("ln2T", [128, D], BF16))
        qev = [ec(nc.sbuf_tensor(f"qev{i}", [128, D], BF16)) for i in range(2)]
        kev = [ec(nc.sbuf_tensor(f"kev{i}", [128, D], BF16)) for i in range(2)]
        qr = ec(nc.sbuf_tensor("qr", [128, D], BF16))
        kr = ec(nc.sbuf_tensor("kr", [128, D], BF16))
        qm = ec(nc.sbuf_tensor("qm", [128, D], BF16))
        km = ec(nc.sbuf_tensor("km", [128, D], BF16))
        qE = ec(nc.sbuf_tensor("qE", [128, D], BF16))
        kE = ec(nc.sbuf_tensor("kE", [128, D], BF16))
        qphi = [ec(nc.sbuf_tensor(f"qphi{i}", [128, D], BF16)) for i in range(2)]
        kphi = [ec(nc.sbuf_tensor(f"kphi{i}", [128, D], BF16)) for i in range(2)]
        rA = ec(nc.sbuf_tensor("rA", [128, 512], BF16))
        rB = ec(nc.sbuf_tensor("rB", [128, 512], BF16))
        rC = ec(nc.sbuf_tensor("rC", [128, 512], BF16))
        rD = ec(nc.sbuf_tensor("rD", [128, 512], BF16))
        qT = [ec(nc.sbuf_tensor(f"qT{i}", [96, H * 128], BF16)) for i in range(2)]
        kT = [ec(nc.sbuf_tensor(f"kT{i}", [96, H * 128], BF16)) for i in range(2)]
        vaug = [ec(nc.sbuf_tensor(f"vaug{i}", [128, H * 65], BF16)) for i in range(2)]
        Asb = ec(nc.sbuf_tensor("Asb", [128, H * 128], BF16))
        KVs = [ec(nc.sbuf_tensor(f"KVs{i}", [96, H * 65], BF16)) for i in range(2)]
        Ysb = ec(nc.sbuf_tensor("Ysb", [128, D], BF16))
        YT = ec(nc.sbuf_tensor("YT", [128, D], BF16))
        x2 = [ec(nc.sbuf_tensor(f"x2_{i}", [128, D], F32)) for i in range(2)]
        outb = [ec(nc.sbuf_tensor(f"outb{i}", [128, D], F32)) for i in range(2)]
        r16a = ec(nc.sbuf_tensor("r16a", [128, 16], F32))
        epsc = ec(nc.sbuf_tensor("epsc", [128, 1], F32))
        fenceV = ec(nc.sbuf_tensor("fenceV", [128, 1], F32))
        fenceS = ec(nc.sbuf_tensor("fenceS", [128, 1], F32))
        r16b = ec(nc.sbuf_tensor("r16b", [128, 16], F32))
        stats = []
        for t in range(3):
            st = {}
            for nm in ("s1", "s2", "nm", "msq", "v1", "v2", "std", "rstd", "nmr"):
                st[nm] = ec(nc.sbuf_tensor(f"st{t}_{nm}", [128, 1], F32))
            stats.append(st)

        kv = [ec(nc.psum_tensor("kv0", [64, 7 * 65], F32)),
              ec(nc.psum_tensor("kv1", [64, 7 * 65], F32)),
              ec(nc.psum_tensor("kv2", [64, 2 * 65], F32))]
        att = ec(nc.psum_tensor("att", [128, 512], F32))
        ptr = ec(nc.psum_tensor("ptr", [128, 1024], BF16))
        pjA = ec(nc.psum_tensor("pjA", [128, 512], F32))
        pjB = ec(nc.psum_tensor("pjB", [128, 512], F32))
        pjC = ec(nc.psum_tensor("pjC", [128, 512], F32))
        bank_pool = [(att, "att"), (pjA, "pjA"), (pjB, "pjB"), (pjC, "pjC")]
        bank_ctr = [0]

        def next_bank():
            b = bank_pool[bank_ctr[0] % 4]
            bank_ctr[0] += 1
            return b

        sems = {e: ec(nc.semaphore(f"sem_{e}")) for e in ("tensor", "vector", "scalar", "gpsimd")}
        dsems = {(e, j): ec(nc.semaphore(f"dsem_{e}{j}"))
                 for e in ("sync", "gpsimd") for j in range(3)}

        block = ec(nc.Block())

        for wsb, wdram, name in ((wq, wq_d, "wq"), (wk, wk_d, "wk"),
                                 (wv, wv_d, "wv"), (wo, wo_d, "wo"),
                                 (wf, wf_d, "wf")):
            s.add("gpsimd",
                  lambda e, a=wsb, b=wdram: e.dma_start(
                      out=a[:].rearrange("p (a o) -> p a o", a=8),
                      in_=b.rearrange("(a p) o -> p a o", p=128)),
                  writes=[name], dma=True, tag=f"load_{name}", cost=7000)
        s.add("scalar", lambda e: e.sign(wf[:], wf[:]),
              reads=["wf"], writes=["wf"], tag="sign_wf")
        s.add("sync", lambda e: e.dma_start(out=maskt[:], in_=mask_d),
              writes=["mask"], dma=True)
        s.add("sync", lambda e: e.dma_start(out=ident[:], in_=ident_d),
              writes=["ident"], dma=True)
        s.add("gpsimd", lambda e: e.dma_start(out=tabC[:], in_=tabC_d),
              writes=["tabC"], dma=True)
        s.add("gpsimd", lambda e: e.dma_start(out=tabS[:], in_=tabS_d),
              writes=["tabS"], dma=True)
        for i in range(2):
            s.add("vector", lambda e, i=i: e.memset(
                vaug[i][:].rearrange("p (h e) -> p h e", e=65)[:, :, 64], 1.0),
                writes=[f"vaug{i}"], tag="ones_vaug")
        s.add("vector", lambda e: e.memset(KVs[0][:], 0.0), writes=["KVs0"])
        s.add("vector", lambda e: e.memset(KVs[1][64:96, :], 0.0), writes=["KVs1"])
        for i in range(2):
            s.add("vector", lambda e, i=i: e.memset(qT[i][64:96, :], 0.0),
                  writes=[f"qT{i}A", f"qT{i}B"], tag="zpad_qT")
            s.add("vector", lambda e, i=i: e.memset(kT[i][64:96, :], 0.0),
                  writes=[f"kT{i}A", f"kT{i}B"], tag="zpad_kT")
        s.add("vector", lambda e: e.memset(epsc[:], EPS_LN), writes=["epsc"])
        s.add("vector", lambda e: e.memset(junkS[0][:], 0.0), writes=["junkS0"])
        for bi in range(3):
            ncols = 130 if bi == 2 else 455
            s.add("tensor",
                  lambda e, bi=bi, ncols=ncols: e.matmul(
                      kv[bi][0:64, 0:ncols], junkS[0][:, 0:64], junkS[0][:, 0:ncols],
                      start=True, stop=False, skip_group_check=True),
                  reads=["junkS0"], writes=[f"kv{bi}"], tag="kv_zero", cost=250)
            for _ in range(4):
                s.add("tensor", lambda e: e.nop(), tag="nop", cost=5)
        for j in range(min(2, NCHUNK)):
            s.add("sync", lambda e, j=j: e.dma_start(
                out=xs[j][:], in_=x_d[j * 128:(j + 1) * 128, :]),
                writes=[f"x{j}"], dma=True, tag=f"load_x{j}", cost=3500)

        tapset = set(DEBUG_TAPS)

        def tap(name, c, res, ap):
            if (name, c) not in tapset:
                return
            rl = list(res) if isinstance(res, (list, tuple)) else [res]
            shp = list(ap.shape)
            d = nc.dram_tensor(f"tap_{name}_{c}", shp, F32, kind="ExternalOutput").ap()
            s.add("gpsimd", lambda e, d=d, ap=ap: e.dma_start(out=d, in_=ap),
                  reads=rl, dma=True, tag=f"tap_{name}_{c}")

        def emit_ln_sums(c, xres, x_sb, sti):
            st = stats[sti]
            pre = f"st{sti}_"

            def R(nm):
                return pre + nm
            s.add("vector", lambda e: e.tensor_scalar(
                out=junkV[sti][:], in0=x_sb[:], scalar1=-1.0 / D, scalar2=0.0,
                op0=ALU.mult, op1=ALU.add, accum_out=st["s1"][:]),
                reads=[xres], writes=[f"junkV{sti}", R("s1")], tag=f"ln{sti}_sum")
            s.add("vector", lambda e: e.tensor_scalar(
                out=fenceV[:], in0=st["s1"][:], scalar1=1.0, scalar2=None,
                op0=ALU.mult),
                reads=[R("s1")], writes=[R("s1")], tag=f"ln{sti}_sumf", cost=80)
            s.add("scalar", lambda e: e.activation(
                junkS[sti][:], x_sb[:], AF.Square, accum_out=st["s2"][:]),
                reads=[xres], writes=[f"junkS{sti}", R("s2")], tag=f"ln{sti}_sq")
            s.add("scalar", lambda e: e.copy(fenceS[:], st["s2"][:]),
                reads=[R("s2")], writes=[R("s2")], tag=f"ln{sti}_sqf", cost=200)

        def emit_ln_rest(c, xres, x_sb, sti, lnres, ln_sb):
            st = stats[sti]
            pre = f"st{sti}_"

            def R(nm):
                return pre + nm
            s.add("scalar", lambda e: e.activation(st["msq"][:], st["s1"][:],
                                                   AF.Square),
                  reads=[R("s1")], writes=[R("msq")], cost=300)
            s.add("scalar", lambda e: e.activation(st["v2"][:], st["msq"][:],
                                                   AF.Copy, scale=-1.0,
                                                   bias=float(EPS_LN)),
                  reads=[R("msq")], writes=[R("v2")], cost=300)
            s.add("scalar", lambda e: e.activation(st["std"][:], st["s2"][:], AF.Ln,
                                                   scale=1.0 / D, bias=st["v2"][:]),
                  reads=[R("s2"), R("v2")], writes=[R("std")], cost=320)
            s.add("scalar", lambda e: e.activation(st["rstd"][:], st["std"][:], AF.Exp, scale=-0.5),
                  reads=[R("std")], writes=[R("rstd")], cost=320)
            s.add("scalar", lambda e: e.activation(st["nmr"][:], st["rstd"][:],
                                                   AF.Copy, scale=st["s1"][:]),
                  reads=[R("s1"), R("rstd")], writes=[R("nmr")], cost=300)
            s.add("vector", lambda e: e.tensor_scalar(
                out=ln_sb[:], in0=x_sb[:], scalar1=st["rstd"][:], scalar2=st["nmr"][:],
                op0=ALU.mult, op1=ALU.add),
                reads=[xres, R("rstd"), R("nmr")], writes=[lnres], tag=f"ln{sti}_apply")

        def emit_transposes(src_res, src_sb, dst_res, dst_sb, tag):
            for i in range(8):
                s.add("tensor", lambda e, i=i: e.transpose(
                    ptr[:, i * 128:(i + 1) * 128], src_sb[:, i * 128:(i + 1) * 128],
                    ident[:]),
                    reads=[src_res, "ident"],
                    writes=["ptr"], tag=f"tr_{tag}{i}")
            for hb in range(2):
                s.add("scalar", lambda e, hb=hb: e.copy(
                    dst_sb[:, hb * 512:(hb + 1) * 512],
                    ptr[:, hb * 512:(hb + 1) * 512]),
                    reads=["ptr"],
                    writes=[dst_res + ("A" if hb == 0 else "B")],
                    tag=f"ev_{tag}{hb}", cost=650)

        def emit_transposes64(src_res, src_sb, dst_res, dst_sb, tag, eng="scalar"):
            for i in range(8):
                s.add("tensor", lambda e, i=i: e.transpose(
                    ptr[:, i * 128:(i + 1) * 128], src_sb[:, i * 128:(i + 1) * 128],
                    ident[:]),
                    reads=[src_res, "ident"],
                    writes=["ptr"], tag=f"tr_{tag}{i}")
            pv = ptr[:].rearrange("p (a t) -> p a t", a=8)
            dv = dst_sb[0:64, :].rearrange("p (a hp t) -> p a hp t", a=8, hp=2)
            for hb in range(2):
                a0, a1 = 4 * hb, 4 * hb + 4
                rres = "ptr"
                wres = dst_res + ("A" if hb == 0 else "B")
                if eng == "scalar":
                    s.add("scalar", lambda e, a0=a0, a1=a1: e.copy(
                        dv[:, a0:a1, 0, :], pv[0:64, a0:a1, :]),
                        reads=[rres], writes=[wres], tag=f"ev_{tag}e{hb}", cost=360)
                    s.add("scalar", lambda e, a0=a0, a1=a1: e.copy(
                        dv[:, a0:a1, 1, :], pv[64:128, a0:a1, :]),
                        reads=[rres], writes=[wres], tag=f"ev_{tag}o{hb}", cost=360)
                else:
                    s.add("vector", lambda e, a0=a0, a1=a1: e.tensor_copy(
                        dv[:, a0:a1, 0, :], pv[0:64, a0:a1, :]),
                        reads=[rres], writes=[wres], tag=f"ev_{tag}e{hb}", cost=360)
                    s.add("vector", lambda e, a0=a0, a1=a1: e.tensor_copy(
                        dv[:, a0:a1, 1, :], pv[64:128, a0:a1, :]),
                        reads=[rres], writes=[wres], tag=f"ev_{tag}o{hb}", cost=360)

        def emit_proj(wres, w_sb, lres, lsb, tag, evac_half):
            lresA, lresB = lres
            banks = (next_bank(), next_bank())
            for o in range(8):
                pj, pjres = banks[0] if o < 4 else banks[1]
                off = (o % 4) * 128
                for i in range(8):
                    s.add("tensor", lambda e, o=o, i=i, pj=pj, off=off, lsb=lsb, w_sb=w_sb: e.matmul(
                        pj[:, off:off + 128],
                        lsb[:, i * 128:(i + 1) * 128],
                        w_sb[:, i * D + o * 128: i * D + o * 128 + 128],
                        start=(i == 0), stop=(i == 7)),
                        reads=[lresA if i < 4 else lresB, wres],
                        writes=[pjres], tag=f"{tag}_o{o}i{i}")
                if o == 3:
                    evac_half(0, banks[0])
            evac_half(1, banks[1])

        def sb_evac(dst_res, dst_sb, tag):
            def evac_half(half, bank):
                pj, pjres = bank
                s.add("scalar", lambda e, half=half, pj=pj, dst_sb=dst_sb: e.copy(
                    dst_sb[:, half * 512:(half + 1) * 512], pj[:]),
                    reads=[pjres], writes=[dst_res], tag=f"{tag}_h{half}")
            return evac_half

        def emit_rope_phi(c, src_res, src_sb, rdst_res, rdst, mdst_res, mdst,
                          edst_res, edst, pdst_res, pdst, tag):
            sv = src_sb[:].rearrange("p (h x j) -> p x h j", x=2, j=32)
            LO, HI = sv[:, 0], sv[:, 1]
            Cb = tabC[:, c * 32:(c + 1) * 32].unsqueeze(1).broadcast_to([128, H, 32])
            Sb = tabS[:, c * 32:(c + 1) * 32].unsqueeze(1).broadcast_to([128, H, 32])
            rAv = rA[:].rearrange("p (h j) -> p h j", j=32)
            rBv = rB[:].rearrange("p (h j) -> p h j", j=32)
            rCv = rC[:].rearrange("p (h j) -> p h j", j=32)
            rDv = rD[:].rearrange("p (h j) -> p h j", j=32)
            dv = rdst[:].rearrange("p (h x j) -> p x h j", x=2, j=32)
            s.add("gpsimd", lambda e: e.tensor_tensor(out=rAv, in0=HI, in1=Sb, op=ALU.mult),
                  reads=[src_res, "tabS"], writes=["rA"], tag=f"{tag}_rA")
            s.add("gpsimd", lambda e: e.tensor_tensor(out=rBv, in0=LO, in1=Sb, op=ALU.mult),
                  reads=[src_res, "tabS"], writes=["rB"], tag=f"{tag}_rB")
            s.add("vector", lambda e: e.tensor_tensor(out=rCv, in0=LO, in1=Cb, op=ALU.mult),
                  reads=[src_res, "tabC"], writes=["rC"], tag=f"{tag}_rC")
            s.add("vector", lambda e: e.tensor_tensor(out=rDv, in0=HI, in1=Cb, op=ALU.mult),
                  reads=[src_res, "tabC"], writes=["rD"], tag=f"{tag}_rD")
            s.add("vector", lambda e: e.tensor_tensor(out=dv[:, 0], in0=rCv, in1=rAv, op=ALU.subtract),
                  reads=["rC", "rA"], writes=[rdst_res], tag=f"{tag}_lo")
            s.add("vector", lambda e: e.tensor_tensor(out=dv[:, 1], in0=rBv, in1=rDv, op=ALU.add),
                  reads=["rB", "rD"], writes=[rdst_res], tag=f"{tag}_hi")
            s.add("vector", lambda e: e.tensor_scalar(
                out=mdst[:], in0=rdst[:], scalar1=0.0, scalar2=None, op0=ALU.min),
                reads=[rdst_res], writes=[mdst_res], tag=f"{tag}_min")
            s.add("scalar", lambda e: e.activation(edst[:], mdst[:], AF.Exp),
                  reads=[mdst_res], writes=[edst_res], tag=f"{tag}_exp")
            s.add("vector", lambda e: e.scalar_tensor_tensor(
                out=pdst[:], in0=rdst[:], scalar=0.0, in1=edst[:],
                op0=ALU.max, op1=ALU.add),
                reads=[rdst_res, edst_res], writes=[pdst_res], tag=f"{tag}_phi")

        def kv_slot(h):
            b = min(h // 7, 2)
            return b, (h - 7 * b) * 65

        def front_ln_sums(c):
            emit_ln_sums(c, f"x{c % 3}", xs[c % 3], c % 2)

        def front_ln_rest(c):
            i2 = c % 2
            emit_ln_rest(c, f"x{c % 3}", xs[c % 3], i2, f"ln1_{i2}", ln1[i2])
            tap("ln1", c, f"ln1_{i2}", ln1[i2][:])
            emit_transposes(f"ln1_{i2}", ln1[i2], f"ln1T{i2}", ln1T[i2], f"ln1T{c}")
            tap("ln1T", c, (f"ln1T{i2}A", f"ln1T{i2}B"), ln1T[i2][:])

        def front_q(c):
            i2 = c % 2
            emit_proj("wq", wq, (f"ln1T{i2}A", f"ln1T{i2}B"), ln1T[i2], f"q{c}",
                      sb_evac(f"qev{i2}", qev[i2], f"qev{c}"))
            tap("qev", c, f"qev{i2}", qev[i2][:])

        def front_k(c):
            i2 = c % 2
            emit_proj("wk", wk, (f"ln1T{i2}A", f"ln1T{i2}B"), ln1T[i2], f"k{c}",
                      sb_evac(f"kev{i2}", kev[i2], f"kev{c}"))
            tap("kev", c, f"kev{i2}", kev[i2][:])

        def front_v(c):
            v_sb = vaug[c % 2]
            vres = f"vaug{c % 2}"
            vv = v_sb[:].rearrange("p (h e) -> p h e", e=65)

            def evac_half(half, bank):
                pj, pjres = bank
                s.add("scalar", lambda e, half=half, pj=pj, vv=vv: e.copy(
                    vv[:, half * 8:(half + 1) * 8, 0:64],
                    pj[:].rearrange("p (h e) -> p h e", e=64)),
                    reads=[pjres], writes=[vres], tag=f"vev{c}_h{half}")
            emit_proj("wv", wv, (f"ln1T{c % 2}A", f"ln1T{c % 2}B"), ln1T[c % 2],
                      f"v{c}", evac_half)
            tap("vaug", c, vres, v_sb[:])

        def front_rope_q(c):
            i2 = c % 2
            emit_rope_phi(c, f"qev{i2}", qev[i2], "qr", qr, "qm", qm, "qE", qE,
                          f"qphi{i2}", qphi[i2], f"rq{c}")
            tap("qphi", c, f"qphi{i2}", qphi[i2][:])

        def front_rope_k(c):
            i2 = c % 2
            emit_rope_phi(c, f"kev{i2}", kev[i2], "kr", kr, "km", km, "kE", kE,
                          f"kphi{i2}", kphi[i2], f"rk{c}")
            tap("kphi", c, f"kphi{i2}", kphi[i2][:])

        def front_qkT(c):
            i2 = c % 2
            emit_transposes64(f"qphi{i2}", qphi[i2], f"qT{i2}", qT[i2], f"qT{c}")
            emit_transposes64(f"kphi{i2}", kphi[i2], f"kT{i2}", kT[i2], f"kT{c}", eng="vector")
            tap("qT", c, (f"qT{i2}A", f"qT{i2}B"), qT[i2][0:64, :])
            tap("kT", c, (f"kT{i2}A", f"kT{i2}B"), kT[i2][0:64, :])

        def back_attn(c):
            i2 = c % 2
            qTs, kTs = qT[i2], kT[i2]
            qTh = lambda h: f"qT{i2}A" if h < 8 else f"qT{i2}B"
            kTh = lambda h: f"kT{i2}A" if h < 8 else f"kT{i2}B"
            v_sb, vres = vaug[i2], f"vaug{i2}"
            kph, kphres = kphi[i2], f"kphi{i2}"
            kvres_r = f"KVs{i2}"
            kvres_w = f"KVs{(c + 1) % 2}"
            mb = maskt[:].unsqueeze(1).broadcast_to([128, 4, 128])
            for g in range(4):
                ab, abres = next_bank()
                for hh in range(4):
                    h = g * 4 + hh
                    s.add("tensor", lambda e, h=h, hh=hh, qTs=qTs, kTs=kTs, ab=ab: e.matmul(
                        ab[:, hh * 128:(hh + 1) * 128],
                        kTs[0:96, h * 128:(h + 1) * 128],
                        qTs[0:96, h * 128:(h + 1) * 128],
                        start=True, stop=True, skip_group_check=True),
                        reads=[qTh(h), kTh(h)], writes=[abres], tag=f"A{c}_h{h}")
                s.add("vector", lambda e, g=g, ab=ab: e.tensor_tensor(
                    out=Asb[:, g * 512:(g + 1) * 512].rearrange("p (h t) -> p h t", t=128),
                    in0=ab[:, 0:512].rearrange("p (h t) -> p h t", t=128),
                    in1=mb, op=ALU.mult),
                    reads=[abres, "mask"], writes=[f"A{g}"], tag=f"mask{c}_g{g}")
                for hh in range(4):
                    h = g * 4 + hh
                    b, off = kv_slot(h)
                    s.add("tensor", lambda e, h=h, b=b, off=off, v_sb=v_sb, kph=kph, c=c: e.matmul(
                        kv[b][0:64, off:off + 65],
                        kph[:, h * 64:(h + 1) * 64],
                        v_sb[:, h * 65:(h + 1) * 65],
                        start=False, stop=(c == NCHUNK - 1), skip_group_check=True),
                        reads=[kphres, vres], writes=[f"kv{b}"], tag=f"kv{c}_h{h}")
                tap(f"Asb{g}", c, f"A{g}", Asb[:, g * 512:(g + 1) * 512])
            if c < NCHUNK - 1:
                kvv = KVs[(c + 1) % 2]
                s.add("scalar", lambda e, kvv=kvv: e.copy(kvv[0:64, 0:455], kv[0][0:64, :]),
                      reads=["kv0"], writes=[kvres_w], tag=f"kvev{c}_0", cost=550)
                s.add("scalar", lambda e, kvv=kvv: e.copy(kvv[0:64, 455:910], kv[1][0:64, :]),
                      reads=["kv1"], writes=[kvres_w], tag=f"kvev{c}_1", cost=550)
                s.add("scalar", lambda e, kvv=kvv: e.copy(kvv[0:64, 910:1040], kv[2][0:64, :]),
                      reads=["kv2"], writes=[kvres_w], tag=f"kvev{c}_2", cost=550)
                tap("KVs", c, kvres_w, KVs[(c + 1) % 2][0:64, :])
            ytiles_after = {0: (0, 1, 2), 1: (3, 4, 5, 6), 2: (7,)}
            for p in range(3):
                nh = 2 if p == 2 else 7
                h0 = 7 * p
                yb, ybres = next_bank()
                for hh in range(nh):
                    h = h0 + hh
                    s.add("tensor", lambda e, h=h, hh=hh, v_sb=v_sb, yb=yb: e.matmul(
                        yb[:, hh * 65:hh * 65 + 65],
                        Asb[:, h * 128:(h + 1) * 128],
                        v_sb[:, h * 65:(h + 1) * 65],
                        start=True, stop=False, skip_group_check=True),
                        reads=[f"A{h // 4}", vres], writes=[ybres], tag=f"Yi{c}_h{h}")
                    s.add("tensor", lambda e, h=h, hh=hh, qTs=qTs, kvsb=KVs[i2], yb=yb: e.matmul(
                        yb[:, hh * 65:hh * 65 + 65],
                        qTs[0:96, h * 128:(h + 1) * 128],
                        kvsb[0:96, h * 65:(h + 1) * 65],
                        start=False, stop=True, skip_group_check=True),
                        reads=[qTh(h), kvres_r], writes=[ybres], tag=f"Yx{c}_h{h}")
                pyv = yb[:, 0:455].rearrange("p (h e) -> p h e", e=65)
                s.add("vector", lambda e, nh=nh, h0=h0, pyv=pyv: e.tensor_scalar(
                    out=r16a[:, h0:h0 + nh], in0=pyv[:, 0:nh, 64], scalar1=EPS_DEN,
                    scalar2=None, op0=ALU.max),
                    reads=[ybres], writes=[f"r16a{p}"], tag=f"den{c}_p{p}", cost=220)
                s.add("vector", lambda e, nh=nh, h0=h0: e.reciprocal(
                    r16b[:, h0:h0 + nh], r16a[:, h0:h0 + nh]),
                    reads=[f"r16a{p}"], writes=[f"r16b{p}"], cost=220)
                for dh, (a, b) in enumerate(((0, min(4, nh)), (min(4, nh), nh))):
                    if b <= a:
                        continue
                    s.add("vector", lambda e, a=a, b=b, h0=h0, pyv=pyv: e.tensor_tensor(
                        out=Ysb[:, (h0 + a) * 64:(h0 + b) * 64].rearrange("p (h e) -> p h e", e=64),
                        in0=pyv[:, a:b, 0:64],
                        in1=r16b[:, h0 + a:h0 + b].unsqueeze(2).broadcast_to([128, b - a, 64]),
                        op=ALU.mult),
                        reads=[ybres, f"r16b{p}"], writes=[f"Y{p}h{dh}"],
                        tag=f"ydiv{c}_p{p}h{dh}", cost=420)

                def yres(h):
                    pp = min(h // 7, 2)
                    return f"Y{pp}h{0 if h - 7 * pp < 4 else 1}"
                for i in ytiles_after[p]:
                    s.add("tensor", lambda e, i=i: e.transpose(
                        ptr[:, i * 128:(i + 1) * 128], Ysb[:, i * 128:(i + 1) * 128],
                        ident[:]),
                        reads=[yres(2 * i), yres(2 * i + 1), "ident"],
                        writes=["ptr"], tag=f"tr_YT{c}{i}")
                tap(f"Ysb{p}", c, (f"Y{p}h0", f"Y{p}h1" if nh > 4 else f"Y{p}h0"),
                    Ysb[:, 7 * p * 64: (7 * p + nh) * 64])
            for hb in range(2):
                s.add("scalar", lambda e, hb=hb: e.copy(
                    YT[:, hb * 512:(hb + 1) * 512], ptr[:, hb * 512:(hb + 1) * 512]),
                    reads=["ptr"],
                    writes=["YTA" if hb == 0 else "YTB"],
                    tag=f"ev_YT{c}{hb}", cost=650)

        def back_wo(c):
            x_sb, xres = xs[c % 3], f"x{c % 3}"
            x2sb, x2res = x2[c % 2], f"x2_{c % 2}"

            def evac_half(half, bank):
                pj, pjres = bank
                s.add("vector", lambda e, half=half, pj=pj, x2sb=x2sb, x_sb=x_sb: e.tensor_tensor(
                    out=x2sb[:, half * 512:(half + 1) * 512], in0=pj[:],
                    in1=x_sb[:, half * 512:(half + 1) * 512], op=ALU.add),
                    reads=[pjres, xres], writes=[x2res], tag=f"res1_{c}_h{half}")
            emit_proj("wo", wo, ("YTA", "YTB"), YT, f"wo{c}", evac_half)
            tap("x2", c, x2res, x2sb[:])
            emit_ln_sums(c, x2res, x2sb, 2)

        def back_ffn(c):
            x2sb, x2res = x2[c % 2], f"x2_{c % 2}"
            osb, outres = outb[c % 2], f"out{c % 2}"
            emit_ln_rest(c, x2res, x2sb, 2, "ln2", ln2)
            emit_transposes("ln2", ln2, "ln2T", ln2T, f"ln2T{c}")

            def evac_half(half, bank):
                pj, pjres = bank
                s.add("vector", lambda e, half=half, pj=pj, osb=osb, x2sb=x2sb: e.tensor_tensor(
                    out=osb[:, half * 512:(half + 1) * 512], in0=pj[:],
                    in1=x2sb[:, half * 512:(half + 1) * 512], op=ALU.add),
                    reads=[pjres, x2res], writes=[outres], tag=f"res2_{c}_h{half}")
            emit_proj("wf", wf, ("ln2TA", "ln2TB"), ln2T, f"ffn{c}", evac_half)
            s.add("sync", lambda e, c=c, osb=osb: e.dma_start(
                out=out_d[c * 128:(c + 1) * 128, :], in_=osb[:]),
                reads=[outres], dma=True, tag=f"store{c}", cost=3500)

        if NCHUNK > 2:
            s.add("sync", lambda e: e.dma_start(out=xs[2][:], in_=x_d[256:384, :]),
                  writes=["x2"], dma=True, tag="load_x2", cost=3500)
        front_ln_sums(0)
        front_ln_rest(0)
        front_q(0)
        front_rope_q(0)
        front_k(0)
        front_rope_k(0)
        front_v(0)
        front_qkT(0)
        for c in range(NCHUNK):
            n = c + 1
            if n < NCHUNK:
                front_ln_sums(n)
            back_attn(c)
            if n < NCHUNK:
                front_ln_rest(n)
                front_q(n)
                front_rope_q(n)
            back_wo(c)
            if n < NCHUNK:
                front_k(n)
                front_rope_k(n)
            back_ffn(c)
            if n < NCHUNK:
                front_v(n)
                front_qkT(n)
            if c + 3 < NCHUNK:
                s.add("sync", lambda e, c=c: e.dma_start(
                    out=xs[(c + 3) % 3][:], in_=x_d[(c + 3) * 128:(c + 4) * 128, :]),
                    writes=[f"x{(c + 3) % 3}"], dma=True, tag=f"load_x{c + 3}", cost=3500)

        order = s.schedule()
        waits, needs_inc, dma_no, inc_no = s.plan(order)
        by_engine = {e: [s.ops[gi] for gi in order if s.ops[gi]["engine"] == e]
                     for e in ENGINES}
        n_sync_dma = sum(1 for op in by_engine["sync"] if op["dma"])

        def emit_engine(name, eng):
            for op in by_engine[name]:
                i = op["i"]
                for (kind, tgt, val) in waits[i]:
                    if kind == "dma":
                        eng.wait_ge(dsems[tgt], 16 * val)
                    else:
                        eng.wait_ge(sems[tgt], inc_no[val])
                instr = op["fn"](eng)
                if op["dma"]:
                    k = dma_no[op["i"]]
                    instr.then_inc(dsems[(name, (k - 1) % 3)], 16)
                elif needs_inc[i] and instr is not None:
                    instr.then_inc(sems[name], 1)
            if name in ("sync", "gpsimd"):
                nq = sum(1 for op in by_engine[name] if op["dma"])
                for j in range(3):
                    m_tot = (nq - j + 2) // 3
                    if m_tot > 0:
                        eng.wait_ge(dsems[(name, j)], 16 * m_tot)

        @block.tensor
        def _(tensor):
            emit_engine("tensor", tensor)

        @block.vector
        def _(vector):
            emit_engine("vector", vector)

        @block.scalar
        def _(scalar):
            emit_engine("scalar", scalar)

        @block.gpsimd
        def _(gpsimd):
            emit_engine("gpsimd", gpsimd)

        @block.sync
        def _(sync):
            emit_engine("sync", sync)

    return nc


def _host_tables():
    inv = 10000.0 ** (-np.arange(32, dtype=np.float64) / 32.0)
    ang = np.arange(T_SEQ, dtype=np.float64)[:, None] * inv[None, :]
    C = np.cos(ang).reshape(NCHUNK, 128, 32).transpose(1, 0, 2).reshape(128, NCHUNK * 32)
    S = np.sin(ang).reshape(NCHUNK, 128, 32).transpose(1, 0, 2).reshape(128, NCHUNK * 32)
    mask = (np.arange(128)[:, None] <= np.arange(128)[None, :]).astype(np.float64)
    ident = np.eye(128)
    bf = ml_dtypes.bfloat16
    return C.astype(np.float32), S.astype(np.float32), mask.astype(bf), ident.astype(bf)


def kernel(L0_attn_x, L0_attn_Wq, L0_attn_Wk, L0_attn_Wv, L0_attn_Wo, L0_ffn_weight):
    x = np.asarray(L0_attn_x, dtype=np.float32)
    wqT = np.ascontiguousarray(np.asarray(L0_attn_Wq, dtype=np.float32).T)
    wkT = np.ascontiguousarray(np.asarray(L0_attn_Wk, dtype=np.float32).T)
    wvT = np.ascontiguousarray(np.asarray(L0_attn_Wv, dtype=np.float32).T)
    woT = np.ascontiguousarray(np.asarray(L0_attn_Wo, dtype=np.float32).T)
    wfT = np.ascontiguousarray(np.asarray(L0_ffn_weight, dtype=np.float32).T)
    tabC, tabS, mask, ident = _host_tables()

    nc = build_program()
    in_maps = []
    for core in range(N_CORES):
        in_maps.append({
            "x": np.ascontiguousarray(x[core]),
            "wqT": wqT, "wkT": wkT, "wvT": wvT, "woT": woT, "wfT": wfT,
            "tabC": tabC, "tabS": tabS, "mask": mask, "ident": ident,
        })
    res = run_bass_kernel_spmd(nc, in_maps, core_ids=list(range(N_CORES)))
    out = np.stack([np.asarray(r["out"]) for r in res.results], axis=0)
    return out


# version 48
# speedup vs baseline: 1.0403x; 1.0097x over previous
import os
import numpy as np
import ml_dtypes
import concourse.bass as bass
import concourse.mybir as mybir
from concourse.bass_utils import run_bass_kernel_spmd
from contextlib import ExitStack

F32 = mybir.dt.float32
BF16 = mybir.dt.bfloat16
AF = mybir.ActivationFunctionType
ALU = mybir.AluOpType

T_SEQ = 4096
D = 1024
H = 16
DH = 64
CH = 128
NCHUNK = T_SEQ // CH
N_CORES = 8
EPS_LN = 1e-5
EPS_DEN = 1e-6

ENGINES = ("tensor", "vector", "scalar", "gpsimd", "sync")

DEBUG_TAPS = ()


class Sched:

    def __init__(self):
        self.ops = []
        self.lastw = {}
        self.readers = {}
        self.eng_n = {e: 0 for e in ENGINES}

    DEF_COST = {"tensor": 70, "vector": 700, "scalar": 900, "gpsimd": 1200,
                "sync": 1500}

    def add(self, engine, fn, reads=(), writes=(), dma=False, tag="", cost=None):
        i = len(self.ops)
        deps = set()
        for r in reads:
            w = self.lastw.get(r)
            if w is not None:
                deps.add(w)
        for r in writes:
            w = self.lastw.get(r)
            if w is not None:
                deps.add(w)
            deps.update(self.readers.get(r, ()))
        deps.discard(i)
        self.ops.append(dict(i=i, engine=engine, fn=fn, deps=sorted(deps),
                             dma=dma, tag=tag,
                             cost=float(cost if cost is not None
                                        else self.DEF_COST[engine])))
        for r in reads:
            self.readers.setdefault(r, []).append(i)
        for r in writes:
            self.lastw[r] = i
            self.readers[r] = []
        return i

    def schedule(self):
        ops = self.ops
        n = len(ops)
        succ = [[] for _ in range(n)]
        pend = [0] * n
        for op in ops:
            for d in op["deps"]:
                succ[d].append(op["i"])
                pend[op["i"]] += 1
        LAT = float(os.environ.get("SIMLAT", 480.0))
        prio = [0.0] * n
        for i in range(n - 1, -1, -1):
            m = 0.0
            for sj in succ[i]:
                if prio[sj] > m:
                    m = prio[sj]
            prio[i] = ops[i]["cost"] + (LAT if succ[i] else 0.0) + m
        eng_free = {e: 0.0 for e in ENGINES}
        avail = [0.0] * n
        done = [0.0] * n
        ready = {e: [] for e in ENGINES}
        for i in range(n):
            if pend[i] == 0:
                ready[ops[i]["engine"]].append(i)
        order = []
        remaining = n
        while remaining:
            best = None
            for e in ENGINES:
                lst = ready[e]
                if not lst:
                    continue
                ef = eng_free[e]
                for i in lst:
                    st = ef if ef > avail[i] else avail[i]
                    key = (st, -prio[i], i)
                    if best is None or key < best[0]:
                        best = (key, e, i)
            key, e, i = best
            st = key[0]
            ready[e].remove(i)
            done[i] = st + ops[i]["cost"]
            eng_free[e] = done[i]
            order.append(i)
            remaining -= 1
            for sj in succ[i]:
                pend[sj] -= 1
                t = done[i] + (LAT if ops[sj]["engine"] != e else 0.0)
                if avail[sj] < t:
                    avail[sj] = t
                if pend[sj] == 0:
                    ready[ops[sj]["engine"]].append(sj)
        cnt = {e: 0 for e in ENGINES}
        for i in order:
            ops[i]["eidx"] = cnt[ops[i]["engine"]]
            cnt[ops[i]["engine"]] += 1
        self.sim_makespan = max(done) if n else 0.0
        if os.environ.get('SIM_PRINT'):
            print('SIM makespan:', round(self.sim_makespan / 1000, 1), 'us')
            start_t = {i: done[i] - ops[i]["cost"] for i in range(n)}
            eng_prev = {}
            last = {e: None for e in ENGINES}
            for i in order:
                e2 = ops[i]["engine"]
                eng_prev[i] = last[e2]
                last[e2] = i
            cur = max(range(n), key=lambda i: done[i])
            from collections import Counter
            hist = Counter(); hist_t = Counter()
            while cur is not None:
                st = start_t[cur]
                nxt = None
                for d in ops[cur]["deps"]:
                    t = done[d] + (LAT if ops[d]["engine"] != ops[cur]["engine"] else 0)
                    if abs(t - st) < 1e-6:
                        nxt = d
                        break
                if nxt is None:
                    ep = eng_prev[cur]
                    if ep is not None and abs(done[ep] - st) < 1e-6:
                        nxt = ep
                import re as _re
                key = _re.sub(r"\d+", "#", ops[cur]["tag"]) or ops[cur]["engine"]
                hist[key] += 1; hist_t[key] += ops[cur]["cost"]
                cur = nxt
            print("SIM critical-path composition (count, total_us):")
            for k, v in hist.most_common(16):
                print(f"  {k:24s} n={v:5d}  t={hist_t[k]/1000:.1f}us")
        return order

    def plan(self, order):
        ops = self.ops
        n = len(ops)
        waits = [[] for _ in range(n)]
        needs_inc = [False] * n
        dma_no = {}
        dma_count = {e: 0 for e in ENGINES}
        clocks = [None] * n
        last_on = {e: None for e in ENGINES}

        def fresh_clock():
            c = {e: -1 for e in ENGINES}
            for e in ENGINES:
                for j in range(3):
                    c[f"{e}_dma{j}"] = 0
            return c

        for gi in order:
            op = ops[gi]
            i = op["i"]
            eng = op["engine"]
            prev = last_on[eng]
            clock = dict(clocks[prev]) if prev is not None else fresh_clock()
            if op["dma"]:
                dma_count[eng] += 1
                k = dma_count[eng]
                dma_no[i] = k
                j = (k - 1) % 3
                m_prev = (k - 1) // 3
                if m_prev > 0 and clock[f"{eng}_dma{j}"] < m_prev:
                    waits[i].append(("dma", (eng, j), m_prev))
                    clock[f"{eng}_dma{j}"] = m_prev
            for d in op["deps"]:
                dop = ops[d]
                de = dop["engine"]
                if dop["dma"]:
                    k = dma_no[d]
                    j = (k - 1) % 3
                    m = (k - 1) // 3 + 1
                    if clock[f"{de}_dma{j}"] >= m:
                        continue
                    waits[i].append(("dma", (de, j), m))
                    clock[f"{de}_dma{j}"] = m
                    dc = clocks[d]
                    for key, v in dc.items():
                        if clock[key] < v:
                            clock[key] = v
                else:
                    if de == eng and de in ("tensor", "sync"):
                        continue
                    if clock[de] >= dop["eidx"]:
                        continue
                    waits[i].append(("eng", de, d))
                    needs_inc[d] = True
                    clock[de] = dop["eidx"]
                    dc = clocks[d]
                    for key, v in dc.items():
                        if clock[key] < v:
                            clock[key] = v
            clocks[i] = clock
            last_on[eng] = i
        inc_no = {}
        cnt = {e: 0 for e in ENGINES}
        for gi in order:
            op = ops[gi]
            if needs_inc[op["i"]] and not op["dma"]:
                cnt[op["engine"]] += 1
                inc_no[op["i"]] = cnt[op["engine"]]
        return waits, needs_inc, dma_no, inc_no


def build_program():
    nc = bass.Bass("TRN2", target_bir_lowering=False)

    x_d = nc.dram_tensor("x", [T_SEQ, D], F32, kind="ExternalInput").ap()
    wq_d = nc.dram_tensor("wqT", [D, D], F32, kind="ExternalInput").ap()
    wk_d = nc.dram_tensor("wkT", [D, D], F32, kind="ExternalInput").ap()
    wv_d = nc.dram_tensor("wvT", [D, D], F32, kind="ExternalInput").ap()
    wo_d = nc.dram_tensor("woT", [D, D], F32, kind="ExternalInput").ap()
    wf_d = nc.dram_tensor("wfT", [D, D], F32, kind="ExternalInput").ap()
    tabC_d = nc.dram_tensor("tabC", [128, NCHUNK * 32], F32, kind="ExternalInput").ap()
    tabS_d = nc.dram_tensor("tabS", [128, NCHUNK * 32], F32, kind="ExternalInput").ap()
    mask_d = nc.dram_tensor("mask", [128, 128], BF16, kind="ExternalInput").ap()
    ident_d = nc.dram_tensor("ident", [128, 128], BF16, kind="ExternalInput").ap()
    out_d = nc.dram_tensor("out", [T_SEQ, D], F32, kind="ExternalOutput").ap()

    s = Sched()

    with ExitStack() as es:
        ec = es.enter_context

        wq = ec(nc.sbuf_tensor("wq", [128, 8 * D], BF16))
        wk = ec(nc.sbuf_tensor("wk", [128, 8 * D], BF16))
        wv = ec(nc.sbuf_tensor("wv", [128, 8 * D], BF16))
        wo = ec(nc.sbuf_tensor("wo", [128, 8 * D], BF16))
        wf = ec(nc.sbuf_tensor("wf", [128, 8 * D], BF16))
        tabC = ec(nc.sbuf_tensor("tabC_s", [128, NCHUNK * 32], BF16))
        tabS = ec(nc.sbuf_tensor("tabS_s", [128, NCHUNK * 32], BF16))
        maskt = ec(nc.sbuf_tensor("mask_s", [128, 128], BF16))
        ident = ec(nc.sbuf_tensor("ident_s", [128, 128], BF16))
        xs = [ec(nc.sbuf_tensor(f"xs{i}", [128, D], F32)) for i in range(3)]
        junkV = [ec(nc.sbuf_tensor(f"junkV{i}", [128, D], BF16)) for i in range(3)]
        junkS = [ec(nc.sbuf_tensor(f"junkS{i}", [128, D], BF16)) for i in range(3)]
        ln1 = [ec(nc.sbuf_tensor(f"ln1_{i}", [128, D], BF16)) for i in range(2)]
        ln2 = ec(nc.sbuf_tensor("ln2", [128, D], BF16))
        ln1T = [ec(nc.sbuf_tensor(f"ln1T{i}", [128, D], BF16)) for i in range(2)]
        ln2T = ec(nc.sbuf_tensor("ln2T", [128, D], BF16))
        qev = [ec(nc.sbuf_tensor(f"qev{i}", [128, D], BF16)) for i in range(2)]
        kev = [ec(nc.sbuf_tensor(f"kev{i}", [128, D], BF16)) for i in range(2)]
        qr = ec(nc.sbuf_tensor("qr", [128, D], BF16))
        kr = ec(nc.sbuf_tensor("kr", [128, D], BF16))
        qm = ec(nc.sbuf_tensor("qm", [128, D], BF16))
        km = ec(nc.sbuf_tensor("km", [128, D], BF16))
        qE = ec(nc.sbuf_tensor("qE", [128, D], BF16))
        kE = ec(nc.sbuf_tensor("kE", [128, D], BF16))
        qphi = [ec(nc.sbuf_tensor(f"qphi{i}", [128, D], BF16)) for i in range(2)]
        kphi = [ec(nc.sbuf_tensor(f"kphi{i}", [128, D], BF16)) for i in range(2)]
        rA = ec(nc.sbuf_tensor("rA", [128, 512], BF16))
        rB = ec(nc.sbuf_tensor("rB", [128, 512], BF16))
        rC = ec(nc.sbuf_tensor("rC", [128, 512], BF16))
        rD = ec(nc.sbuf_tensor("rD", [128, 512], BF16))
        qT = [ec(nc.sbuf_tensor(f"qT{i}", [96, H * 128], BF16)) for i in range(2)]
        kT = [ec(nc.sbuf_tensor(f"kT{i}", [96, H * 128], BF16)) for i in range(2)]
        vaug = [ec(nc.sbuf_tensor(f"vaug{i}", [128, H * 65], BF16)) for i in range(2)]
        Asb = ec(nc.sbuf_tensor("Asb", [128, H * 128], BF16))
        KVs = [ec(nc.sbuf_tensor(f"KVs{i}", [96, H * 65], BF16)) for i in range(2)]
        Ysb = ec(nc.sbuf_tensor("Ysb", [128, D], BF16))
        YT = ec(nc.sbuf_tensor("YT", [128, D], BF16))
        x2 = [ec(nc.sbuf_tensor(f"x2_{i}", [128, D], F32)) for i in range(2)]
        outb = [ec(nc.sbuf_tensor(f"outb{i}", [128, D], F32)) for i in range(2)]
        r16a = ec(nc.sbuf_tensor("r16a", [128, 16], F32))
        epsc = ec(nc.sbuf_tensor("epsc", [128, 1], F32))
        fenceV = ec(nc.sbuf_tensor("fenceV", [128, 1], F32))
        fenceS = ec(nc.sbuf_tensor("fenceS", [128, 1], F32))
        r16b = ec(nc.sbuf_tensor("r16b", [128, 16], F32))
        stats = []
        for t in range(3):
            st = {}
            for nm in ("s1", "s2", "nm", "msq", "v1", "v2", "std", "rstd", "nmr"):
                st[nm] = ec(nc.sbuf_tensor(f"st{t}_{nm}", [128, 1], F32))
            stats.append(st)

        kv = [ec(nc.psum_tensor("kv0", [64, 7 * 65], F32)),
              ec(nc.psum_tensor("kv1", [64, 7 * 65], F32)),
              ec(nc.psum_tensor("kv2", [64, 2 * 65], F32))]
        att = ec(nc.psum_tensor("att", [128, 512], F32))
        ptr = ec(nc.psum_tensor("ptr", [128, 1024], BF16))
        pjA = ec(nc.psum_tensor("pjA", [128, 512], F32))
        pjB = ec(nc.psum_tensor("pjB", [128, 512], F32))
        pjC = ec(nc.psum_tensor("pjC", [128, 512], F32))
        bank_pool = [(att, "att"), (pjA, "pjA"), (pjB, "pjB"), (pjC, "pjC")]
        bank_ctr = [0]

        def next_bank():
            b = bank_pool[bank_ctr[0] % 4]
            bank_ctr[0] += 1
            return b

        sems = {e: ec(nc.semaphore(f"sem_{e}")) for e in ("tensor", "vector", "scalar", "gpsimd")}
        dsems = {(e, j): ec(nc.semaphore(f"dsem_{e}{j}"))
                 for e in ("sync", "gpsimd") for j in range(3)}

        block = ec(nc.Block())

        for wsb, wdram, name in ((wq, wq_d, "wq"), (wk, wk_d, "wk"),
                                 (wv, wv_d, "wv"), (wo, wo_d, "wo"),
                                 (wf, wf_d, "wf")):
            s.add("gpsimd",
                  lambda e, a=wsb, b=wdram: e.dma_start(
                      out=a[:].rearrange("p (a o) -> p a o", a=8),
                      in_=b.rearrange("(a p) o -> p a o", p=128)),
                  writes=[name], dma=True, tag=f"load_{name}", cost=7000)
        s.add("scalar", lambda e: e.sign(wf[:], wf[:]),
              reads=["wf"], writes=["wf"], tag="sign_wf")
        s.add("sync", lambda e: e.dma_start(out=maskt[:], in_=mask_d),
              writes=["mask"], dma=True)
        s.add("sync", lambda e: e.dma_start(out=ident[:], in_=ident_d),
              writes=["ident"], dma=True)
        s.add("gpsimd", lambda e: e.dma_start(out=tabC[:], in_=tabC_d),
              writes=["tabC"], dma=True)
        s.add("gpsimd", lambda e: e.dma_start(out=tabS[:], in_=tabS_d),
              writes=["tabS"], dma=True)
        for i in range(2):
            s.add("vector", lambda e, i=i: e.memset(
                vaug[i][:].rearrange("p (h e) -> p h e", e=65)[:, :, 64], 1.0),
                writes=[f"vaug{i}"], tag="ones_vaug")
        s.add("vector", lambda e: e.memset(KVs[0][:], 0.0), writes=["KVs0"])
        s.add("vector", lambda e: e.memset(KVs[1][64:96, :], 0.0), writes=["KVs1"])
        for i in range(2):
            s.add("vector", lambda e, i=i: e.memset(qT[i][64:96, :], 0.0),
                  writes=[f"qT{i}A", f"qT{i}B"], tag="zpad_qT")
            s.add("vector", lambda e, i=i: e.memset(kT[i][64:96, :], 0.0),
                  writes=[f"kT{i}A", f"kT{i}B"], tag="zpad_kT")
        s.add("vector", lambda e: e.memset(epsc[:], EPS_LN), writes=["epsc"])
        s.add("vector", lambda e: e.memset(junkS[0][:], 0.0), writes=["junkS0"])
        for bi in range(3):
            ncols = 130 if bi == 2 else 455
            s.add("tensor",
                  lambda e, bi=bi, ncols=ncols: e.matmul(
                      kv[bi][0:64, 0:ncols], junkS[0][:, 0:64], junkS[0][:, 0:ncols],
                      start=True, stop=False, skip_group_check=True),
                  reads=["junkS0"], writes=[f"kv{bi}"], tag="kv_zero", cost=250)
            for _ in range(4):
                s.add("tensor", lambda e: e.nop(), tag="nop", cost=5)
        for j in range(min(2, NCHUNK)):
            s.add("sync", lambda e, j=j: e.dma_start(
                out=xs[j][:], in_=x_d[j * 128:(j + 1) * 128, :]),
                writes=[f"x{j}"], dma=True, tag=f"load_x{j}", cost=3500)

        tapset = set(DEBUG_TAPS)

        def tap(name, c, res, ap):
            if (name, c) not in tapset:
                return
            rl = list(res) if isinstance(res, (list, tuple)) else [res]
            shp = list(ap.shape)
            d = nc.dram_tensor(f"tap_{name}_{c}", shp, F32, kind="ExternalOutput").ap()
            s.add("gpsimd", lambda e, d=d, ap=ap: e.dma_start(out=d, in_=ap),
                  reads=rl, dma=True, tag=f"tap_{name}_{c}")

        def emit_ln_sums(c, xres, x_sb, sti):
            st = stats[sti]
            pre = f"st{sti}_"

            def R(nm):
                return pre + nm
            s.add("vector", lambda e: e.tensor_scalar(
                out=junkV[sti][:], in0=x_sb[:], scalar1=-1.0 / D, scalar2=0.0,
                op0=ALU.mult, op1=ALU.add, accum_out=st["s1"][:]),
                reads=[xres], writes=[f"junkV{sti}", R("s1")], tag=f"ln{sti}_sum")
            s.add("vector", lambda e: e.tensor_scalar(
                out=fenceV[:], in0=st["s1"][:], scalar1=1.0, scalar2=None,
                op0=ALU.mult),
                reads=[R("s1")], writes=[R("s1")], tag=f"ln{sti}_sumf", cost=80)
            s.add("scalar", lambda e: e.activation(
                junkS[sti][:], x_sb[:], AF.Square, accum_out=st["s2"][:]),
                reads=[xres], writes=[f"junkS{sti}", R("s2")], tag=f"ln{sti}_sq")
            s.add("scalar", lambda e: e.copy(fenceS[:], st["s2"][:]),
                reads=[R("s2")], writes=[R("s2")], tag=f"ln{sti}_sqf", cost=200)

        def emit_ln_rest(c, xres, x_sb, sti, lnres, ln_sb):
            st = stats[sti]
            pre = f"st{sti}_"

            def R(nm):
                return pre + nm
            s.add("scalar", lambda e: e.activation(st["msq"][:], st["s1"][:],
                                                   AF.Square),
                  reads=[R("s1")], writes=[R("msq")], cost=300)
            s.add("scalar", lambda e: e.activation(st["v2"][:], st["msq"][:],
                                                   AF.Copy, scale=-1.0,
                                                   bias=float(EPS_LN)),
                  reads=[R("msq")], writes=[R("v2")], cost=300)
            s.add("scalar", lambda e: e.activation(st["std"][:], st["s2"][:], AF.Ln,
                                                   scale=1.0 / D, bias=st["v2"][:]),
                  reads=[R("s2"), R("v2")], writes=[R("std")], cost=320)
            s.add("scalar", lambda e: e.activation(st["rstd"][:], st["std"][:], AF.Exp, scale=-0.5),
                  reads=[R("std")], writes=[R("rstd")], cost=320)
            s.add("scalar", lambda e: e.activation(st["nmr"][:], st["rstd"][:],
                                                   AF.Copy, scale=st["s1"][:]),
                  reads=[R("s1"), R("rstd")], writes=[R("nmr")], cost=300)
            s.add("vector", lambda e: e.tensor_scalar(
                out=ln_sb[:], in0=x_sb[:], scalar1=st["rstd"][:], scalar2=st["nmr"][:],
                op0=ALU.mult, op1=ALU.add),
                reads=[xres, R("rstd"), R("nmr")], writes=[lnres], tag=f"ln{sti}_apply")

        def emit_transposes(src_res, src_sb, dst_res, dst_sb, tag):
            for i in range(8):
                s.add("tensor", lambda e, i=i: e.transpose(
                    ptr[:, i * 128:(i + 1) * 128], src_sb[:, i * 128:(i + 1) * 128],
                    ident[:]),
                    reads=[src_res, "ident"],
                    writes=["ptr"], tag=f"tr_{tag}{i}")
            for hb in range(2):
                s.add("scalar", lambda e, hb=hb: e.copy(
                    dst_sb[:, hb * 512:(hb + 1) * 512],
                    ptr[:, hb * 512:(hb + 1) * 512]),
                    reads=["ptr"],
                    writes=[dst_res + ("A" if hb == 0 else "B")],
                    tag=f"ev_{tag}{hb}", cost=650)

        def emit_transposes64(src_res, src_sb, dst_res, dst_sb, tag, eng="scalar"):
            for i in range(8):
                s.add("tensor", lambda e, i=i: e.transpose(
                    ptr[:, i * 128:(i + 1) * 128], src_sb[:, i * 128:(i + 1) * 128],
                    ident[:]),
                    reads=[src_res, "ident"],
                    writes=["ptr"], tag=f"tr_{tag}{i}")
            pv = ptr[:].rearrange("p (a t) -> p a t", a=8)
            dv = dst_sb[0:64, :].rearrange("p (a hp t) -> p a hp t", a=8, hp=2)
            for hb in range(2):
                a0, a1 = 4 * hb, 4 * hb + 4
                rres = "ptr"
                wres = dst_res + ("A" if hb == 0 else "B")
                if eng == "scalar":
                    s.add("scalar", lambda e, a0=a0, a1=a1: e.copy(
                        dv[:, a0:a1, 0, :], pv[0:64, a0:a1, :]),
                        reads=[rres], writes=[wres], tag=f"ev_{tag}e{hb}", cost=360)
                    s.add("scalar", lambda e, a0=a0, a1=a1: e.copy(
                        dv[:, a0:a1, 1, :], pv[64:128, a0:a1, :]),
                        reads=[rres], writes=[wres], tag=f"ev_{tag}o{hb}", cost=360)
                else:
                    s.add("vector", lambda e, a0=a0, a1=a1: e.tensor_copy(
                        dv[:, a0:a1, 0, :], pv[0:64, a0:a1, :]),
                        reads=[rres], writes=[wres], tag=f"ev_{tag}e{hb}", cost=360)
                    s.add("vector", lambda e, a0=a0, a1=a1: e.tensor_copy(
                        dv[:, a0:a1, 1, :], pv[64:128, a0:a1, :]),
                        reads=[rres], writes=[wres], tag=f"ev_{tag}o{hb}", cost=360)

        def emit_proj(wres, w_sb, lres, lsb, tag, evac_half):
            lresA, lresB = lres
            banks = (next_bank(), next_bank())
            for o in range(8):
                pj, pjres = banks[0] if o < 4 else banks[1]
                off = (o % 4) * 128
                for i in range(8):
                    s.add("tensor", lambda e, o=o, i=i, pj=pj, off=off, lsb=lsb, w_sb=w_sb: e.matmul(
                        pj[:, off:off + 128],
                        lsb[:, i * 128:(i + 1) * 128],
                        w_sb[:, i * D + o * 128: i * D + o * 128 + 128],
                        start=(i == 0), stop=(i == 7)),
                        reads=[lresA if i < 4 else lresB, wres],
                        writes=[pjres], tag=f"{tag}_o{o}i{i}")
                if o == 3:
                    evac_half(0, banks[0])
            evac_half(1, banks[1])

        def sb_evac(dst_res, dst_sb, tag):
            def evac_half(half, bank):
                pj, pjres = bank
                s.add("scalar", lambda e, half=half, pj=pj, dst_sb=dst_sb: e.copy(
                    dst_sb[:, half * 512:(half + 1) * 512], pj[:]),
                    reads=[pjres], writes=[dst_res], tag=f"{tag}_h{half}")
            return evac_half

        def emit_rope_phi(c, src_res, src_sb, rdst_res, rdst, mdst_res, mdst,
                          edst_res, edst, pdst_res, pdst, tag):
            sv = src_sb[:].rearrange("p (h x j) -> p x h j", x=2, j=32)
            LO, HI = sv[:, 0], sv[:, 1]
            Cb = tabC[:, c * 32:(c + 1) * 32].unsqueeze(1).broadcast_to([128, H, 32])
            Sb = tabS[:, c * 32:(c + 1) * 32].unsqueeze(1).broadcast_to([128, H, 32])
            rAv = rA[:].rearrange("p (h j) -> p h j", j=32)
            rBv = rB[:].rearrange("p (h j) -> p h j", j=32)
            rCv = rC[:].rearrange("p (h j) -> p h j", j=32)
            rDv = rD[:].rearrange("p (h j) -> p h j", j=32)
            dv = rdst[:].rearrange("p (h x j) -> p x h j", x=2, j=32)
            s.add("gpsimd", lambda e: e.tensor_tensor(out=rAv, in0=HI, in1=Sb, op=ALU.mult),
                  reads=[src_res, "tabS"], writes=["rA"], tag=f"{tag}_rA")
            s.add("gpsimd", lambda e: e.tensor_tensor(out=rBv, in0=LO, in1=Sb, op=ALU.mult),
                  reads=[src_res, "tabS"], writes=["rB"], tag=f"{tag}_rB")
            s.add("vector", lambda e: e.tensor_tensor(out=rCv, in0=LO, in1=Cb, op=ALU.mult),
                  reads=[src_res, "tabC"], writes=["rC"], tag=f"{tag}_rC")
            s.add("vector", lambda e: e.tensor_tensor(out=rDv, in0=HI, in1=Cb, op=ALU.mult),
                  reads=[src_res, "tabC"], writes=["rD"], tag=f"{tag}_rD")
            s.add("vector", lambda e: e.tensor_tensor(out=dv[:, 0], in0=rCv, in1=rAv, op=ALU.subtract),
                  reads=["rC", "rA"], writes=[rdst_res], tag=f"{tag}_lo")
            s.add("vector", lambda e: e.tensor_tensor(out=dv[:, 1], in0=rBv, in1=rDv, op=ALU.add),
                  reads=["rB", "rD"], writes=[rdst_res], tag=f"{tag}_hi")
            s.add("vector", lambda e: e.tensor_scalar(
                out=mdst[:], in0=rdst[:], scalar1=0.0, scalar2=None, op0=ALU.min),
                reads=[rdst_res], writes=[mdst_res], tag=f"{tag}_min")
            s.add("scalar", lambda e: e.activation(edst[:], mdst[:], AF.Exp),
                  reads=[mdst_res], writes=[edst_res], tag=f"{tag}_exp")
            s.add("vector", lambda e: e.scalar_tensor_tensor(
                out=pdst[:], in0=rdst[:], scalar=0.0, in1=edst[:],
                op0=ALU.max, op1=ALU.add),
                reads=[rdst_res, edst_res], writes=[pdst_res], tag=f"{tag}_phi")

        def kv_slot(h):
            b = min(h // 7, 2)
            return b, (h - 7 * b) * 65

        def front_ln_sums(c):
            emit_ln_sums(c, f"x{c % 3}", xs[c % 3], c % 2)

        def front_ln_rest(c):
            i2 = c % 2
            emit_ln_rest(c, f"x{c % 3}", xs[c % 3], i2, f"ln1_{i2}", ln1[i2])
            tap("ln1", c, f"ln1_{i2}", ln1[i2][:])
            emit_transposes(f"ln1_{i2}", ln1[i2], f"ln1T{i2}", ln1T[i2], f"ln1T{c}")
            tap("ln1T", c, (f"ln1T{i2}A", f"ln1T{i2}B"), ln1T[i2][:])

        def front_q(c):
            i2 = c % 2
            emit_proj("wq", wq, (f"ln1T{i2}A", f"ln1T{i2}B"), ln1T[i2], f"q{c}",
                      sb_evac(f"qev{i2}", qev[i2], f"qev{c}"))
            tap("qev", c, f"qev{i2}", qev[i2][:])

        def front_k(c):
            i2 = c % 2
            emit_proj("wk", wk, (f"ln1T{i2}A", f"ln1T{i2}B"), ln1T[i2], f"k{c}",
                      sb_evac(f"kev{i2}", kev[i2], f"kev{c}"))
            tap("kev", c, f"kev{i2}", kev[i2][:])

        def front_v(c):
            v_sb = vaug[c % 2]
            vres = f"vaug{c % 2}"
            vv = v_sb[:].rearrange("p (h e) -> p h e", e=65)

            def evac_half(half, bank):
                pj, pjres = bank
                s.add("scalar", lambda e, half=half, pj=pj, vv=vv: e.copy(
                    vv[:, half * 8:(half + 1) * 8, 0:64],
                    pj[:].rearrange("p (h e) -> p h e", e=64)),
                    reads=[pjres], writes=[vres], tag=f"vev{c}_h{half}")
            emit_proj("wv", wv, (f"ln1T{c % 2}A", f"ln1T{c % 2}B"), ln1T[c % 2],
                      f"v{c}", evac_half)
            tap("vaug", c, vres, v_sb[:])

        def front_rope_q(c):
            i2 = c % 2
            emit_rope_phi(c, f"qev{i2}", qev[i2], "qr", qr, "qm", qm, "qE", qE,
                          f"qphi{i2}", qphi[i2], f"rq{c}")
            tap("qphi", c, f"qphi{i2}", qphi[i2][:])

        def front_rope_k(c):
            i2 = c % 2
            emit_rope_phi(c, f"kev{i2}", kev[i2], "kr", kr, "km", km, "kE", kE,
                          f"kphi{i2}", kphi[i2], f"rk{c}")
            tap("kphi", c, f"kphi{i2}", kphi[i2][:])

        def front_qkT(c):
            i2 = c % 2
            emit_transposes64(f"qphi{i2}", qphi[i2], f"qT{i2}", qT[i2], f"qT{c}")
            emit_transposes64(f"kphi{i2}", kphi[i2], f"kT{i2}", kT[i2], f"kT{c}", eng="vector")
            tap("qT", c, (f"qT{i2}A", f"qT{i2}B"), qT[i2][0:64, :])
            tap("kT", c, (f"kT{i2}A", f"kT{i2}B"), kT[i2][0:64, :])

        def back_attn(c):
            i2 = c % 2
            qTs, kTs = qT[i2], kT[i2]
            qTh = lambda h: f"qT{i2}A" if h < 8 else f"qT{i2}B"
            kTh = lambda h: f"kT{i2}A" if h < 8 else f"kT{i2}B"
            v_sb, vres = vaug[i2], f"vaug{i2}"
            kph, kphres = kphi[i2], f"kphi{i2}"
            kvres_r = f"KVs{i2}"
            kvres_w = f"KVs{(c + 1) % 2}"
            mb = maskt[:].unsqueeze(1).broadcast_to([128, 4, 128])
            for g in range(4):
                ab, abres = next_bank()
                for hh in range(4):
                    h = g * 4 + hh
                    s.add("tensor", lambda e, h=h, hh=hh, qTs=qTs, kTs=kTs, ab=ab: e.matmul(
                        ab[:, hh * 128:(hh + 1) * 128],
                        kTs[0:96, h * 128:(h + 1) * 128],
                        qTs[0:96, h * 128:(h + 1) * 128],
                        start=True, stop=True, skip_group_check=True),
                        reads=[qTh(h), kTh(h)], writes=[abres], tag=f"A{c}_h{h}")
                s.add("vector", lambda e, g=g, ab=ab: e.tensor_tensor(
                    out=Asb[:, g * 512:(g + 1) * 512].rearrange("p (h t) -> p h t", t=128),
                    in0=ab[:, 0:512].rearrange("p (h t) -> p h t", t=128),
                    in1=mb, op=ALU.mult),
                    reads=[abres, "mask"], writes=[f"A{g}"], tag=f"mask{c}_g{g}")
                for hh in range(4):
                    h = g * 4 + hh
                    b, off = kv_slot(h)
                    s.add("tensor", lambda e, h=h, b=b, off=off, v_sb=v_sb, kph=kph, c=c: e.matmul(
                        kv[b][0:64, off:off + 65],
                        kph[:, h * 64:(h + 1) * 64],
                        v_sb[:, h * 65:(h + 1) * 65],
                        start=False, stop=(c == NCHUNK - 1), skip_group_check=True),
                        reads=[kphres, vres], writes=[f"kv{b}"], tag=f"kv{c}_h{h}")
                tap(f"Asb{g}", c, f"A{g}", Asb[:, g * 512:(g + 1) * 512])
            if c < NCHUNK - 1:
                kvv = KVs[(c + 1) % 2]
                s.add("scalar", lambda e, kvv=kvv: e.copy(kvv[0:64, 0:455], kv[0][0:64, :]),
                      reads=["kv0"], writes=[kvres_w], tag=f"kvev{c}_0", cost=550)
                s.add("scalar", lambda e, kvv=kvv: e.copy(kvv[0:64, 455:910], kv[1][0:64, :]),
                      reads=["kv1"], writes=[kvres_w], tag=f"kvev{c}_1", cost=550)
                s.add("scalar", lambda e, kvv=kvv: e.copy(kvv[0:64, 910:1040], kv[2][0:64, :]),
                      reads=["kv2"], writes=[kvres_w], tag=f"kvev{c}_2", cost=550)
                tap("KVs", c, kvres_w, KVs[(c + 1) % 2][0:64, :])
            ytiles_after = {0: (0, 1, 2), 1: (3, 4, 5, 6), 2: (7,)}
            for p in range(3):
                nh = 2 if p == 2 else 7
                h0 = 7 * p
                yb, ybres = next_bank()
                for hh in range(nh):
                    h = h0 + hh
                    s.add("tensor", lambda e, h=h, hh=hh, v_sb=v_sb, yb=yb: e.matmul(
                        yb[:, hh * 65:hh * 65 + 65],
                        Asb[:, h * 128:(h + 1) * 128],
                        v_sb[:, h * 65:(h + 1) * 65],
                        start=True, stop=False, skip_group_check=True),
                        reads=[f"A{h // 4}", vres], writes=[ybres], tag=f"Yi{c}_h{h}")
                    s.add("tensor", lambda e, h=h, hh=hh, qTs=qTs, kvsb=KVs[i2], yb=yb: e.matmul(
                        yb[:, hh * 65:hh * 65 + 65],
                        qTs[0:96, h * 128:(h + 1) * 128],
                        kvsb[0:96, h * 65:(h + 1) * 65],
                        start=False, stop=True, skip_group_check=True),
                        reads=[qTh(h), kvres_r], writes=[ybres], tag=f"Yx{c}_h{h}")
                pyv = yb[:, 0:455].rearrange("p (h e) -> p h e", e=65)
                s.add("vector", lambda e, nh=nh, h0=h0, pyv=pyv: e.tensor_scalar(
                    out=r16a[:, h0:h0 + nh], in0=pyv[:, 0:nh, 64], scalar1=EPS_DEN,
                    scalar2=None, op0=ALU.max),
                    reads=[ybres], writes=[f"r16a{p}"], tag=f"den{c}_p{p}", cost=220)
                s.add("vector", lambda e, nh=nh, h0=h0: e.reciprocal(
                    r16b[:, h0:h0 + nh], r16a[:, h0:h0 + nh]),
                    reads=[f"r16a{p}"], writes=[f"r16b{p}"], cost=220)
                for dh, (a, b) in enumerate(((0, min(4, nh)), (min(4, nh), nh))):
                    if b <= a:
                        continue
                    s.add("vector", lambda e, a=a, b=b, h0=h0, pyv=pyv: e.tensor_tensor(
                        out=Ysb[:, (h0 + a) * 64:(h0 + b) * 64].rearrange("p (h e) -> p h e", e=64),
                        in0=pyv[:, a:b, 0:64],
                        in1=r16b[:, h0 + a:h0 + b].unsqueeze(2).broadcast_to([128, b - a, 64]),
                        op=ALU.mult),
                        reads=[ybres, f"r16b{p}"], writes=[f"Y{p}h{dh}"],
                        tag=f"ydiv{c}_p{p}h{dh}", cost=420)

                def yres(h):
                    pp = min(h // 7, 2)
                    return f"Y{pp}h{0 if h - 7 * pp < 4 else 1}"
                for i in ytiles_after[p]:
                    s.add("tensor", lambda e, i=i: e.transpose(
                        ptr[:, i * 128:(i + 1) * 128], Ysb[:, i * 128:(i + 1) * 128],
                        ident[:]),
                        reads=[yres(2 * i), yres(2 * i + 1), "ident"],
                        writes=["ptr"], tag=f"tr_YT{c}{i}")
                tap(f"Ysb{p}", c, (f"Y{p}h0", f"Y{p}h1" if nh > 4 else f"Y{p}h0"),
                    Ysb[:, 7 * p * 64: (7 * p + nh) * 64])
            for hb in range(2):
                s.add("scalar", lambda e, hb=hb: e.copy(
                    YT[:, hb * 512:(hb + 1) * 512], ptr[:, hb * 512:(hb + 1) * 512]),
                    reads=["ptr"],
                    writes=["YTA" if hb == 0 else "YTB"],
                    tag=f"ev_YT{c}{hb}", cost=650)

        def back_wo(c):
            x_sb, xres = xs[c % 3], f"x{c % 3}"
            x2sb, x2res = x2[c % 2], f"x2_{c % 2}"

            def evac_half(half, bank):
                pj, pjres = bank
                s.add("vector", lambda e, half=half, pj=pj, x2sb=x2sb, x_sb=x_sb: e.tensor_tensor(
                    out=x2sb[:, half * 512:(half + 1) * 512], in0=pj[:],
                    in1=x_sb[:, half * 512:(half + 1) * 512], op=ALU.add),
                    reads=[pjres, xres], writes=[x2res], tag=f"res1_{c}_h{half}")
            emit_proj("wo", wo, ("YTA", "YTB"), YT, f"wo{c}", evac_half)
            tap("x2", c, x2res, x2sb[:])
            emit_ln_sums(c, x2res, x2sb, 2)

        def back_ffn(c):
            x2sb, x2res = x2[c % 2], f"x2_{c % 2}"
            osb, outres = outb[c % 2], f"out{c % 2}"
            emit_ln_rest(c, x2res, x2sb, 2, "ln2", ln2)
            emit_transposes("ln2", ln2, "ln2T", ln2T, f"ln2T{c}")

            def evac_half(half, bank):
                pj, pjres = bank
                s.add("vector", lambda e, half=half, pj=pj, osb=osb, x2sb=x2sb: e.tensor_tensor(
                    out=osb[:, half * 512:(half + 1) * 512], in0=pj[:],
                    in1=x2sb[:, half * 512:(half + 1) * 512], op=ALU.add),
                    reads=[pjres, x2res], writes=[outres], tag=f"res2_{c}_h{half}")
            emit_proj("wf", wf, ("ln2TA", "ln2TB"), ln2T, f"ffn{c}", evac_half)
            s.add("sync", lambda e, c=c, osb=osb: e.dma_start(
                out=out_d[c * 128:(c + 1) * 128, :], in_=osb[:]),
                reads=[outres], dma=True, tag=f"store{c}", cost=3500)

        if NCHUNK > 2:
            s.add("sync", lambda e: e.dma_start(out=xs[2][:], in_=x_d[256:384, :]),
                  writes=["x2"], dma=True, tag="load_x2", cost=3500)
        front_ln_sums(0)
        front_ln_rest(0)
        front_q(0)
        front_rope_q(0)
        front_k(0)
        front_rope_k(0)
        front_v(0)
        front_qkT(0)
        for c in range(NCHUNK):
            n = c + 1
            if n < NCHUNK:
                front_ln_sums(n)
            back_attn(c)
            if n < NCHUNK:
                front_ln_rest(n)
                front_q(n)
                front_rope_q(n)
            back_wo(c)
            if n < NCHUNK:
                front_k(n)
                front_rope_k(n)
            back_ffn(c)
            if n < NCHUNK:
                front_v(n)
                front_qkT(n)
            if c + 3 < NCHUNK:
                s.add("sync", lambda e, c=c: e.dma_start(
                    out=xs[(c + 3) % 3][:], in_=x_d[(c + 3) * 128:(c + 4) * 128, :]),
                    writes=[f"x{(c + 3) % 3}"], dma=True, tag=f"load_x{c + 3}", cost=3500)

        order = s.schedule()
        waits, needs_inc, dma_no, inc_no = s.plan(order)
        by_engine = {e: [s.ops[gi] for gi in order if s.ops[gi]["engine"] == e]
                     for e in ENGINES}
        n_sync_dma = sum(1 for op in by_engine["sync"] if op["dma"])

        def emit_engine(name, eng):
            for op in by_engine[name]:
                i = op["i"]
                for (kind, tgt, val) in waits[i]:
                    if kind == "dma":
                        eng.wait_ge(dsems[tgt], 16 * val)
                    else:
                        eng.wait_ge(sems[tgt], inc_no[val])
                instr = op["fn"](eng)
                if op["dma"]:
                    k = dma_no[op["i"]]
                    instr.then_inc(dsems[(name, (k - 1) % 3)], 16)
                elif needs_inc[i] and instr is not None:
                    instr.then_inc(sems[name], 1)
            if name in ("sync", "gpsimd"):
                nq = sum(1 for op in by_engine[name] if op["dma"])
                for j in range(3):
                    m_tot = (nq - j + 2) // 3
                    if m_tot > 0:
                        eng.wait_ge(dsems[(name, j)], 16 * m_tot)

        @block.tensor
        def _(tensor):
            emit_engine("tensor", tensor)

        @block.vector
        def _(vector):
            emit_engine("vector", vector)

        @block.scalar
        def _(scalar):
            emit_engine("scalar", scalar)

        @block.gpsimd
        def _(gpsimd):
            emit_engine("gpsimd", gpsimd)

        @block.sync
        def _(sync):
            emit_engine("sync", sync)

    return nc


def _host_tables():
    inv = 10000.0 ** (-np.arange(32, dtype=np.float64) / 32.0)
    ang = np.arange(T_SEQ, dtype=np.float64)[:, None] * inv[None, :]
    C = np.cos(ang).reshape(NCHUNK, 128, 32).transpose(1, 0, 2).reshape(128, NCHUNK * 32)
    S = np.sin(ang).reshape(NCHUNK, 128, 32).transpose(1, 0, 2).reshape(128, NCHUNK * 32)
    mask = (np.arange(128)[:, None] <= np.arange(128)[None, :]).astype(np.float64)
    ident = np.eye(128)
    bf = ml_dtypes.bfloat16
    return C.astype(np.float32), S.astype(np.float32), mask.astype(bf), ident.astype(bf)


def kernel(L0_attn_x, L0_attn_Wq, L0_attn_Wk, L0_attn_Wv, L0_attn_Wo, L0_ffn_weight):
    x = np.asarray(L0_attn_x, dtype=np.float32)
    wqT = np.ascontiguousarray(np.asarray(L0_attn_Wq, dtype=np.float32).T)
    wkT = np.ascontiguousarray(np.asarray(L0_attn_Wk, dtype=np.float32).T)
    wvT = np.ascontiguousarray(np.asarray(L0_attn_Wv, dtype=np.float32).T)
    woT = np.ascontiguousarray(np.asarray(L0_attn_Wo, dtype=np.float32).T)
    wfT = np.ascontiguousarray(np.asarray(L0_ffn_weight, dtype=np.float32).T)
    tabC, tabS, mask, ident = _host_tables()

    nc = build_program()
    in_maps = []
    for core in range(N_CORES):
        in_maps.append({
            "x": np.ascontiguousarray(x[core]),
            "wqT": wqT, "wkT": wkT, "wvT": wvT, "woT": woT, "wfT": wfT,
            "tabC": tabC, "tabS": tabS, "mask": mask, "ident": ident,
        })
    res = run_bass_kernel_spmd(nc, in_maps, core_ids=list(range(N_CORES)))
    out = np.stack([np.asarray(r["out"]) for r in res.results], axis=0)
    return out
